# version 08
# speedup vs baseline: 1.2342x; 1.0010x over previous
import contextlib
import numpy as np
import ml_dtypes
import concourse.bass as bass
import concourse.mybir as mybir
from concourse.bass_utils import run_bass_kernel_spmd

F32 = mybir.dt.float32
BF16 = mybir.dt.bfloat16
I32 = mybir.dt.int32
AF = mybir.ActivationFunctionType
ALU = mybir.AluOpType
AX = mybir.AxisListType

T = 2048
D = 1024
NCH = D // 128
TTS = 512
NTT = T // TTS

DMA_RR = True
SCOPE_TAG = False
ENGS = ("sp", "pe", "act", "dve", "gp")
_BASS_ENG = {"sp": "sync", "pe": "tensor", "act": "scalar", "dve": "vector", "gp": "gpsimd"}


class Tile:
    __slots__ = ("h", "lw", "readers", "name")

    def __init__(self, h, name=""):
        self.h = h
        self.name = name
        self.lw = None
        self.readers = []

    def __getitem__(self, sl):
        return self.h[sl]


class Op:
    __slots__ = ("eng", "fn", "gidx", "deps", "need_inc", "inc_ord", "is_dma",
                 "phase", "tag", "strict")

    def __init__(self, eng, fn, is_dma, gidx, tag=""):
        self.strict = False
        self.eng = eng
        self.fn = fn
        self.is_dma = is_dma
        self.gidx = gidx
        self.deps = []
        self.need_inc = False
        self.inc_ord = None
        self.phase = None
        self.tag = tag


class _Barrier:
    pass


class Ker:
    def __init__(self, nc, stack: contextlib.ExitStack):
        self.nc = nc
        self.stack = stack
        self.ops = []
        self.dbg_waits = {}
        self.n_barriers = 0
        self._uid = 0
        self._dma_rr = 0

    def sb(self, name, shape, dt, scope=None):
        h = (scope or self.stack).enter_context(
            self.nc.sbuf_tensor(self._u(name), list(shape), dt))
        return Tile(h, name)

    def ps(self, name, shape, dt, scope=None):
        h = (scope or self.stack).enter_context(
            self.nc.psum_tensor(self._u(name), list(shape), dt))
        return Tile(h, name)

    def _u(self, name):
        self._uid += 1
        return f"{name}_{self._uid}"

    def op(self, eng, fn, reads=(), writes=(), tag="", strict=False):
        o = Op(eng, fn, False, len(self.ops), tag)
        o.strict = strict
        self._track(o, reads, writes)
        self.ops.append(o)
        return o

    def dma(self, eng, out_ap, in_ap, reads=(), writes=(), transpose=False, tag=""):
        if eng == "sp" and DMA_RR:
            eng = ("sp", "act")[self._dma_rr % 2]
            self._dma_rr += 1
        if transpose:
            fn = lambda e, o=out_ap, i=in_ap: e.dma_start_transpose(o, i)
        else:
            fn = lambda e, o=out_ap, i=in_ap: e.dma_start(o, i)
        o = Op(eng, fn, True, len(self.ops), tag)
        self._track(o, reads, writes)
        self.ops.append(o)
        return o

    def _track(self, o, reads, writes):
        deps = {}
        for t in reads:
            if t.lw is not None:
                deps[t.lw.gidx] = t.lw
        for t in writes:
            if t.lw is not None:
                deps[t.lw.gidx] = t.lw
            for r in t.readers:
                deps[r.gidx] = r
        for d in deps.values():
            if d is o:
                continue
            if d.eng == o.eng and not d.is_dma and not d.strict:
                continue
            o.deps.append(d)
        for t in reads:
            t.readers.append(o)
        for t in writes:
            t.lw = o
            t.readers = []

    def barrier(self):
        self.ops.append(_Barrier())
        self.n_barriers += 1

    def lower(self, verbose=False):
        nc = self.nc
        phase = 0
        for o in self.ops:
            if isinstance(o, _Barrier):
                phase += 1
            else:
                o.phase = phase
        n_phases = phase + 1
        for o in self.ops:
            if isinstance(o, _Barrier):
                continue
            for d in o.deps:
                if d.phase == o.phase and not d.is_dma:
                    d.need_inc = True
        last_op = {}
        for o in self.ops:
            if isinstance(o, _Barrier):
                continue
            if not o.is_dma:
                last_op[(o.phase, o.eng)] = o
        for o in last_op.values():
            o.need_inc = True
        R = 6
        inc_cnt = {e: 0 for e in ENGS}
        dma_cnt = {e: 0 for e in ENGS}
        for o in self.ops:
            if isinstance(o, _Barrier):
                continue
            if o.is_dma:
                o.inc_ord = dma_cnt[o.eng]
                dma_cnt[o.eng] += 1
            elif o.need_inc:
                inc_cnt[o.eng] += 1
                o.inc_ord = inc_cnt[o.eng]
        esem = {e: self.stack.enter_context(nc.semaphore(f"e_{e}"))
                for e in ENGS if inc_cnt[e] > 0}
        dsem = {e: [self.stack.enter_context(nc.semaphore(f"d_{e}{i}"))
                    for i in range(min(R, dma_cnt[e]))]
                for e in ENGS if dma_cnt[e] > 0}
        bsem = [self.stack.enter_context(nc.semaphore(f"bar{i}"))
                for i in range(self.n_barriers)]
        streams = {e: [] for e in ENGS}
        hw = {}
        run_inc = {e: 0 for e in ENGS}
        run_dma = {e: 0 for e in ENGS}
        bar_idx = 0
        for o in self.ops:
            if isinstance(o, _Barrier):
                bi = bar_idx
                bar_idx += 1
                for e in ENGS:
                    finals = []
                    nd = run_dma[e]
                    if nd > 0:
                        ds = dsem[e]
                        for i_ in range(len(ds)):
                            cnt = (nd - i_ + len(ds) - 1) // len(ds)
                            if cnt > 0:
                                finals.append((ds[i_], 16 * cnt))
                    ni = run_inc[e]
                    es = esem.get(e)
                    bs = bsem[bi]

                    def bfn(eng, finals=finals, bs=bs, ni=ni, es=es):
                        for sem_, val_ in finals:
                            eng.wait_ge(sem_, val_)
                        if ni > 0:
                            eng.wait_ge(es, ni)
                        eng.nop().then_inc(bs, 1)
                        eng.wait_ge(bs, len(ENGS))
                    streams[e].append(bfn)
                continue
            waits = []
            for d in o.deps:
                if d.phase < o.phase:
                    continue
                if d.is_dma:
                    ds = dsem[d.eng]
                    si = d.inc_ord % len(ds)
                    key = (o.eng, d.eng, True, si)
                    val = 16 * (d.inc_ord // len(ds) + 1)
                    sem = ds[si]
                else:
                    key = (o.eng, d.eng, False)
                    val = d.inc_ord
                    sem = esem[d.eng]
                if hw.get(key, 0) < val:
                    hw[key] = val
                    waits.append((sem, val))
            pre_waits = []
            if o.is_dma:
                ds = dsem[o.eng]
                si = o.inc_ord % len(ds)
                cyc = o.inc_ord // len(ds)
                if cyc > 0:
                    pre_waits.append((ds[si], 16 * cyc))
                inc = (ds[si], 16)
                run_dma[o.eng] += 1
            elif o.need_inc:
                inc = (esem[o.eng], 1)
                run_inc[o.eng] = o.inc_ord
            else:
                inc = None

            self.dbg_waits[o.gidx] = (list(waits), o.need_inc, o.inc_ord)

            def ofn(eng, waits=waits, pre_waits=pre_waits, fn=o.fn, inc=inc):
                for sem, val in waits:
                    eng.wait_ge(sem, val)
                for sem, val in pre_waits:
                    eng.wait_ge(sem, val)
                ins = fn(eng)
                if inc is not None:
                    ins.then_inc(inc[0], inc[1])
            ofn.ph = o.phase
            streams[o.eng].append(ofn)
        for e in ENGS:
            nd = run_dma[e]
            if nd > 0:
                ds = dsem[e]
                finals = []
                for i_ in range(len(ds)):
                    cnt = (nd - i_ + len(ds) - 1) // len(ds)
                    if cnt > 0:
                        finals.append((ds[i_], 16 * cnt))

                def ffn(eng, finals=finals):
                    for sem_, val_ in finals:
                        eng.wait_ge(sem_, val_)
                streams[e].append(ffn)
        if verbose:
            tot = {e: sum(1 for o in self.ops
                          if not isinstance(o, _Barrier) and o.eng == e) for e in ENGS}
            print(f"[mk] ops per engine: {tot}, phases={n_phases}, "
                  f"incs={inc_cnt}, dmas={dma_cnt}")
        with nc.Block() as block:
            for e in ENGS:
                lst = streams[e]

                def body(eng, lst=lst):
                    for f in lst:
                        ph = getattr(f, "ph", None)
                        if SCOPE_TAG and ph is not None:
                            with nc.named_scope(f"ph{ph:02d}"):
                                f(eng)
                        else:
                            f(eng)
                getattr(block, _BASS_ENG[e])(body)



CONST_VALS = [1e-5, -0.5, 0.125, 1e-12, -1.0, 2.0, 6.0, 0.5, -0.125, 1.0/256.0]


class Env:

    def __init__(self, k: Ker, dins, scope=None):
        self.k = k
        self.ones128 = k.sb("ones128", [128, 1], F32, scope)
        self.ones1r = k.sb("ones1r", [1, 128], F32, scope)
        self.onesrow = k.sb("onesrow", [1, TTS], F32, scope)
        self.eyeb = k.sb("eyeb", [128, 128], BF16, scope)
        self.fvals = k.sb("fvals", [128, len(CONST_VALS)], F32, scope)
        self.ones1rb = k.sb("ones1rb", [1, 128], BF16, scope)
        k.dma("sp", self.ones128[:, :], dins["c_ones128"], writes=[self.ones128])
        k.dma("sp", self.ones1r[:, :], dins["c_ones1r"], writes=[self.ones1r])
        k.dma("sp", self.onesrow[:, :], dins["c_onesrow"], writes=[self.onesrow])
        k.dma("sp", self.eyeb[:, :], dins["c_eyeb"], writes=[self.eyeb])
        k.dma("sp", self.fvals[:, :], dins["c_fvals"], writes=[self.fvals])
        if "c_ones1rb" in dins:
            k.dma("sp", self.ones1rb[:, :], dins["c_ones1rb"], writes=[self.ones1rb])
        for i, v in enumerate(CONST_VALS):
            k.nc.const_aps.register(F32, float(v), self.fvals.h[:, i:i + 1])


def emit_ln_rows(k, env, pb_a, pb_b, rows, n, eps=1e-5, tag="ln"):
    m, v, r = rows["m"], rows["v"], rows["r"]
    k.op("dve", lambda e: e.tensor_single_scalar(m[0:1, :], pb_a[0:1, :], 1.0 / n, ALU.mult),
         reads=[pb_a], writes=[m], tag=tag + "_m", strict=True)
    k.op("dve", lambda e: e.tensor_single_scalar(v[0:1, :], pb_b[0:1, :], 1.0 / n, ALU.mult),
         reads=[pb_b], writes=[v], tag=tag + "_v", strict=True)
    k.op("dve", lambda e: e.tensor_mul(r[0:1, :], m[0:1, :], m[0:1, :]),
         reads=[m], writes=[r], strict=True)
    k.op("dve", lambda e: e.tensor_tensor(v[0:1, :], v[0:1, :], r[0:1, :], ALU.subtract),
         reads=[v, r], writes=[v], strict=True)
    k.op("act", lambda e: e.activation(r[0:1, :], v[0:1, :], AF.Ln, bias=float(eps)),
         reads=[v], writes=[r], strict=True)
    k.op("act", lambda e: e.activation(r[0:1, :], r[0:1, :], AF.Exp, scale=-0.5),
         reads=[r], writes=[r], strict=True)



def emit_l0_ffn(k: Ker, env, dins, xin, xout):
    with contextlib.ExitStack() as sc:
        H = [k.sb(f"H{c}", [128, T], F32, sc) for c in range(NCH)]
        hb = [k.sb(f"hb{c}", [128, T], BF16, sc) for c in range(NCH)]
        stg = [k.sb(f"stg{i}", [128, TTS], F32, sc) for i in range(2)]
        xsq_scr = [k.sb(f"sqscr{i}", [128, TTS], F32, sc) for i in range(2)]
        ap_scr = xsq_scr
        rows = {n_: k.sb(f"row{n_}", [1, TTS], F32, sc) for n_ in ("m", "v", "r")}
        wT = [k.sb(f"wT{w}", [128, NCH * D], BF16, sc) for w in range(2)]
        wstage = [k.sb(f"wst{i}", [128, D], I32, sc) for i in range(2)]
        wf_scr = k.sb("wf", [128, D], F32, sc)
        scT = k.sb("scT", [64, D], F32, sc)
        rsel = k.sb("rsel", [64, D], F32, sc)
        la_bf = [k.sb(f"labf{w}", [128, 256], BF16, sc) for w in range(2)]
        lb_bf = [k.sb(f"lbbf{w}", [32, D], BF16, sc) for w in range(2)]
        brow = k.sb("brow", [1, 2 * D], F32, sc)
        gam = k.sb("gam", [128, NCH], F32, sc)
        bet = k.sb("bet", [128, NCH], F32, sc)
        y1f = [k.sb(f"y1f{c}", [128, TTS], F32, sc) for c in range(NCH)]
        xg = [k.sb(f"xg{c}", [128, TTS], BF16, sc) for c in range(NCH)]
        u_bf = k.sb("ubf", [32, TTS], BF16, sc)
        pb_a = k.ps("pba", [128, TTS], F32, sc)
        pb_b = k.ps("pbb", [128, TTS], F32, sc)
        pb_y = [k.ps(f"pby{i}", [128, TTS], F32, sc) for i in range(2)]
        pb_u = k.ps("pbu", [32, TTS], F32, sc)
        pb_s = k.ps("pbs", [128, D], F32, sc)

        k.dma("sp", rsel[:, :], dins["c_rsel"], writes=[rsel])

        def ln_stats(src_ap_fn, read_tiles):
            for c in range(NCH):
                k.op("pe", lambda e, c=c: e.matmul(
                    pb_a[0:1, :], env.ones128[:, :], src_ap_fn(c),
                    start=(c == 0), stop=(c == NCH - 1)),
                    reads=[env.ones128, read_tiles[c]], writes=[pb_a])
            for c in range(NCH):
                scr = xsq_scr[c % 2]
                k.op("dve", lambda e, c=c, scr=scr: e.tensor_mul(
                    scr[:, :], src_ap_fn(c), src_ap_fn(c)),
                    reads=[read_tiles[c]], writes=[scr])
                k.op("pe", lambda e, scr=scr, c=c: e.matmul(
                    pb_b[0:1, :], env.ones128[:, :], scr[:, :],
                    start=(c == 0), stop=(c == NCH - 1)),
                    reads=[env.ones128, scr], writes=[pb_b])
            emit_ln_rows(k, env, pb_a, pb_b, rows, D)
            k.op("pe", lambda e: e.matmul(pb_a[:, :], env.ones1r[:, :],
                                          rows["m"][0:1, :], start=True, stop=True),
                 reads=[env.ones1r, rows["m"]], writes=[pb_a])
            k.op("pe", lambda e: e.matmul(pb_b[:, :], env.ones1r[:, :],
                                          rows["r"][0:1, :], start=True, stop=True),
                 reads=[env.ones1r, rows["r"]], writes=[pb_b])

        def preln_tt(tt):
            ts = slice(tt * TTS, (tt + 1) * TTS)
            for c in range(NCH):
                st = stg[c % 2]
                k.dma("sp", st[:, :], xin[c * 128:(c + 1) * 128, ts],
                      writes=[st], tag="ldx")
                k.op("pe", lambda e, c=c, st=st: e.matmul(
                    pb_a[0:1, :], env.ones128[:, :], st[:, :],
                    start=(c == 0), stop=(c == NCH - 1)),
                    reads=[env.ones128, st], writes=[pb_a])
                scr = xsq_scr[c % 2]
                k.op("dve", lambda e, st=st, scr=scr: e.tensor_mul(
                    scr[:, :], st[:, :], st[:, :]),
                    reads=[st], writes=[scr])
                k.op("pe", lambda e, scr=scr, c=c: e.matmul(
                    pb_b[0:1, :], env.ones128[:, :], scr[:, :],
                    start=(c == 0), stop=(c == NCH - 1)),
                    reads=[env.ones128, scr], writes=[pb_b])
            emit_ln_rows(k, env, pb_a, pb_b, rows, D)
            k.op("pe", lambda e: e.matmul(pb_a[:, :], env.ones1r[:, :],
                                          rows["m"][0:1, :], start=True, stop=True),
                 reads=[env.ones1r, rows["m"]], writes=[pb_a])
            k.op("pe", lambda e: e.matmul(pb_b[:, :], env.ones1r[:, :],
                                          rows["r"][0:1, :], start=True, stop=True),
                 reads=[env.ones1r, rows["r"]], writes=[pb_b])
            for c in range(NCH):
                st = stg[c % 2]
                k.dma("sp", st[:, :], xin[c * 128:(c + 1) * 128, ts],
                      writes=[st], tag="ldx2")
                scr = ap_scr[c % 2]
                k.op("dve", lambda e, st=st, scr=scr: e.tensor_tensor(
                    scr[:, :], st[:, :], pb_a[:, :], ALU.subtract),
                    reads=[st, pb_a], writes=[scr])
                k.op("dve", lambda e, c=c, ts=ts, scr=scr: e.tensor_tensor(
                    H[c][:, ts], scr[:, :], pb_b[:, :], ALU.mult),
                    reads=[scr, pb_b], writes=[H[c]])
                k.op("dve", lambda e, c=c, ts=ts: e.tensor_copy(
                    hb[c][:, ts], H[c][:, ts]),
                    reads=[H[c]], writes=[hb[c]])

        for tt in range(NTT):
            preln_tt(tt)

        def prep_weights(j, w):
            k.dma("sp", scT[:, :], dins["L0_ffn_scT"][j], writes=[scT], tag="ldsc")
            for dc in range(NCH):
                st = wstage[dc % 2]
                k.dma("sp", st[:, :], dins["L0_ffn_qwT"][j, dc * 128:(dc + 1) * 128, :],
                      writes=[st], tag="ldw")
                for half in range(2):
                    k.op("pe", lambda e, dc=dc, half=half: e.matmul(
                        pb_s[:, half * TTS:(half + 1) * TTS],
                        rsel[:, dc * 128:(dc + 1) * 128],
                        scT[:, half * TTS:(half + 1) * TTS],
                        start=True, stop=True),
                        reads=[rsel, scT], writes=[pb_s])
                k.op("dve", lambda e, st=st: e.tensor_scalar(
                    wf_scr[:, :], st[:, :], 1.0 / 7.5, -1.0, ALU.mult, ALU.add),
                    reads=[st], writes=[wf_scr])
                k.op("dve", lambda e, dc=dc, w=w: e.tensor_tensor(
                    wT[w][:, dc * D:(dc + 1) * D], wf_scr[:, :], pb_s[:, :], ALU.mult),
                    reads=[wf_scr, pb_s], writes=[wT[w]])

        def prep_lora_bias(j, w):
            k.dma("sp", wf_scr[:, 0:256].rearrange("p (c r) -> p c r", r=32),
                  dins["L0_ffn_laT"][j].rearrange("(c p) r -> p c r", p=128),
                  writes=[wf_scr], tag="ldla")
            k.op("dve", lambda e, w=w: e.tensor_copy(la_bf[w][:, :], wf_scr[:, 0:256]),
                 reads=[wf_scr], writes=[la_bf[w]])
            k.dma("sp", wf_scr[0:32, :], dins["L0_ffn_lbT"][j],
                  writes=[wf_scr], tag="ldlb")
            k.op("dve", lambda e, w=w: e.tensor_copy(
                lb_bf[w][0:32, :], wf_scr[0:32, :]),
                reads=[wf_scr], writes=[lb_bf[w]])
            k.dma("sp", brow[0:1, w * D:(w + 1) * D], dins["L0_ffn_biases"][j:j + 1, :],
                  writes=[brow], tag="ldbias")

        def qlora_mm(src_ap_fn, src_tiles, w, consumer):
            for dc in range(NCH):
                k.op("pe", lambda e, dc=dc, w=w: e.matmul(
                    pb_u[0:32, :], la_bf[w][:, dc * 32:(dc + 1) * 32], src_ap_fn(dc),
                    start=(dc == 0), stop=(dc == NCH - 1)),
                    reads=[la_bf[w], src_tiles[dc]], writes=[pb_u])
            k.op("dve", lambda e: e.tensor_copy(u_bf[0:32, :], pb_u[0:32, :]),
                 reads=[pb_u], writes=[u_bf])
            for oc in range(NCH):
                pb = pb_y[oc % 2]
                for dc in range(NCH):
                    k.op("pe", lambda e, dc=dc, oc=oc, pb=pb, w=w: e.matmul(
                        pb[:, :], wT[w][:, dc * D + oc * 128: dc * D + (oc + 1) * 128],
                        src_ap_fn(dc), start=(dc == 0), stop=False),
                        reads=[wT[w], src_tiles[dc]], writes=[pb])
                k.op("pe", lambda e, oc=oc, pb=pb, w=w: e.matmul(
                    pb[:, :], lb_bf[w][0:32, oc * 128:(oc + 1) * 128],
                    u_bf[0:32, :], start=False, stop=False),
                    reads=[lb_bf[w], u_bf], writes=[pb])
                k.op("pe", lambda e, oc=oc, pb=pb, w=w: e.matmul(
                    pb[:, :], brow[0:1, w * D + oc * 128:w * D + (oc + 1) * 128],
                    env.onesrow[0:1, :], start=False, stop=True),
                    reads=[brow, env.onesrow], writes=[pb])
                consumer(oc, pb)

        for i in range(3):
            prep_weights(2 * i, 0)
            prep_lora_bias(2 * i, 0)
            prep_weights(2 * i + 1, 1)
            prep_lora_bias(2 * i + 1, 1)
            k.dma("sp", gam[:, :], dins["L0_ffn_gam"][i], writes=[gam], tag="ldgam")
            k.dma("sp", bet[:, :], dins["L0_ffn_bet"][i], writes=[bet], tag="ldbet")
            for tt in range(NTT):
                ts = slice(tt * TTS, (tt + 1) * TTS)

                def eat_y1(oc, pb):
                    k.op("dve", lambda e, oc=oc, pb=pb: e.tensor_copy(
                        y1f[oc][:, :], pb[:, :]), reads=[pb], writes=[y1f[oc]])
                qlora_mm(lambda dc, ts=ts: hb[dc][:, ts], hb, 0, eat_y1)
                ln_stats(lambda c: y1f[c][:, :], y1f)
                for c in range(NCH):
                    scr = ap_scr[c % 2]
                    k.op("dve", lambda e, c=c, scr=scr: e.tensor_tensor(
                        scr[:, :], y1f[c][:, :], pb_a[:, :], ALU.subtract),
                        reads=[y1f[c], pb_a], writes=[scr])
                    k.op("dve", lambda e, scr=scr: e.tensor_tensor(
                        scr[:, :], scr[:, :], pb_b[:, :], ALU.mult),
                        reads=[scr, pb_b], writes=[scr])
                    k.op("dve", lambda e, c=c, scr=scr: e.tensor_scalar(
                        scr[:, :], scr[:, :], gam[:, c:c + 1], bet[:, c:c + 1],
                        ALU.mult, ALU.add),
                        reads=[scr, gam, bet], writes=[scr])
                    k.op("act", lambda e, c=c, scr=scr: e.activation(
                        xg[c][:, :], scr[:, :], AF.Gelu),
                        reads=[scr], writes=[xg[c]])

                def eat_y2(oc, pb, ts=ts):
                    k.op("dve", lambda e, oc=oc, pb=pb, ts=ts: e.tensor_tensor(
                        H[oc][:, ts], H[oc][:, ts], pb[:, :], ALU.add),
                        reads=[H[oc], pb], writes=[H[oc]])
                    k.op("dve", lambda e, oc=oc, ts=ts: e.tensor_copy(
                        hb[oc][:, ts], H[oc][:, ts]),
                        reads=[H[oc]], writes=[hb[oc]])
                qlora_mm(lambda dc: xg[dc][:, :], xg, 1, eat_y2)

        for tt in range(NTT):
            ts = slice(tt * TTS, (tt + 1) * TTS)
            for c in range(NCH):
                st = stg[c % 2]
                k.dma("sp", st[:, :], xin[c * 128:(c + 1) * 128, ts],
                      writes=[st], tag="ldx3")
                scr = ap_scr[c % 2]
                k.op("dve", lambda e, c=c, ts=ts, scr=scr, st=st: e.tensor_tensor(
                    scr[:, :], st[:, :], H[c][:, ts], ALU.add),
                    reads=[st, H[c]], writes=[scr])
                k.dma("sp", xout[c * 128:(c + 1) * 128, ts], scr[:, :],
                      reads=[scr], tag="stx")
    k.barrier()



def host_consts():
    c = {}
    c["c_ones128"] = np.ones((128, 1), np.float32)
    c["c_ones1r"] = np.ones((1, 128), np.float32)
    c["c_onesrow"] = np.ones((1, TTS), np.float32)
    c["c_eyeb"] = np.eye(128, dtype=ml_dtypes.bfloat16)
    rsel = np.zeros((64, D), np.float32)
    for p in range(D):
        rsel[(p % 128) // 16 + 0, p] = 0.0
    rsel[:] = 0.0
    for dc in range(8):
        for p in range(128):
            rsel[dc * 8 + p // 16, dc * 128 + p] = 1.0
    c["c_rsel"] = rsel
    c["c_fvals"] = np.tile(np.array(CONST_VALS, np.float32)[None, :], (128, 1))
    c["c_onesrowb"] = np.ones((1, TTS), ml_dtypes.bfloat16)
    c["c_ones1rb"] = np.ones((1, 128), ml_dtypes.bfloat16)
    c["c_eyef"] = np.eye(128, dtype=np.float32)
    c["c_ones65"] = np.ones((65, 128), ml_dtypes.bfloat16)
    c["c_umask_s"] = np.triu(np.ones((128, 128)), 1).astype(ml_dtypes.bfloat16)
    c["c_umask_i"] = np.triu(np.ones((128, 128)), 0).astype(ml_dtypes.bfloat16)
    return c


def host_prep_l0_ffn(inputs):
    d = {}
    d["L0_ffn_qwT"] = np.ascontiguousarray(
        inputs["L0_ffn_qweights"].transpose(0, 2, 1)).astype(np.int32)
    d["L0_ffn_scT"] = np.ascontiguousarray(
        inputs["L0_ffn_scales"].transpose(0, 2, 1)).astype(np.float32)
    d["L0_ffn_laT"] = np.ascontiguousarray(
        inputs["L0_ffn_lora_a"].transpose(0, 2, 1)).astype(np.float32)
    d["L0_ffn_lbT"] = np.ascontiguousarray(
        inputs["L0_ffn_lora_b"].transpose(0, 2, 1)).astype(np.float32)
    d["L0_ffn_biases"] = np.ascontiguousarray(inputs["L0_ffn_biases"])
    d["L0_ffn_gam"] = np.ascontiguousarray(
        inputs["L0_ffn_ln_gamma"].reshape(3, 8, 128).transpose(0, 2, 1))
    d["L0_ffn_bet"] = np.ascontiguousarray(
        inputs["L0_ffn_ln_beta"].reshape(3, 8, 128).transpose(0, 2, 1))
    return d


_L0FFN_SPECS = {
    "L0_ffn_qwT": ([6, D, D], I32),
    "L0_ffn_scT": ([6, 64, D], F32),
    "L0_ffn_laT": ([6, D, 32], F32),
    "L0_ffn_lbT": ([6, 32, D], F32),
    "L0_ffn_biases": ([6, D], F32),
    "L0_ffn_gam": ([3, 128, 8], F32),
    "L0_ffn_bet": ([3, 128, 8], F32),
}

_CONST_SPECS = {
    "c_ones128": ([128, 1], F32),
    "c_ones1r": ([1, 128], F32),
    "c_onesrow": ([1, TTS], F32),
    "c_eyeb": ([128, 128], BF16),
    "c_rsel": ([64, D], F32),
    "c_fvals": ([128, len(CONST_VALS)], F32),
}


def build_l0_ffn_program():
    nc = bass.Bass("TRN2", target_bir_lowering=False)
    stack = contextlib.ExitStack()
    dins = {}
    for name, (shape, dt) in {**_CONST_SPECS, **_L0FFN_SPECS}.items():
        dins[name] = nc.dram_tensor(name, shape, dt, kind="ExternalInput").ap()
    xin = nc.dram_tensor("xin", [D, T], F32, kind="ExternalInput").ap()
    xout = nc.dram_tensor("xout", [D, T], F32, kind="ExternalOutput").ap()
    with stack:
        k = Ker(nc, stack)
        env = Env(k, dins)
        k.barrier()
        emit_l0_ffn(k, env, dins, xin, xout)
        k.lower(verbose=True)
    return nc

def emit_fp4_qdq(k: Ker, env, src, out_bf, scr, N, src_sl=None, out_sl=None, eng2="dve", boff=0):
    nb = N // 16
    sa = src[:, 0:N] if src_sl is None else src_sl
    oa = out_bf[:, 0:N] if out_sl is None else out_sl
    bm, s, rcp = scr["bm"], scr["s"], scr["rcp"]
    absx, acc1, acc2, sb16 = scr["absx"], scr["acc1"], scr["acc2"], scr["sb16"]
    b0, b1 = boff, boff + nb
    k.op("dve", lambda e: e.reduce_max(
        bm[:, b0:b1].unsqueeze(2), sa.rearrange("p (a b) -> p a b", b=16),
        axis=AX.X, apply_absolute_value=True),
        reads=[src], writes=[bm], tag="q_bm", strict=True)
    k.op("dve", lambda e: e.tensor_scalar(
        s[:, b0:b1], bm[:, b0:b1], 1.0 / 6.0, 1e-12, ALU.mult, ALU.max),
        reads=[bm], writes=[s], tag="q_s", strict=True)
    for i0 in range(b0, b1, 128):
        i1 = min(b1, i0 + 128)
        k.op("dve", lambda e, i0=i0, i1=i1: e.reciprocal(rcp[:, i0:i1], s[:, i0:i1]),
             reads=[s], writes=[rcp], tag="q_rcp", strict=True)
    k.op("act", lambda e: e.activation(absx[:, 0:N], sa, AF.Abs),
         reads=[src], writes=[absx], tag="q_abs", strict=True)
    rep = lambda t: t[:, b0:b1].unsqueeze(2).broadcast_to([128, nb, 16])
    r3 = lambda ap: ap.rearrange("p (a b) -> p a b", b=16)
    k.op(eng2, lambda e: e.tensor_tensor(
        r3(absx[:, 0:N]), r3(absx[:, 0:N]), rep(rcp), ALU.mult),
        reads=[absx, rcp], writes=[absx], tag="q_div", strict=True)
    a = absx
    k.op("dve", lambda e: e.tensor_single_scalar(acc1[:, 0:N], a[:, 0:N], 0.25, ALU.is_gt),
         reads=[a], writes=[acc1], strict=True)
    for thr in (0.75, 1.25, 1.75):
        k.op("dve", lambda e, thr=thr: e.scalar_tensor_tensor(
            acc1[:, 0:N], a[:, 0:N], thr, acc1[:, 0:N], ALU.is_gt, ALU.add),
            reads=[a, acc1], writes=[acc1], strict=True)
    k.op("dve", lambda e: e.tensor_scalar(acc2[:, 0:N], a[:, 0:N], 5.0, 2.0, ALU.is_gt, ALU.mult),
         reads=[a], writes=[acc2], strict=True)
    for thr in (2.5, 3.5):
        k.op("dve", lambda e, thr=thr: e.scalar_tensor_tensor(
            acc2[:, 0:N], a[:, 0:N], thr, acc2[:, 0:N], ALU.is_gt, ALU.add),
            reads=[a, acc2], writes=[acc2], strict=True)
    k.op("dve", lambda e: e.scalar_tensor_tensor(
        acc1[:, 0:N], acc1[:, 0:N], 0.5, acc2[:, 0:N], ALU.mult, ALU.add),
        reads=[acc1, acc2], writes=[acc1], tag="q_comb", strict=True)
    k.op(eng2, lambda e: e.tensor_tensor(
        r3(oa), r3(acc1[:, 0:N]), rep(s), ALU.mult),
        reads=[acc1, s], writes=[out_bf], tag="q_deq", strict=True)
    I16 = mybir.dt.int16
    k.op("dve", lambda e: e.tensor_single_scalar(
        sb16[:, 0:N], sa.bitcast(I16)[:, 1:2 * N:2], -32768, ALU.bitwise_and),
        reads=[src], writes=[sb16], tag="q_sgn", strict=True)
    k.op("dve", lambda e: e.tensor_tensor(
        oa.bitcast(I16), oa.bitcast(I16), sb16[:, 0:N], ALU.bitwise_or),
        reads=[out_bf, sb16], writes=[out_bf], tag="q_or", strict=True)



OFF = 2816
NOC = OFF // 128
NOT = 6


def emit_l1_ffn(k: Ker, env, dins, xin, xout, wgq_scr, wuq_scr, dbg=None):
    with contextlib.ExitStack() as sc:
        stg = [k.sb(f"fstg{i}", [128, TTS], F32, sc) for i in range(2)]
        xsq_scr = [k.sb(f"fsq{i}", [128, TTS], F32, sc) for i in range(2)]
        rows = {n_: k.sb(f"frow{n_}", [1, TTS], F32, sc) for n_ in ("m", "v", "r")}
        mrcol = k.sb("mrcol", [128, 32], F32, sc)
        eyef = k.sb("eyef", [128, 128], F32, sc)
        k.dma("sp", eyef[:, :], dins["c_eyef"], writes=[eyef])
        scr = {
            "bm": k.sb("qbm", [128, 176], F32, sc),
            "s": k.sb("qs", [128, 176], F32, sc),
            "rcp": k.sb("qrcp", [128, 176], F32, sc),
            "absx": k.sb("qabs", [128, D], F32, sc),
            "acc1": k.sb("qa1", [128, D], F32, sc),
            "acc2": k.sb("qa2", [128, D], F32, sc),
            "sb16": k.sb("qsb", [128, D], mybir.dt.int16, sc),
        }
        wstage = [k.sb("fwst0", [128, OFF], F32, sc), k.sb("fwst1", [128, D], F32, sc)]
        wq_nat = k.sb("wqnat", [128, OFF], BF16, sc)
        trst = k.sb("trst", [128, 1024], BF16, sc)
        wdqT = k.sb("wdqT", [128, NOC * 1024], BF16, sc)
        xqT = k.sb("xqT", [128, NCH * TTS], BF16, sc)
        hqT = k.sb("hqT", [128, NOC * TTS], BF16, sc)
        wgt = [k.sb(f"wgt{i}", [128, NCH * TTS], BF16, sc) for i in range(2)]
        wut = [k.sb(f"wut{i}", [128, NCH * TTS], BF16, sc) for i in range(2)]
        x_tok = k.sb("x_tok", [128, D], F32, sc)
        xq_tok = k.sb("xq_tok", [128, D], BF16, sc)
        h_f32 = k.sb("h_f32", [128, TTS], F32, sc)
        sg_f32 = k.sb("sg_f32", [128, TTS], F32, sc)
        hq_bf = k.sb("hq_bf", [128, TTS], BF16, sc)
        pb_a = k.ps("fpba", [128, TTS], F32, sc)
        pb_b = k.ps("fpbb", [128, TTS], F32, sc)
        pb_g = k.ps("fpbg", [128, TTS], F32, sc)
        pb_u2 = k.ps("fpbu", [128, TTS], F32, sc)
        pb_t2 = k.ps("fpbt2", [128, 1024], F32, sc)
        pb_tr = k.ps("fpbtr", [128, 1024], BF16, sc)
        pb_y = [pb_g, pb_u2]

        for tt in range(NTT):
            ts = slice(tt * TTS, (tt + 1) * TTS)
            for c in range(NCH):
                st = stg[c % 2]
                k.dma("sp", st[:, :], xin[c * 128:(c + 1) * 128, ts],
                      writes=[st], tag="f_ldx")
                k.op("pe", lambda e, c=c, st=st: e.matmul(
                    pb_a[0:1, :], env.ones128[:, :], st[:, :],
                    start=(c == 0), stop=(c == NCH - 1)),
                    reads=[env.ones128, st], writes=[pb_a])
                sq = xsq_scr[c % 2]
                k.op("dve", lambda e, st=st, sq=sq: e.tensor_mul(
                    sq[:, :], st[:, :], st[:, :]),
                    reads=[st], writes=[sq])
                k.op("pe", lambda e, sq=sq, c=c: e.matmul(
                    pb_b[0:1, :], env.ones128[:, :], sq[:, :],
                    start=(c == 0), stop=(c == NCH - 1)),
                    reads=[env.ones128, sq], writes=[pb_b])
            emit_ln_rows(k, env, pb_a, pb_b, rows, D)
            for i in range(4):
                ci = tt * 4 + i
                k.op("pe", lambda e, i=i, ci=ci: e.transpose(
                    pb_t2[:, ci:ci + 1], rows["m"][0:1, i * 128:(i + 1) * 128], eyef[0:1, 0:1]),
                    reads=[rows["m"], eyef], writes=[pb_t2])
                k.op("pe", lambda e, i=i, ci=ci: e.transpose(
                    pb_t2[:, 16 + ci:16 + ci + 1], rows["r"][0:1, i * 128:(i + 1) * 128],
                    eyef[0:1, 0:1]),
                    reads=[rows["r"], eyef], writes=[pb_t2])
        k.op("dve", lambda e: e.tensor_copy(mrcol[:, 0:32], pb_t2[:, 0:32]),
             reads=[pb_t2], writes=[mrcol])

        def quant_gateup(dsrc, dscr):
            for ocn in range(NOC):
                st = wstage[ocn % 2]
                k.dma("sp", st[:, 0:D], dsrc[ocn * 128:(ocn + 1) * 128, :],
                      writes=[st], tag="f_ldw")
                emit_fp4_qdq(k, env, st, wq_nat, scr, D,
                             src_sl=st[:, 0:D], out_sl=wq_nat[:, 0:D], eng2="gp")
                for c in range(NCH):
                    k.op("pe", lambda e, c=c: e.transpose(
                        pb_tr[:, c * 128:(c + 1) * 128], wq_nat[:, c * 128:(c + 1) * 128],
                        env.eyeb[:, :]),
                        reads=[wq_nat, env.eyeb], writes=[pb_tr])
                k.op("act", lambda e: e.copy(trst[:, :], pb_tr[:, 0:1024]),
                     reads=[pb_tr], writes=[trst])
                k.dma("sp",
                      dscr.rearrange("(c p) o -> p c o", p=128)[:, :, ocn * 128:(ocn + 1) * 128],
                      trst[:, :].rearrange("p (c o) -> p c o", o=128),
                      reads=[trst], tag="f_stw")

        quant_gateup(dins["L1_ffn_w_gate"], wgq_scr)
        quant_gateup(dins["L1_ffn_w_up"], wuq_scr)

        for oc in range(NCH):
            st = wstage[0]
            k.dma("sp", st[:, :], dins["L1_ffn_w_down"][oc * 128:(oc + 1) * 128, :],
                  writes=[st], tag="f_ldwd")
            for o0 in range(0, OFF, D):
                n_ = min(D, OFF - o0)
                emit_fp4_qdq(k, env, st, wq_nat, scr, n_,
                             src_sl=st[:, o0:o0 + n_], out_sl=wq_nat[:, o0:o0 + n_],
                             eng2="gp")
            for b0 in range(0, NOC, 8):
                b1 = min(NOC, b0 + 8)
                for kc in range(b0, b1):
                    k.op("pe", lambda e, kc=kc, b0=b0: e.transpose(
                        pb_tr[:, (kc - b0) * 128:(kc - b0 + 1) * 128],
                        wq_nat[:, kc * 128:(kc + 1) * 128], env.eyeb[:, :]),
                        reads=[wq_nat, env.eyeb], writes=[pb_tr])
                n = b1 - b0
                k.op("act", lambda e, b0=b0, n=n, oc=oc: e.copy(
                    wdqT[:, :].rearrange("p (kc o) -> p kc o", o=1024)[
                        :, b0:b0 + n, oc * 128:(oc + 1) * 128],
                    pb_tr[:, 0:n * 128].rearrange("p (kc o) -> p kc o", o=128)),
                    reads=[pb_tr], writes=[wdqT])

        if dbg and "wdqT" in dbg:
            k.dma("sp", dbg["wdqT"], wdqT[:, :], reads=[wdqT], tag="dbg")
        for tt in range(NTT):
            ts = slice(tt * TTS, (tt + 1) * TTS)
            for tsub in range(4):
                t0 = tt * TTS + tsub * 128
                for c in range(NCH):
                    st = stg[c % 2]
                    k.dma("sp", st[:, 0:128], xin[c * 128:(c + 1) * 128, t0:t0 + 128],
                          writes=[st], tag="f_ldx2")
                    k.op("pe", lambda e, c=c, st=st: e.transpose(
                        pb_t2[:, c * 128:(c + 1) * 128], st[:, 0:128], eyef[:, :]),
                        reads=[st, eyef], writes=[pb_t2])
                k.op("act", lambda e: e.copy(x_tok[:, :], pb_t2[:, 0:D]),
                     reads=[pb_t2], writes=[x_tok])
                ci = tt * 4 + tsub
                k.op("dve", lambda e, ci=ci: e.tensor_scalar(
                    x_tok[:, :], x_tok[:, :], mrcol[:, ci:ci + 1], mrcol[:, 16 + ci:16 + ci + 1],
                    ALU.subtract, ALU.mult),
                    reads=[x_tok, mrcol], writes=[x_tok])
                emit_fp4_qdq(k, env, x_tok, xq_tok, scr, D, eng2="gp")
                for c in range(NCH):
                    k.op("pe", lambda e, c=c: e.transpose(
                        pb_tr[:, c * 128:(c + 1) * 128], xq_tok[:, c * 128:(c + 1) * 128],
                        env.eyeb[:, :]),
                        reads=[xq_tok, env.eyeb], writes=[pb_tr])
                k.op("act", lambda e, tsub=tsub: e.copy(
                    xqT[:, :].rearrange("p (c t) -> p c t", t=TTS)[:, :, tsub * 128:(tsub + 1) * 128],
                    pb_tr[:, 0:1024].rearrange("p (c t) -> p c t", t=128)),
                    reads=[pb_tr], writes=[xqT])
            if dbg and tt == 0 and "xqT" in dbg:
                k.dma("sp", dbg["xqT"], xqT[:, :], reads=[xqT], tag="dbg")
            for ot in range(NOT):
                osz = 512 if ot < 5 else 256
                kcs = list(range(ot * 4, ot * 4 + osz // 128))
                wg = wgt[ot % 2]
                wu = wut[ot % 2]
                k.dma("sp",
                      wg[:, 0:NCH * osz].rearrange("p (c o) -> p c o", o=osz),
                      wgq_scr.rearrange("(c p) o -> p c o", p=128)[:, :, ot * 512:ot * 512 + osz],
                      writes=[wg], tag="f_ldgw")
                k.dma("sp",
                      wu[:, 0:NCH * osz].rearrange("p (c o) -> p c o", o=osz),
                      wuq_scr.rearrange("(c p) o -> p c o", p=128)[:, :, ot * 512:ot * 512 + osz],
                      writes=[wu], tag="f_lduw")
                for tsub in range(4):
                    tso = tsub * 128
                    for c in range(NCH):
                        k.op("pe", lambda e, c=c, tso=tso, wg=wg, osz=osz: e.matmul(
                            pb_g[:, 0:osz], xqT[:, c * TTS + tso:c * TTS + tso + 128],
                            wg[:, c * osz:(c + 1) * osz],
                            start=(c == 0), stop=(c == NCH - 1)),
                            reads=[xqT, wg], writes=[pb_g])
                    for c in range(NCH):
                        k.op("pe", lambda e, c=c, tso=tso, wu=wu, osz=osz: e.matmul(
                            pb_u2[:, 0:osz], xqT[:, c * TTS + tso:c * TTS + tso + 128],
                            wu[:, c * osz:(c + 1) * osz],
                            start=(c == 0), stop=(c == NCH - 1)),
                            reads=[xqT, wu], writes=[pb_u2])
                    k.op("act", lambda e, osz=osz: e.activation(
                        sg_f32[:, 0:osz], pb_g[:, 0:osz], AF.Sigmoid),
                        reads=[pb_g], writes=[sg_f32])
                    k.op("dve", lambda e, osz=osz: e.tensor_tensor(
                        sg_f32[:, 0:osz], sg_f32[:, 0:osz], pb_g[:, 0:osz], ALU.mult),
                        reads=[sg_f32, pb_g], writes=[sg_f32])
                    k.op("dve", lambda e, osz=osz: e.tensor_tensor(
                        h_f32[:, 0:osz], sg_f32[:, 0:osz], pb_u2[:, 0:osz], ALU.mult),
                        reads=[sg_f32, pb_u2], writes=[h_f32])
                    emit_fp4_qdq(k, env, h_f32, hq_bf, scr, osz,
                                 src_sl=h_f32[:, 0:osz], out_sl=hq_bf[:, 0:osz], eng2="gp")
                    for i, kc in enumerate(kcs):
                        k.op("pe", lambda e, i=i: e.transpose(
                            pb_tr[:, i * 128:(i + 1) * 128], hq_bf[:, i * 128:(i + 1) * 128],
                            env.eyeb[:, :]),
                            reads=[hq_bf, env.eyeb], writes=[pb_tr])
                    nk = len(kcs)
                    k.op("act", lambda e, nk=nk, kcs=kcs, tso=tso: e.copy(
                        hqT[:, :].rearrange("p (kc t) -> p kc t", t=TTS)[
                            :, kcs[0]:kcs[0] + nk, tso:tso + 128],
                        pb_tr[:, 0:nk * 128].rearrange("p (kc t) -> p kc t", t=128)),
                        reads=[pb_tr], writes=[hqT])
            if dbg and tt == 0 and "hqT" in dbg:
                k.dma("sp", dbg["hqT"], hqT[:, :], reads=[hqT], tag="dbg")
            for oc in range(NCH):
                pb = pb_y[oc % 2]
                for kc in range(NOC):
                    k.op("pe", lambda e, kc=kc, oc=oc, pb=pb: e.matmul(
                        pb[:, :], wdqT[:, kc * 1024 + oc * 128:kc * 1024 + (oc + 1) * 128],
                        hqT[:, kc * TTS:(kc + 1) * TTS],
                        start=(kc == 0), stop=(kc == NOC - 1)),
                        reads=[wdqT, hqT], writes=[pb])
                st = stg[oc % 2]
                k.dma("sp", st[:, :], xin[oc * 128:(oc + 1) * 128, ts],
                      writes=[st], tag="f_ldx3")
                sq = xsq_scr[oc % 2]
                k.op("dve", lambda e, pb=pb, st=st, sq=sq: e.tensor_tensor(
                    sq[:, :], st[:, :], pb[:, :], ALU.add),
                    reads=[st, pb], writes=[sq])
                k.dma("sp", xout[oc * 128:(oc + 1) * 128, ts], sq[:, :],
                      reads=[sq], tag="f_stx")
    k.barrier()


def host_prep_l1_ffn(inputs):
    return {
        "L1_ffn_w_gate": np.ascontiguousarray(inputs["L1_ffn_w_gate"]),
        "L1_ffn_w_up": np.ascontiguousarray(inputs["L1_ffn_w_up"]),
        "L1_ffn_w_down": np.ascontiguousarray(inputs["L1_ffn_w_down"]),
    }


_L1FFN_SPECS = {
    "L1_ffn_w_gate": ([OFF, D], F32),
    "L1_ffn_w_up": ([OFF, D], F32),
    "L1_ffn_w_down": ([D, OFF], F32),
}


def build_l1_ffn_program(debug=False):
    nc = bass.Bass("TRN2", target_bir_lowering=False)
    stack = contextlib.ExitStack()
    dins = {}
    specs = {**_CONST_SPECS, "c_eyef": ([128, 128], F32), **_L1FFN_SPECS}
    for name, (shape, dt) in specs.items():
        dins[name] = nc.dram_tensor(name, shape, dt, kind="ExternalInput").ap()
    xin = nc.dram_tensor("xin", [D, T], F32, kind="ExternalInput").ap()
    xout = nc.dram_tensor("xout", [D, T], F32, kind="ExternalOutput").ap()
    kind = dict(kind="ExternalOutput") if debug else {}
    wgq_scr = nc.dram_tensor("wgq_scr", [D, OFF], BF16, **kind).ap()
    wuq_scr = nc.dram_tensor("wuq_scr", [D, OFF], BF16, **kind).ap()
    dbg = None
    if debug:
        dbg = {
            "wdqT": nc.dram_tensor("dbg_wdqT", [128, NOC * 1024], BF16, kind="ExternalOutput").ap(),
            "xqT": nc.dram_tensor("dbg_xqT", [128, NCH * TTS], BF16, kind="ExternalOutput").ap(),
            "hqT": nc.dram_tensor("dbg_hqT", [128, NOC * TTS], BF16, kind="ExternalOutput").ap(),
        }
    with stack:
        k = Ker(nc, stack)
        env = Env(k, dins)
        k.barrier()
        emit_l1_ffn(k, env, dins, xin, xout, wgq_scr, wuq_scr, dbg)
        k.lower(verbose=True)
    return nc

HID = 4096
NHC = HID // 128
NHEAD = 16
HD = 64


def emit_l1_attn(k: Ker, env, dins, xin, xout, w1b_scr, w2b_scr, rows_scr):
    names = ("q", "k", "v")
    with contextlib.ExitStack() as sc0:
        kT = k.sb("l1kT", [128, NCH * T], BF16, sc0)
        qT = k.sb("l1qT", [128, NCH * T], BF16, sc0)
        vaug = [k.sb(f"l1va{i}", [128, NHEAD * 65], BF16, sc0) for i in range(T // 128)]
        eyef = k.sb("l1eyef", [128, 128], F32, sc0)
        onesrb = k.sb("onesrb", [1, TTS], BF16, sc0)
        k.dma("sp", eyef[:, :], dins["c_eyef"], writes=[eyef])
        k.dma("sp", onesrb[:, :], dins["c_onesrowb"], writes=[onesrb])

        with contextlib.ExitStack() as sc:
            stg = [k.sb(f"astg{i}", [128, TTS], F32, sc) for i in range(2)]
            sq = [k.sb(f"asq{i}", [128, TTS], F32, sc) for i in range(2)]
            rows = {n_: k.sb(f"arow{n_}", [1, TTS], F32, sc) for n_ in ("m", "v", "r")}
            pb_a = k.ps("apba", [128, TTS], F32, sc)
            pb_b = k.ps("apbb", [128, TTS], F32, sc)
            for tt in range(NTT):
                ts = slice(tt * TTS, (tt + 1) * TTS)
                for c in range(NCH):
                    st = stg[c % 2]
                    k.dma("sp", st[:, :], xin[c * 128:(c + 1) * 128, ts],
                          writes=[st], tag="a_ldx")
                    k.op("pe", lambda e, st=st, c=c: e.matmul(
                        pb_a[0:1, :], env.ones128[:, :], st[:, :],
                        start=(c == 0), stop=(c == NCH - 1)),
                        reads=[env.ones128, st], writes=[pb_a])
                    s2 = sq[c % 2]
                    k.op("dve", lambda e, st=st, s2=s2: e.tensor_mul(
                        s2[:, :], st[:, :], st[:, :]),
                        reads=[st], writes=[s2])
                    k.op("pe", lambda e, s2=s2, c=c: e.matmul(
                        pb_b[0:1, :], env.ones128[:, :], s2[:, :],
                        start=(c == 0), stop=(c == NCH - 1)),
                        reads=[env.ones128, s2], writes=[pb_b])
                emit_ln_rows(k, env, pb_a, pb_b, rows, D)
                k.dma("sp", rows_scr[0:1, ts], rows["m"][0:1, :], reads=[rows["m"]],
                      tag="a_strow")
                k.dma("sp", rows_scr[1:2, ts], rows["r"][0:1, :], reads=[rows["r"]],
                      tag="a_strow")
        k.barrier()

        with contextlib.ExitStack() as sc:
            wst = [k.sb(f"cwst{i}", [128, HID], F32, sc) for i in range(2)]
            wbf = k.sb("cwbf", [128, HID], BF16, sc)
            for j, nm in enumerate(names):
                for c in range(NCH):
                    st = wst[c % 2]
                    k.dma("sp", st[:, :], dins[f"L1_attn_{nm}_w1"][c * 128:(c + 1) * 128, :],
                          writes=[st], tag="a_ldw1")
                    k.op("dve", lambda e, st=st: e.tensor_copy(wbf[:, :], st[:, :]),
                         reads=[st], writes=[wbf])
                    k.dma("sp", w1b_scr[j, c * 128:(c + 1) * 128, :], wbf[:, :],
                          reads=[wbf], tag="a_stw1")
                for hc in range(NHC):
                    st = wst[hc % 2]
                    k.dma("sp", st[:, 0:1024], dins[f"L1_attn_{nm}_w2"][hc * 128:(hc + 1) * 128, :],
                          writes=[st], tag="a_ldw2")
                    k.op("dve", lambda e, st=st: e.tensor_copy(wbf[:, 0:1024], st[:, 0:1024]),
                         reads=[st], writes=[wbf])
                    k.dma("sp", w2b_scr[j, hc * 128:(hc + 1) * 128, :], wbf[:, 0:1024],
                          reads=[wbf], tag="a_stw2")
        k.barrier()

        for j, nm in enumerate(names):
            with contextlib.ExitStack() as sc:
                b1r = k.sb("ab1r", [1, HID], BF16, sc)
                b2r = k.sb("ab2r", [1, D], BF16, sc)
                mrow = k.sb("amrow", [1, TTS], F32, sc)
                rrow = k.sb("arrow", [1, TTS], F32, sc)
                for i_ in range(HID // TTS):
                    k.dma("sp", mrow[:, :], dins[f"L1_attn_{nm}_b1"][0:1, i_ * TTS:(i_ + 1) * TTS],
                          writes=[mrow], tag="a_ldb1")
                    k.op("dve", lambda e, i_=i_: e.tensor_copy(
                        b1r[0:1, i_ * TTS:(i_ + 1) * TTS], mrow[0:1, :]),
                        reads=[mrow], writes=[b1r])
                for i_ in range(D // TTS):
                    k.dma("sp", mrow[:, :], dins[f"L1_attn_{nm}_b2"][0:1, i_ * TTS:(i_ + 1) * TTS],
                          writes=[mrow], tag="a_ldb2")
                    k.op("dve", lambda e, i_=i_: e.tensor_copy(
                        b2r[0:1, i_ * TTS:(i_ + 1) * TTS], mrow[0:1, :]),
                        reads=[mrow], writes=[b2r])
                stg = [k.sb(f"bstg{i}", [128, TTS], F32, sc) for i in range(2)]
                xn = [k.sb(f"axn{c}", [128, TTS], BF16, sc) for c in range(NCH)]
                apscr = [k.sb(f"aap{i}", [128, TTS], F32, sc) for i in range(2)]
                h_tt = k.sb("ahtt", [128, NHC * TTS], BF16, sc)
                w1w = [k.sb(f"aw1w{i}", [128, NCH * 128], BF16, sc) for i in range(2)]
                w2h = k.sb("aw2h", [128, NHC * TTS // 1], BF16, sc)
                pb_a = k.ps("bpba", [128, TTS], F32, sc)
                pb_b = k.ps("bpbb", [128, TTS], F32, sc)
                pb_h = [k.ps(f"bpbh{i}", [128, TTS], F32, sc) for i in range(2)]
                pb_o = [k.ps(f"bpbo{i}", [128, TTS], F32, sc) for i in range(2)]
                for tt in range(NTT):
                    ts = slice(tt * TTS, (tt + 1) * TTS)
                    k.dma("sp", mrow[:, :], rows_scr[0:1, ts], writes=[mrow], tag="a_ldrow")
                    k.dma("sp", rrow[:, :], rows_scr[1:2, ts], writes=[rrow], tag="a_ldrow")
                    k.op("pe", lambda e: e.matmul(pb_a[:, :], env.ones1r[:, :], mrow[0:1, :],
                                                  start=True, stop=True),
                         reads=[env.ones1r, mrow], writes=[pb_a])
                    k.op("pe", lambda e: e.matmul(pb_b[:, :], env.ones1r[:, :], rrow[0:1, :],
                                                  start=True, stop=True),
                         reads=[env.ones1r, rrow], writes=[pb_b])
                    for c in range(NCH):
                        st = stg[c % 2]
                        k.dma("sp", st[:, :], xin[c * 128:(c + 1) * 128, ts],
                              writes=[st], tag="a_ldx2")
                        scr = apscr[c % 2]
                        k.op("dve", lambda e, st=st, scr=scr: e.tensor_tensor(
                            scr[:, :], st[:, :], pb_a[:, :], ALU.subtract),
                            reads=[st, pb_a], writes=[scr])
                        k.op("dve", lambda e, c=c, scr=scr: e.tensor_tensor(
                            xn[c][:, :], scr[:, :], pb_b[:, :], ALU.mult),
                            reads=[scr, pb_b], writes=[xn[c]])
                    for hc in range(NHC):
                        ww = w1w[hc % 2]
                        k.dma("sp",
                              ww[:, :].rearrange("p (c n) -> p c n", n=128),
                              w1b_scr[j].rearrange("(c p) n -> p c n", p=128)[
                                  :, :, hc * 128:(hc + 1) * 128],
                              writes=[ww], tag="a_ldw1b")
                        pbh = pb_h[hc % 2]
                        for dc in range(NCH):
                            k.op("pe", lambda e, dc=dc, ww=ww, pbh=pbh: e.matmul(
                                pbh[:, :], ww[:, dc * 128:(dc + 1) * 128], xn[dc][:, :],
                                start=(dc == 0), stop=False),
                                reads=[ww, xn[dc]], writes=[pbh])
                        k.op("pe", lambda e, hc=hc, pbh=pbh: e.matmul(
                            pbh[:, :], b1r[0:1, hc * 128:(hc + 1) * 128], onesrb[0:1, :],
                            start=False, stop=True),
                            reads=[b1r, onesrb], writes=[pbh])
                        k.op("act", lambda e, hc=hc, pbh=pbh: e.activation(
                            h_tt[:, hc * TTS:(hc + 1) * TTS], pbh[:, :], AF.Gelu),
                            reads=[pbh], writes=[h_tt])
                    for ohalf in range(2):
                        osl = slice(ohalf * 512, ohalf * 512 + 512)
                        for g in range(4):
                            k.dma("sp",
                                  w2h[:, g * 8 * TTS:(g + 1) * 8 * TTS].rearrange(
                                      "p (c n) -> p c n", n=TTS),
                                  w2b_scr[j].rearrange("(c p) n -> p c n", p=128)[
                                      :, g * 8:(g + 1) * 8, osl],
                                  writes=[w2h], tag="a_ldw2b")
                        if nm == "v":
                            for tsub in range(4):
                                ti = tt * 4 + tsub
                                pbo = pb_o[tsub % 2]
                                for hc in range(NHC):
                                    k.op("pe", lambda e, hc=hc, tsub=tsub, pbo=pbo: e.matmul(
                                        pbo[:, :],
                                        h_tt[:, hc * TTS + tsub * 128:hc * TTS + (tsub + 1) * 128],
                                        w2h[:, hc * TTS:(hc + 1) * TTS],
                                        start=(hc == 0), stop=False),
                                        reads=[h_tt, w2h], writes=[pbo])
                                k.op("pe", lambda e, osl=osl, pbo=pbo: e.matmul(
                                    pbo[:, :], env.ones1rb[0:1, :], b2r[0:1, osl],
                                    start=False, stop=True),
                                    reads=[env.ones1rb, b2r], writes=[pbo])
                                h0 = ohalf * 8
                                k.op("dve", lambda e, ti=ti, h0=h0, pbo=pbo: e.tensor_copy(
                                    vaug[ti][:, :].rearrange("p (hh c) -> p hh c", c=65)[
                                        :, h0:h0 + 8, 0:64],
                                    pbo[:, :].rearrange("p (hh c) -> p hh c", c=64)),
                                    reads=[pbo], writes=[vaug[ti]])
                        else:
                            dst = qT if nm == "q" else kT
                            for oq in range(4):
                                oc = ohalf * 4 + oq
                                pbo = pb_o[oq % 2]
                                for hc in range(NHC):
                                    k.op("pe", lambda e, hc=hc, oq=oq, pbo=pbo: e.matmul(
                                        pbo[:, :], w2h[:, hc * TTS + oq * 128:hc * TTS + (oq + 1) * 128],
                                        h_tt[:, hc * TTS:(hc + 1) * TTS],
                                        start=(hc == 0), stop=False),
                                        reads=[w2h, h_tt], writes=[pbo])
                                k.op("pe", lambda e, oc=oc, pbo=pbo: e.matmul(
                                    pbo[:, :], b2r[0:1, oc * 128:(oc + 1) * 128], onesrb[0:1, :],
                                    start=False, stop=True),
                                    reads=[b2r, onesrb], writes=[pbo])
                                k.op("dve", lambda e, oc=oc, tt=tt, dst=dst, pbo=pbo: e.tensor_copy(
                                    dst[:, oc * T + tt * TTS:oc * T + (tt + 1) * TTS], pbo[:, :]),
                                    reads=[pbo], writes=[dst])
            k.barrier()

        for ti in range(T // 128):
            k.op("dve", lambda e, ti=ti: e.memset(
                vaug[ti][:, :].rearrange("p (hh c) -> p hh c", c=65)[:, :, 64:65], 1.0),
                writes=[vaug[ti]], strict=True)

        with contextlib.ExitStack() as sc:
            wp = k.sb("awp", [128, NCH * 1024], BF16, sc)
            wpst = [k.sb(f"awpst{i}", [128, 1024], F32, sc) for i in range(2)]
            for c in range(NCH):
                st = wpst[c % 2]
                k.dma("sp", st[:, :], dins["L1_attn_proj_w"][c * 128:(c + 1) * 128, :],
                      writes=[st], tag="a_ldwp")
                k.op("dve", lambda e, st=st, c=c: e.tensor_copy(
                    wp[:, c * 1024:(c + 1) * 1024], st[:, :]),
                    reads=[st], writes=[wp])
            pbrow = k.sb("apbr", [1, D], BF16, sc)
            pbst = k.sb("apbst", [1, D], F32, sc)
            k.dma("sp", pbst[:, :], dins["L1_attn_proj_b"], writes=[pbst])
            k.op("dve", lambda e: e.tensor_copy(pbrow[:, :], pbst[:, :]),
                 reads=[pbst], writes=[pbrow])
            nsq = k.sb("ansq", [128, TTS], F32, sc)
            nB = k.sb("anB", [1, 32], F32, sc)
            negB = k.sb("anegB", [128, 16], F32, sc)
            ptil = [k.sb(f"aptil{i}", [128, TTS], BF16, sc) for i in range(3)]
            otok4 = [k.sb(f"aotok{i}", [128, NHEAD * HD], BF16, sc) for i in range(4)]
            otT = k.sb("aotT", [128, NCH * TTS], BF16, sc)
            rq2 = [k.sb(f"arq{i}", [128, 1], F32, sc) for i in range(2)]
            stg = [k.sb(f"cstg{i}", [128, TTS], F32, sc) for i in range(2)]
            res = [k.sb(f"cres{i}", [128, TTS], F32, sc) for i in range(2)]
            pb_s = [k.ps(f"cpbs{i}", [128, TTS], F32, sc) for i in range(2)]
            pb_av = [k.ps(f"cpbav{j}", [128, TTS], F32, sc) for j in range(2)]
            pb_t = k.ps("cpbt", [128, 1024], BF16, sc)
            pb_y = k.ps("cpby", [128, TTS], F32, sc)

            for h in range(NHEAD):
                c0, r0 = h // 2, (h % 2) * 64
                psl = slice(r0, r0 + 64)
                for which, src_t in ((0, qT), (1, kT)):
                    for tt4 in range(NTT):
                        k.op("dve", lambda e, src_t=src_t, c0=c0, psl=psl, tt4=tt4: e.tensor_mul(
                            nsq[psl, :],
                            src_t[psl, c0 * T + tt4 * TTS:c0 * T + (tt4 + 1) * TTS],
                            src_t[psl, c0 * T + tt4 * TTS:c0 * T + (tt4 + 1) * TTS]),
                            reads=[src_t], writes=[nsq], strict=True)
                        k.op("pe", lambda e, psl=psl: e.matmul(
                            pb_s[0][0:1, :], env.ones128[psl, :], nsq[psl, :],
                            start=True, stop=True),
                            reads=[env.ones128, nsq], writes=[pb_s[0]])
                        k.op("dve", lambda e, which=which, tt4=tt4: e.reduce_max(
                            nB[0:1, 3:4], pb_s[0][0:1, :], axis=AX.X),
                            reads=[pb_s[0]], writes=[nB], strict=True)
                        if tt4 == 0:
                            k.op("dve", lambda e, which=which: e.tensor_copy(
                                nB[0:1, which:which + 1], nB[0:1, 3:4]),
                                reads=[nB], writes=[nB], strict=True)
                        else:
                            k.op("dve", lambda e, which=which: e.tensor_tensor(
                                nB[0:1, which:which + 1], nB[0:1, which:which + 1],
                                nB[0:1, 3:4], ALU.max),
                                reads=[nB], writes=[nB], strict=True)
                k.op("dve", lambda e: e.tensor_mul(nB[0:1, 2:3], nB[0:1, 0:1], nB[0:1, 1:2]),
                     reads=[nB], writes=[nB], strict=True)
                k.op("act", lambda e: e.activation(nB[0:1, 2:3], nB[0:1, 2:3], AF.Ln, bias=1e-12),
                     reads=[nB], writes=[nB], strict=True)
                k.op("act", lambda e: e.activation(nB[0:1, 2:3], nB[0:1, 2:3], AF.Exp, scale=0.5),
                     reads=[nB], writes=[nB], strict=True)
                k.op("dve", lambda e, h=h: e.tensor_single_scalar(
                    nB[0:1, 8 + h:9 + h], nB[0:1, 2:3], -0.125, ALU.mult),
                    reads=[nB], writes=[nB], strict=True)
            for h in range(NHEAD):
                k.op("pe", lambda e, h=h: e.matmul(
                    pb_s[0][:, h:h + 1], env.ones1r[:, :], nB[0:1, 8 + h:9 + h],
                    start=True, stop=True),
                    reads=[env.ones1r, nB], writes=[pb_s[0]])
            k.op("dve", lambda e: e.tensor_copy(negB[:, 0:16], pb_s[0][:, 0:16]),
                 reads=[pb_s[0]], writes=[negB], strict=True)

            for qt in range(NTT):
                qs0 = qt * TTS
                for h in range(NHEAD):
                    c0, r0 = h // 2, (h % 2) * 64
                    psl = slice(r0, r0 + 64)
                    NKT = T // 128
                    pav = pb_av[h % 2]
                    for kt in range(NKT + 1):
                        if kt < NKT:
                            pbs = pb_s[kt % 2]
                            k.op("pe", lambda e, kt=kt, c0=c0, psl=psl, qs0=qs0, pbs=pbs: e.matmul(
                                pbs[:, :], kT[psl, c0 * T + kt * 128:c0 * T + (kt + 1) * 128],
                                qT[psl, c0 * T + qs0:c0 * T + qs0 + TTS],
                                start=True, stop=True),
                                reads=[kT, qT], writes=[pbs])
                            pt = ptil[kt % 3]
                            k.op("act", lambda e, h=h, pbs=pbs, pt=pt: e.activation(
                                pt[:, :], pbs[:, :], AF.Exp, bias=negB[:, h:h + 1], scale=0.125),
                                reads=[pbs, negB], writes=[pt])
                        if kt >= 1:
                            ktp = kt - 1
                            ptp = ptil[ktp % 3]
                            for qsub in range(4):
                                k.op("pe", lambda e, ktp=ktp, qsub=qsub, h=h, ptp=ptp, pav=pav: e.matmul(
                                    pav[:, qsub * 128:qsub * 128 + 65], ptp[:, qsub * 128:(qsub + 1) * 128],
                                    vaug[ktp][:, h * 65:(h + 1) * 65],
                                    start=(ktp == 0), stop=(ktp == NKT - 1)),
                                    reads=[ptp, vaug[ktp]], writes=[pav])
                    pav = pb_av[h % 2]
                    rq = rq2[h % 2]
                    for qsub in range(4):
                        k.op("dve", lambda e, qsub=qsub, pav=pav, rq=rq: e.reciprocal(
                            rq[:, 0:1], pav[:, qsub * 128 + 64:qsub * 128 + 65]),
                            reads=[pav], writes=[rq], strict=True)
                        k.op("dve", lambda e, h=h, qsub=qsub, pav=pav, rq=rq: e.tensor_scalar(
                            otok4[qsub][:, h * HD:(h + 1) * HD], pav[:, qsub * 128:qsub * 128 + 64],
                            rq[:, 0:1], None, ALU.mult),
                            reads=[pav, rq], writes=[otok4[qsub]])
                for qsub in range(4):
                    for c in range(NCH):
                        k.op("pe", lambda e, c=c, qsub=qsub: e.transpose(
                            pb_t[:, c * 128:(c + 1) * 128], otok4[qsub][:, c * 128:(c + 1) * 128],
                            env.eyeb[:, :]),
                            reads=[otok4[qsub], env.eyeb], writes=[pb_t])
                    k.op("dve", lambda e, qsub=qsub: e.tensor_copy(
                        otT[:, :].rearrange("p (c n) -> p c n", n=TTS)[
                            :, :, qsub * 128:(qsub + 1) * 128],
                        pb_t[:, :].rearrange("p (c n) -> p c n", n=128)),
                        reads=[pb_t], writes=[otT])
                for oc in range(NCH):
                    for fc in range(NCH):
                        k.op("pe", lambda e, fc=fc, oc=oc: e.matmul(
                            pb_y[:, :], wp[:, fc * 1024 + oc * 128:fc * 1024 + (oc + 1) * 128],
                            otT[:, fc * TTS:(fc + 1) * TTS],
                            start=(fc == 0), stop=False),
                            reads=[wp, otT], writes=[pb_y])
                    k.op("pe", lambda e, oc=oc: e.matmul(
                        pb_y[:, :], pbrow[0:1, oc * 128:(oc + 1) * 128], onesrb[0:1, :],
                        start=False, stop=True),
                        reads=[pbrow, onesrb], writes=[pb_y])
                    st = stg[oc % 2]
                    k.dma("sp", st[:, :], xin[oc * 128:(oc + 1) * 128, qs0:qs0 + TTS],
                          writes=[st], tag="a_ldx3")
                    rs = res[oc % 2]
                    k.op("dve", lambda e, st=st, rs=rs: e.tensor_tensor(
                        rs[:, :], st[:, :], pb_y[:, :], ALU.add),
                        reads=[st, pb_y], writes=[rs])
                    k.dma("sp", xout[oc * 128:(oc + 1) * 128, qs0:qs0 + TTS], rs[:, :],
                          reads=[rs], tag="a_stx")
    k.barrier()


def host_prep_l1_attn(inputs):
    d = {}
    for nm in ("q", "k", "v"):
        d[f"L1_attn_{nm}_w1"] = np.ascontiguousarray(inputs[f"L1_attn_{nm}_w1"])
        d[f"L1_attn_{nm}_w2"] = np.ascontiguousarray(inputs[f"L1_attn_{nm}_w2"])
        d[f"L1_attn_{nm}_b1"] = inputs[f"L1_attn_{nm}_b1"].reshape(1, HID)
        d[f"L1_attn_{nm}_b2"] = inputs[f"L1_attn_{nm}_b2"].reshape(1, D)
    d["L1_attn_proj_w"] = np.ascontiguousarray(inputs["L1_attn_proj_w"])
    d["L1_attn_proj_b"] = inputs["L1_attn_proj_b"].reshape(1, D)
    return d


_L1ATTN_SPECS = {}
for _nm in ("q", "k", "v"):
    _L1ATTN_SPECS[f"L1_attn_{_nm}_w1"] = ([D, HID], F32)
    _L1ATTN_SPECS[f"L1_attn_{_nm}_w2"] = ([HID, D], F32)
    _L1ATTN_SPECS[f"L1_attn_{_nm}_b1"] = ([1, HID], F32)
    _L1ATTN_SPECS[f"L1_attn_{_nm}_b2"] = ([1, D], F32)
_L1ATTN_SPECS["L1_attn_proj_w"] = ([D, D], F32)
_L1ATTN_SPECS["L1_attn_proj_b"] = ([1, D], F32)


def build_l1_attn_program():
    nc = bass.Bass("TRN2", target_bir_lowering=False)
    stack = contextlib.ExitStack()
    dins = {}
    specs = {**_CONST_SPECS, "c_eyef": ([128, 128], F32),
             "c_onesrowb": ([1, TTS], BF16), "c_ones1rb": ([1, 128], BF16),
             **_L1ATTN_SPECS}
    for name, (shape, dt) in specs.items():
        dins[name] = nc.dram_tensor(name, shape, dt, kind="ExternalInput").ap()
    xin = nc.dram_tensor("xin", [D, T], F32, kind="ExternalInput").ap()
    xout = nc.dram_tensor("xout", [D, T], F32, kind="ExternalOutput").ap()
    w1b_scr = nc.dram_tensor("w1b_scr", [3, D, HID], BF16).ap()
    w2b_scr = nc.dram_tensor("w2b_scr", [3, HID, D], BF16).ap()
    rows_scr = nc.dram_tensor("rows_scr", [2, T], F32).ap()
    with stack:
        k = Ker(nc, stack)
        env = Env(k, dins)
        k.barrier()
        emit_l1_attn(k, env, dins, xin, xout, w1b_scr, w2b_scr, rows_scr)
        k.lower(verbose=True)
    return nc

L0CUT = 99
NCK_RUN = 99
CHUNK_BAR = False
STAGE_CUT = 99
DK = 128
DV = 256
NH4 = 4
CC = 128
NCK = T // CC


def emit_l0_attn(k: Ker, env, dins, xin, xout, dbg=None):
    with contextlib.ExitStack() as sc0:
        eyef = k.sb("deyef", [128, 128], F32, sc0)
        umask_s = k.sb("dums", [128, 128], BF16, sc0)
        umask_i = k.sb("dumi", [128, 128], BF16, sc0)
        k.dma("sp", eyef[:, :], dins["c_eyef"], writes=[eyef])
        k.dma("sp", umask_s[:, :], dins["c_umask_s"], writes=[umask_s])
        k.dma("sp", umask_i[:, :], dins["c_umask_i"], writes=[umask_i])
        xn = [k.sb(f"dxn{c}", [128, T], BF16, sc0) for c in range(NCH)]
        qT = k.sb("dqT", [128, NH4 * T], BF16, sc0)
        kT = k.sb("dkT", [128, NH4 * T], BF16, sc0)
        beta = [k.sb(f"dbeta{i}", [65, T], BF16, sc0) for i in range(2)]
        ones65 = k.sb("dones65", [65, 128], BF16, sc0)
        k.dma("sp", ones65[:, :], dins["c_ones65"], writes=[ones65])
        wq = k.sb("dwq", [128, NCH * 512], BF16, sc0)
        wk = k.sb("dwk", [128, NCH * 512], BF16, sc0)
        wv = k.sb("dwv", [128, NCH * D], BF16, sc0)
        wg = k.sb("dwg", [128, NCH * D], BF16, sc0)
        wo = k.sb("dwo", [128, NCH * D], BF16, sc0)
        wb = k.sb("dwb", [128, NCH * 4], BF16, sc0)
        nwcol = k.sb("dnw", [128, 2], F32, sc0)
        k.dma("sp", nwcol[:, :], dins["L0_attn_nwcol"], writes=[nwcol])

        QSCALE = 2.0 ** -3.5

        def load_w(dst, src_ap, ncols, scale=None, nw=False):
            for dc in range(NCH):
                st = wst[dc % 2]
                k.dma("sp", st[:, 0:ncols], src_ap[dc * 128:(dc + 1) * 128, :],
                      writes=[st], tag="d_ldw")
                if nw:
                    k.op("dve", lambda e, dc=dc, st=st: e.tensor_scalar(
                        dst[:, dc * ncols:(dc + 1) * ncols], st[:, 0:ncols],
                        nwcol[:, dc % 2:dc % 2 + 1], None, ALU.mult),
                        reads=[st, nwcol], writes=[dst])
                elif scale is not None:
                    k.op("dve", lambda e, dc=dc, st=st, scale=scale: e.tensor_single_scalar(
                        dst[:, dc * ncols:(dc + 1) * ncols], st[:, 0:ncols], scale, ALU.mult),
                        reads=[st], writes=[dst])
                else:
                    k.op("dve", lambda e, dc=dc, st=st: e.tensor_copy(
                        dst[:, dc * ncols:(dc + 1) * ncols], st[:, 0:ncols]),
                        reads=[st], writes=[dst])

        with contextlib.ExitStack() as sc:
            wst = [k.sb(f"dwst{i}", [128, D], F32, sc) for i in range(2)]
            load_w(wq, dins["L0_attn_Wq"], 512, scale=QSCALE)
            if L0CUT >= 2:
                load_w(wk, dins["L0_attn_Wk"], 512)
            if L0CUT >= 5:
                load_w(wv, dins["L0_attn_Wv"], D)
            if L0CUT >= 6:
                load_w(wg, dins["L0_attn_Wg"], D)
            if L0CUT >= 7:
                load_w(wo, dins["L0_attn_Wo"], D, nw=True)
            if L0CUT >= 4:
                load_w(wb, dins["L0_attn_Wb"], 4)
            stg = [k.sb(f"dstg{i}", [128, TTS], F32, sc) for i in range(2)]
            sq = [k.sb(f"dsq{i}", [128, TTS], F32, sc) for i in range(2)]
            rows = {n_: k.sb(f"drow{n_}", [1, TTS], F32, sc) for n_ in ("m", "v", "r")}
            pb_a = k.ps("dpba", [128, TTS], F32, sc)
            pb_b = k.ps("dpbb", [128, TTS], F32, sc)
            for tt in range(NTT):
                ts = slice(tt * TTS, (tt + 1) * TTS)
                for c in range(NCH):
                    st = stg[c % 2]
                    k.dma("sp", st[:, :], xin[c * 128:(c + 1) * 128, ts],
                          writes=[st], tag="d_ldx")
                    k.op("pe", lambda e, st=st, c=c: e.matmul(
                        pb_a[0:1, :], env.ones128[:, :], st[:, :],
                        start=(c == 0), stop=(c == NCH - 1)),
                        reads=[env.ones128, st], writes=[pb_a])
                    s2 = sq[c % 2]
                    k.op("dve", lambda e, st=st, s2=s2: e.tensor_mul(
                        s2[:, :], st[:, :], st[:, :]),
                        reads=[st], writes=[s2])
                    k.op("pe", lambda e, s2=s2, c=c: e.matmul(
                        pb_b[0:1, :], env.ones128[:, :], s2[:, :],
                        start=(c == 0), stop=(c == NCH - 1)),
                        reads=[env.ones128, s2], writes=[pb_b])
                emit_ln_rows(k, env, pb_a, pb_b, rows, D)
                k.op("pe", lambda e: e.matmul(pb_a[:, :], env.ones1r[:, :], rows["m"][0:1, :],
                                              start=True, stop=True),
                     reads=[env.ones1r, rows["m"]], writes=[pb_a])
                k.op("pe", lambda e: e.matmul(pb_b[:, :], env.ones1r[:, :], rows["r"][0:1, :],
                                              start=True, stop=True),
                     reads=[env.ones1r, rows["r"]], writes=[pb_b])
                for c in range(NCH):
                    st = stg[c % 2]
                    k.dma("sp", st[:, :], xin[c * 128:(c + 1) * 128, ts],
                          writes=[st], tag="d_ldx2")
                    scr = sq[c % 2]
                    k.op("dve", lambda e, st=st, scr=scr: e.tensor_tensor(
                        scr[:, :], st[:, :], pb_a[:, :], ALU.subtract),
                        reads=[st, pb_a], writes=[scr])
                    k.op("dve", lambda e, c=c, ts=ts, scr=scr: e.tensor_tensor(
                        xn[c][:, ts], scr[:, :], pb_b[:, :], ALU.mult),
                        reads=[scr, pb_b], writes=[xn[c]])
        k.barrier()

        with contextlib.ExitStack() as sc:
            rrow = k.sb("drr", [1, TTS], F32, sc)
            sqs = k.sb("dsqs", [128, TTS], F32, sc)
            pb_p = k.ps("dpbp", [128, TTS], F32, sc)
            pb_r = k.ps("dpbr", [128, TTS], F32, sc)
            pb_q = [k.ps(f"dpbq{i}", [128, TTS], F32, sc) for i in range(2)]
            for h in range(NH4):
                for tt in range(NTT):
                    ts = slice(tt * TTS, (tt + 1) * TTS)
                    pbq = pb_q[tt % 2]
                    for dc in range(NCH):
                        k.op("pe", lambda e, dc=dc, h=h, ts=ts, pbq=pbq: e.matmul(
                            pbq[:, :], wq[:, dc * 512 + h * 128:dc * 512 + (h + 1) * 128],
                            xn[dc][:, ts], start=(dc == 0), stop=(dc == NCH - 1)),
                            reads=[wq, xn[dc]], writes=[pbq])
                    k.op("dve", lambda e, h=h, tt=tt, pbq=pbq: e.tensor_copy(
                        qT[:, h * T + tt * TTS:h * T + (tt + 1) * TTS], pbq[:, :]),
                        reads=[pbq], writes=[qT])
                    if L0CUT < 2:
                        continue
                    pbk = pb_q[(tt + 1) % 2]
                    for dc in range(NCH):
                        k.op("pe", lambda e, dc=dc, h=h, ts=ts, pbk=pbk: e.matmul(
                            pbk[:, :], wk[:, dc * 512 + h * 128:dc * 512 + (h + 1) * 128],
                            xn[dc][:, ts], start=(dc == 0), stop=(dc == NCH - 1)),
                            reads=[wk, xn[dc]], writes=[pbk])
                    k.op("dve", lambda e, h=h, tt=tt, pbk=pbk: e.tensor_copy(
                        kT[:, h * T + tt * TTS:h * T + (tt + 1) * TTS], pbk[:, :]),
                        reads=[pbk], writes=[kT])
                    if L0CUT < 3:
                        continue
                    k.op("dve", lambda e, h=h, tt=tt: e.tensor_mul(
                        sqs[:, :], kT[:, h * T + tt * TTS:h * T + (tt + 1) * TTS], kT[:, h * T + tt * TTS:h * T + (tt + 1) * TTS]),
                        reads=[kT], writes=[sqs])
                    k.op("pe", lambda e: e.matmul(
                        pb_p[0:1, :], env.ones128[:, :], sqs[:, :], start=True, stop=True),
                        reads=[env.ones128, sqs], writes=[pb_p])
                    k.op("dve", lambda e: e.tensor_copy(rrow[0:1, :], pb_p[0:1, :]),
                         reads=[pb_p], writes=[rrow], strict=True)
                    k.op("act", lambda e: e.activation(rrow[0:1, :], rrow[0:1, :],
                                                       AF.Ln, bias=1e-12),
                         reads=[rrow], writes=[rrow], strict=True)
                    k.op("act", lambda e: e.activation(rrow[0:1, :], rrow[0:1, :],
                                                       AF.Exp, scale=-0.5),
                         reads=[rrow], writes=[rrow], strict=True)
                    k.op("pe", lambda e: e.matmul(pb_r[:, :], env.ones1r[:, :], rrow[0:1, :],
                                                  start=True, stop=True),
                         reads=[env.ones1r, rrow], writes=[pb_r])
                    k.op("dve", lambda e, h=h, tt=tt: e.tensor_tensor(
                        kT[:, h * T + tt * TTS:h * T + (tt + 1) * TTS],
                        kT[:, h * T + tt * TTS:h * T + (tt + 1) * TTS], pb_r[:, :], ALU.mult),
                        reads=[kT, pb_r], writes=[kT])
                    k.op("pe", lambda e, h=h, ts=ts: e.matmul(
                        pb_p[0:1, 0:TTS], wb[0:128, 0 * 4 + h:h + 1] if False else
                        wb[:, h:h + 1], xn[0][:, ts], start=True, stop=False),
                        reads=[wb, xn[0]], writes=[pb_p]) if False else None
            for h in range(NH4 if L0CUT >= 4 else 0):
                for tt in range(NTT):
                    ts = slice(tt * TTS, (tt + 1) * TTS)
                    for dc in range(NCH):
                        k.op("pe", lambda e, dc=dc, h=h, ts=ts: e.matmul(
                            pb_p[0:1, :], wb[:, dc * 4 + h:dc * 4 + h + 1], xn[dc][:, ts],
                            start=(dc == 0), stop=(dc == NCH - 1)),
                            reads=[wb, xn[dc]], writes=[pb_p])
                    k.op("act", lambda e, h=h, tt=tt: e.activation(
                        beta[h // 2][32 * (h % 2):32 * (h % 2) + 1,
                                     tt * TTS:(tt + 1) * TTS],
                        pb_p[0:1, :], AF.Sigmoid),
                        reads=[pb_p], writes=[beta[h // 2]])
        if dbg:
            k.dma("sp", dbg["wq"], wq[:, :], reads=[wq], tag="dbg")
            k.dma("sp", dbg["xn7"], xn[7][:, :], reads=[xn[7]], tag="dbg")
            k.dma("sp", dbg["qT"], qT[:, :], reads=[qT], tag="dbg")
            k.dma("sp", dbg["kT"], kT[:, :], reads=[kT], tag="dbg")
            k.dma("sp", dbg["xn"], xn[0][:, :], reads=[xn[0]], tag="dbg")
        k.barrier()

        with contextlib.ExitStack() as sc:
            S = [k.sb(f"dS{h}", [128, DV], F32, sc) for h in range(NH4)]
            for h in range(NH4):
                k.op("dve", lambda e, h=h: e.memset(S[h][:, :], 0.0), writes=[S[h]])
            S_bf = [k.sb(f"dSbf{h}", [128, DV], BF16, sc) for h in range(NH4)]
            v_bf = [k.sb(f"dvbf{h}", [128, DV], BF16, sc) for h in range(NH4)]
            u_scr = k.sb("duscr", [128, DV], F32, sc)
            u_bf = [k.sb(f"dubf{h}", [128, DV], BF16, sc) for h in range(NH4)]
            d_bf = [k.sb(f"ddbf{h}", [128, DV], BF16, sc) for h in range(NH4)]
            n_bf = [k.sb(f"dnbf{h}", [128, CC], BF16, sc) for h in range(NH4)]
            nt_bf = [k.sb(f"dntbf{h}", [128, CC], BF16, sc) for h in range(NH4)]
            gt_bf = [k.sb(f"dgtbf{h}", [128, CC], BF16, sc) for h in range(NH4)]
            ktok = [k.sb(f"dktok{h}", [128, CC], BF16, sc) for h in range(NH4)]
            bcol = k.sb("dbcol", [128, 4], F32, sc)
            EE = [[k.sb(f"dE{i}{h}", [128, CC], BF16, sc) for h in range(NH4)]
                  for i in range(2)]
            FF = [[k.sb(f"dF{i}{h}", [128, CC], BF16, sc) for h in range(NH4)]
                  for i in range(2)]
            XX = [[k.sb(f"dX{i}{h}", [128, CC], BF16, sc) for h in range(NH4)]
                  for i in range(2)]
            YY = [[k.sb(f"dY{i}{h}", [128, CC], BF16, sc) for h in range(NH4)]
                  for i in range(2)]
            rms = k.sb("drms", [128, 4], F32, sc)
            o_f32 = [k.sb(f"dof32{h}", [128, DV], F32, sc) for h in range(NH4)]
            on_scr = k.sb("don", [128, DV], F32, sc)
            gsilu = k.sb("dgsilu", [128, (T // 128) * D], BF16, sc)
            otok = k.sb("dotok", [128, D], BF16, sc)
            otT = k.sb("dotT", [128, D], BF16, sc)
            stg = [k.sb(f"estg{i}", [128, CC], F32, sc) for i in range(2)]
            res = [o_f32[0], o_f32[1]]
            pb1 = k.ps("epb1", [128, TTS], F32, sc)
            pb2 = k.ps("epb2", [128, TTS], F32, sc)
            pb3 = k.ps("epb3", [128, TTS], F32, sc)
            pb4 = k.ps("epb4", [128, TTS], F32, sc)
            pb5 = k.ps("epb5", [128, TTS], F32, sc)
            pbt = k.ps("epbt", [128, 1024], BF16, sc)

            sg_scr = u_scr
            for ti in range(T // 128):
                tcs = slice(ti * 128, (ti + 1) * 128)
                for h in range(NH4):
                    pbg = pb4 if (h % 2 == 0) else pb2
                    off = 0 if (h % 2 == 0) else DV
                    for dc in range(NCH):
                        k.op("pe", lambda e, dc=dc, h=h, tcs=tcs, pbg=pbg, off=off: e.matmul(
                            pbg[:, off:off + DV], xn[dc][:, tcs],
                            wg[:, dc * D + h * DV:dc * D + (h + 1) * DV],
                            start=(dc == 0), stop=(dc == NCH - 1)),
                            reads=[xn[dc], wg], writes=[pbg])
                    k.op("act", lambda e, pbg=pbg, off=off: e.activation(
                        sg_scr[:, :], pbg[:, off:off + DV], AF.Sigmoid),
                        reads=[pbg], writes=[sg_scr])
                    k.op("dve", lambda e, ti=ti, h=h, pbg=pbg, off=off: e.tensor_tensor(
                        gsilu[:, ti * D + h * DV:ti * D + (h + 1) * DV],
                        sg_scr[:, :], pbg[:, off:off + DV], ALU.mult),
                        reads=[sg_scr, pbg], writes=[gsilu], strict=True)
            for c in range(min(NCK, NCK_RUN) if L0CUT >= 5 else 0):
                cs = slice(c * CC, (c + 1) * CC)
                ks = [kT[:, h * T + c * CC:h * T + (c + 1) * CC] for h in range(NH4)]
                qs = [qT[:, h * T + c * CC:h * T + (c + 1) * CC] for h in range(NH4)]
                pbV = lambda h: (pb2 if h < 2 else pb3)[:, (h % 2) * DV:(h % 2 + 1) * DV]
                pbVt = lambda h: pb2 if h < 2 else pb3
                for h in range(NH4):
                    k.op("dve", lambda e, h=h: e.tensor_copy(S_bf[h][:, :], S[h][:, :]),
                         reads=[S[h]], writes=[S_bf[h]], tag=f"s0cp_c{c}_h{h}")
                for h in range(NH4 if STAGE_CUT >= 1 else 0):
                    for dc in range(NCH):
                        k.op("pe", lambda e, dc=dc, h=h, cs=cs: e.matmul(
                            pbV(h), xn[dc][:, cs],
                            wv[:, dc * D + h * DV:dc * D + (h + 1) * DV],
                            start=(dc == 0), stop=(dc == NCH - 1)),
                            reads=[xn[dc], wv], writes=[pbVt(h)])
                for h in range(NH4 if STAGE_CUT >= 1 else 0):
                    k.op("act", lambda e, h=h, pbV=pbV: e.activation(
                        v_bf[h][:, :], pbV(h), AF.Copy),
                        reads=[pbVt(h)], writes=[v_bf[h]])
                for h in range(NH4 if STAGE_CUT >= 2 else 0):
                    k.op("pe", lambda e, h=h, ks=ks: e.matmul(
                        pb1[:, h * CC:(h + 1) * CC], ks[h], ks[h], start=True, stop=True),
                        reads=[kT], writes=[pb1])
                for h in range(NH4 if STAGE_CUT >= 3 else 0):
                    r0_ = 32 * (h % 2)
                    k.op("dve", lambda e, h=h, c=c, r0_=r0_: e.tensor_copy(
                        otT[0:1, h * CC:(h + 1) * CC],
                        beta[h // 2][r0_:r0_ + 1, c * CC:(c + 1) * CC]),
                        reads=[beta[h // 2]], writes=[otT], strict=True)
                for h in range(NH4 if STAGE_CUT >= 3 else 0):
                    k.op("pe", lambda e, h=h: e.matmul(
                        pb4[:, h * CC:(h + 1) * CC], env.ones1rb[:, :],
                        otT[0:1, h * CC:(h + 1) * CC],
                        start=True, stop=True),
                        reads=[env.ones1rb, otT], writes=[pb4], tag=f"s3bc_c{c}_h{h}")
                for h in range(NH4 if STAGE_CUT >= 4 else 0):
                    k.op("dve", lambda e, h=h: e.tensor_tensor(
                        u_scr[:, 0:CC], pb1[:, h * CC:(h + 1) * CC], umask_s[:, :], ALU.mult),
                        reads=[pb1, umask_s], writes=[u_scr], strict=True)
                    k.op("dve", lambda e, h=h: e.tensor_tensor(
                        n_bf[h][:, :], u_scr[:, 0:CC], pb4[:, h * CC:(h + 1) * CC], ALU.mult),
                        reads=[u_scr, pb4], writes=[n_bf[h]], strict=True)
                for h in range(NH4 if STAGE_CUT >= 5 else 0):
                    k.op("pe", lambda e, h=h: e.transpose(
                        pbt[:, 896 + 2 * h:897 + 2 * h],
                        otT[0:1, h * CC:(h + 1) * CC], env.eyeb[0:1, 0:1]),
                        reads=[otT, env.eyeb], writes=[pbt])
                for h in range(NH4 if STAGE_CUT >= 5 else 0):
                    k.op("dve", lambda e, h=h: e.tensor_copy(
                        bcol[:, h:h + 1], pbt[:, 896 + 2 * h:897 + 2 * h]),
                        reads=[pbt], writes=[bcol], strict=True)
                for h in range(NH4 if STAGE_CUT >= 6 else 0):
                    k.op("pe", lambda e, h=h: e.transpose(
                        pbt[:, h * 128:(h + 1) * 128], n_bf[h][:, :], env.eyeb[:, :]),
                        reads=[n_bf[h], env.eyeb], writes=[pbt])
                for h in range(NH4 if STAGE_CUT >= 6 else 0):
                    k.op("act", lambda e, h=h: e.activation(
                        nt_bf[h][:, :], pbt[:, h * 128:(h + 1) * 128], AF.Copy),
                        reads=[pbt], writes=[nt_bf[h]], strict=True)
                for h in range(NH4 if STAGE_CUT >= 7 else 0):
                    k.op("dve", lambda e, h=h: e.tensor_tensor(
                        XX[0][h][:, :], env.eyeb[:, :], n_bf[h][:, :], ALU.subtract),
                        reads=[env.eyeb, n_bf[h]], writes=[XX[0][h]], strict=True)
                    k.op("dve", lambda e, h=h: e.tensor_tensor(
                        YY[0][h][:, :], env.eyeb[:, :], nt_bf[h][:, :], ALU.subtract),
                        reads=[env.eyeb, nt_bf[h]], writes=[YY[0][h]], strict=True)
                Ep, Fp = n_bf, nt_bf
                Xp, Yp = XX[0], YY[0]
                for it in range(6 if STAGE_CUT >= 8 else 0):
                    En = EE[it % 2]
                    Fn = FF[it % 2]
                    Xn_ = XX[(it + 1) % 2]
                    Yn_ = YY[(it + 1) % 2]
                    for h in range(NH4):
                        k.op("pe", lambda e, h=h, Fp=Fp, Ep=Ep: e.matmul(
                            pb5[:, h * CC:(h + 1) * CC], Fp[h][:, :], Ep[h][:, :],
                            start=True, stop=True),
                            reads=[Fp[h], Ep[h]], writes=[pb5], tag=f"s8nm_c{c}_it{it}_h{h}")
                    for h in range(NH4):
                        k.op("act", lambda e, h=h, En=En: e.activation(
                            En[h][:, :], pb5[:, h * CC:(h + 1) * CC], AF.Copy),
                            reads=[pb5], writes=[En[h]], strict=True)
                    if it < 5:
                        for h in range(NH4):
                            k.op("pe", lambda e, h=h, Ep=Ep, Fp=Fp: e.matmul(
                                pb5[:, h * CC:(h + 1) * CC], Ep[h][:, :], Fp[h][:, :],
                                start=True, stop=True),
                                reads=[Ep[h], Fp[h]], writes=[pb5])
                        for h in range(NH4):
                            k.op("act", lambda e, h=h, Fn=Fn: e.activation(
                                Fn[h][:, :], pb5[:, h * CC:(h + 1) * CC], AF.Copy),
                                reads=[pb5], writes=[Fn[h]], strict=True)
                    for h in range(NH4):
                        k.op("pe", lambda e, h=h, Yp=Yp, En=En: e.matmul(
                            pb5[:, h * CC:(h + 1) * CC], Yp[h][:, :], En[h][:, :],
                            start=True, stop=True),
                            reads=[Yp[h], En[h]], writes=[pb5])
                    for h in range(NH4):
                        k.op("dve", lambda e, h=h, Xp=Xp, Xn_=Xn_: e.tensor_tensor(
                            Xn_[h][:, :], Xp[h][:, :], pb5[:, h * CC:(h + 1) * CC], ALU.add),
                            reads=[Xp[h], pb5], writes=[Xn_[h]], strict=True)
                    if it < 5:
                        for h in range(NH4):
                            k.op("pe", lambda e, h=h, En=En, Yp=Yp: e.matmul(
                                pb5[:, h * CC:(h + 1) * CC], En[h][:, :], Yp[h][:, :],
                                start=True, stop=True),
                                reads=[En[h], Yp[h]], writes=[pb5])
                        for h in range(NH4):
                            k.op("dve", lambda e, h=h, Yp=Yp, Yn_=Yn_: e.tensor_tensor(
                                Yn_[h][:, :], Yp[h][:, :], pb5[:, h * CC:(h + 1) * CC],
                                ALU.add),
                                reads=[Yp[h], pb5], writes=[Yn_[h]], strict=True)
                    Ep, Fp = En, Fn
                    Xp, Yp = Xn_, Yn_
                Xfin = Xp
                for h in range(NH4 if STAGE_CUT >= 9 else 0):
                    k.op("pe", lambda e, h=h, ks=ks, pbV=pbV: e.matmul(
                        pbV(h), ks[h], S_bf[h][:, :], start=True, stop=True),
                        reads=[kT, S_bf[h]], writes=[pbVt(h)], tag=f"s9p0_c{c}_h{h}")
                for h in range(NH4 if STAGE_CUT >= 10 else 0):
                    k.op("dve", lambda e, h=h: e.tensor_tensor(
                        u_scr[:, :], v_bf[h][:, :], pbV(h), ALU.subtract),
                        reads=[v_bf[h], pbVt(h)], writes=[u_scr], strict=True)
                    k.op("dve", lambda e, h=h: e.tensor_scalar(
                        u_bf[h][:, :], u_scr[:, :], bcol[:, h:h + 1], None, ALU.mult),
                        reads=[u_scr, bcol], writes=[u_bf[h]], strict=True)
                for h in range(NH4 if STAGE_CUT >= 11 else 0):
                    k.op("pe", lambda e, h=h, Xfin=Xfin: e.matmul(
                        pbV(h), Xfin[h][:, :], u_bf[h][:, :], start=True, stop=True),
                        reads=[Xfin[h], u_bf[h]], writes=[pbVt(h)])
                for h in range(NH4 if STAGE_CUT >= 11 else 0):
                    k.op("act", lambda e, h=h, pbV=pbV: e.activation(
                        d_bf[h][:, :], pbV(h), AF.Copy),
                        reads=[pbVt(h)], writes=[d_bf[h]], strict=True)
                for h in range(NH4 if STAGE_CUT >= 12 else 0):
                    k.op("pe", lambda e, h=h, ks=ks, qs=qs: e.matmul(
                        pb1[:, h * CC:(h + 1) * CC], ks[h], qs[h], start=True, stop=True),
                        reads=[kT, qT], writes=[pb1])
                for h in range(NH4 if STAGE_CUT >= 12 else 0):
                    k.op("dve", lambda e, h=h: e.tensor_tensor(
                        gt_bf[h][:, :], pb1[:, h * CC:(h + 1) * CC], umask_i[:, :], ALU.mult),
                        reads=[pb1, umask_i], writes=[gt_bf[h]], strict=True)
                for h in range(NH4 if STAGE_CUT >= 13 else 0):
                    k.op("pe", lambda e, h=h, ks=ks: e.transpose(
                        pbt[:, 512 + h * 128:512 + (h + 1) * 128], ks[h], env.eyeb[:, :]),
                        reads=[kT, env.eyeb], writes=[pbt])
                for h in range(NH4 if STAGE_CUT >= 13 else 0):
                    k.op("act", lambda e, h=h: e.activation(
                        ktok[h][:, :], pbt[:, 512 + h * 128:512 + (h + 1) * 128], AF.Copy),
                        reads=[pbt], writes=[ktok[h]], strict=True)
                for h in range(NH4 if STAGE_CUT >= 14 else 0):
                    k.op("pe", lambda e, h=h, qs=qs, pbV=pbV: e.matmul(
                        pbV(h), qs[h], S_bf[h][:, :], start=True, stop=False),
                        reads=[qT, S_bf[h]], writes=[pbVt(h)], tag=f"s14o_c{c}_h{h}")
                    k.op("pe", lambda e, h=h: e.matmul(
                        pbV(h), gt_bf[h][:, :], d_bf[h][:, :], start=False, stop=True),
                        reads=[gt_bf[h], d_bf[h]], writes=[pbVt(h)])
                for h in range(NH4 if STAGE_CUT >= 15 else 0):
                    k.op("pe", lambda e, h=h: e.matmul(
                        pb5[:, (h % 2) * DV:(h % 2 + 1) * DV] if False else
                        pb5[:, h * CC:(h + 1) * CC] if False else
                        (pb5 if h < 2 else pb4)[:, (h % 2) * DV:(h % 2 + 1) * DV],
                        ktok[h][:, :], d_bf[h][:, :], start=True, stop=True),
                        reads=[ktok[h], d_bf[h]], writes=[pb5 if h < 2 else pb4])
                for h in range(NH4 if STAGE_CUT >= 15 else 0):
                    k.op("dve", lambda e, h=h: e.tensor_tensor(
                        S[h][:, :], S[h][:, :],
                        (pb5 if h < 2 else pb4)[:, (h % 2) * DV:(h % 2 + 1) * DV], ALU.add),
                        reads=[S[h], pb5 if h < 2 else pb4], writes=[S[h]], strict=True,
                        tag=f"s15add_c{c}_h{h}")
                for h in range(NH4 if STAGE_CUT >= 16 else 0):
                    k.op("dve", lambda e, h=h: e.tensor_copy(o_f32[h][:, :], pbV(h)),
                         reads=[pbVt(h)], writes=[o_f32[h]], strict=True)
                    k.op("dve", lambda e, h=h: e.tensor_mul(
                        u_scr[:, :], o_f32[h][:, :], o_f32[h][:, :]),
                        reads=[o_f32[h]], writes=[u_scr], strict=True)
                    k.op("dve", lambda e, h=h: e.reduce_sum(
                        rms[:, h:h + 1], u_scr[:, :], axis=AX.X),
                        reads=[u_scr], writes=[rms], strict=True)
                if STAGE_CUT >= 16:
                    k.op("act", lambda e: e.activation(rms[:, 0:4], rms[:, 0:4], AF.Ln,
                                                       scale=1.0 / 256.0, bias=1e-5),
                         reads=[rms], writes=[rms], strict=True)
                    k.op("act", lambda e: e.activation(rms[:, 0:4], rms[:, 0:4], AF.Exp,
                                                       scale=-0.5),
                         reads=[rms], writes=[rms], strict=True)
                for h in range(NH4 if STAGE_CUT >= 16 else 0):
                    k.op("dve", lambda e, h=h: e.tensor_scalar(
                        on_scr[:, :], o_f32[h][:, :], rms[:, h:h + 1], None, ALU.mult),
                        reads=[o_f32[h], rms], writes=[on_scr], strict=True)
                    k.op("dve", lambda e, h=h, c=c: e.tensor_tensor(
                        otok[:, h * DV:(h + 1) * DV], on_scr[:, :],
                        gsilu[:, c * D + h * DV:c * D + (h + 1) * DV], ALU.mult),
                        reads=[on_scr, gsilu], writes=[otok], strict=True)
                if dbg and c == 0:
                    k.dma("sp", dbg["S0"], S[0][:, :], reads=[S[0]], tag="dbg")
                    k.dma("sp", dbg["otok"], otok[:, :], reads=[otok], tag="dbg")
                    k.dma("sp", dbg["X6"], Xfin[3][:, :], reads=[Xfin[3]], tag="dbg")
                    k.dma("sp", dbg["dbf"], d_bf[3][:, :], reads=[d_bf[3]], tag="dbg")
                    k.dma("sp", dbg["ubf"], u_bf[3][:, :], reads=[u_bf[3]], tag="dbg")
                    k.dma("sp", dbg["nbf"], n_bf[3][:, :], reads=[n_bf[3]], tag="dbg")
                    k.dma("sp", dbg["bcol"], bcol[:, 0:4], reads=[bcol], tag="dbg")
                for fc in range(NCH if STAGE_CUT >= 17 else 0):
                    k.op("pe", lambda e, fc=fc: e.transpose(
                        pbt[:, 384 + fc * 128:384 + (fc + 1) * 128] if False else
                        pbt[:, fc * 128:(fc + 1) * 128],
                        otok[:, fc * 128:(fc + 1) * 128], env.eyeb[:, :]),
                        reads=[otok, env.eyeb], writes=[pbt])
                if STAGE_CUT >= 17:
                    k.op("dve", lambda e: e.tensor_copy(otT[:, :], pbt[:, 0:D]),
                         reads=[pbt], writes=[otT], strict=True)
                for oc in range(NCH if STAGE_CUT >= 17 else 0):
                    for fc in range(NCH):
                        k.op("pe", lambda e, fc=fc, oc=oc: e.matmul(
                            pb5[:, 0:CC], wo[:, fc * D + oc * 128:fc * D + (oc + 1) * 128],
                            otT[:, fc * 128:(fc + 1) * 128],
                            start=(fc == 0), stop=(fc == NCH - 1)),
                            reads=[wo, otT], writes=[pb5])
                    st = stg[0] if False else None
                    st = res[oc % 2]
                    ld = stg[oc % 2] if False else None
                    pass
                    ldt = kimported = None
                    del st, ld, ldt, kimported
                    stl = res[oc % 2]
                    k.dma("sp", stl[:, 0:CC], xin[oc * 128:(oc + 1) * 128, cs],
                          writes=[stl], tag="d_ldx3")
                    k.op("dve", lambda e, stl=stl: e.tensor_tensor(
                        stl[:, 0:CC], stl[:, 0:CC], pb5[:, 0:CC], ALU.add),
                        reads=[stl, pb5], writes=[stl], strict=True)
                    k.dma("sp", xout[oc * 128:(oc + 1) * 128, cs], stl[:, 0:CC],
                          reads=[stl], tag="d_stx")
                if CHUNK_BAR:
                    k.barrier()
    k.barrier()


def host_prep_l0_attn(inputs):
    d = {}
    for nm in ("Wq", "Wk", "Wv", "Wb", "Wg", "Wo"):
        d[f"L0_attn_{nm}"] = np.ascontiguousarray(inputs[f"L0_attn_{nm}"])
    nw = inputs["L0_attn_norm_w"].reshape(2, 128).T
    d["L0_attn_nwcol"] = np.ascontiguousarray(nw).astype(np.float32)
    return d


_L0ATTN_SPECS = {
    "L0_attn_Wq": ([D, 512], F32),
    "L0_attn_Wk": ([D, 512], F32),
    "L0_attn_Wv": ([D, D], F32),
    "L0_attn_Wb": ([D, 4], F32),
    "L0_attn_Wg": ([D, D], F32),
    "L0_attn_Wo": ([D, D], F32),
    "L0_attn_nwcol": ([128, 2], F32),
}


def build_l0_attn_program(debug=False):
    nc = bass.Bass("TRN2", target_bir_lowering=False)
    stack = contextlib.ExitStack()
    dins = {}
    specs = {**_CONST_SPECS, "c_eyef": ([128, 128], F32),
             "c_ones1rb": ([1, 128], BF16),
             "c_umask_s": ([128, 128], BF16), "c_umask_i": ([128, 128], BF16),
             "c_ones65": ([65, 128], BF16),
             **_L0ATTN_SPECS}
    for name, (shape, dt) in specs.items():
        dins[name] = nc.dram_tensor(name, shape, dt, kind="ExternalInput").ap()
    xin = nc.dram_tensor("xin", [D, T], F32, kind="ExternalInput").ap()
    xout = nc.dram_tensor("xout", [D, T], F32, kind="ExternalOutput").ap()
    dbg = None
    if debug:
        dbg = {
            "qT": nc.dram_tensor("dbg_qT", [128, NH4 * T], BF16, kind="ExternalOutput").ap(),
            "kT": nc.dram_tensor("dbg_kT", [128, NH4 * T], BF16, kind="ExternalOutput").ap(),
            "beta": nc.dram_tensor("dbg_beta", [1, NH4 * T], BF16, kind="ExternalOutput").ap(),
            "xn": nc.dram_tensor("dbg_xn", [128, T], BF16, kind="ExternalOutput").ap(),
            "S0": nc.dram_tensor("dbg_S0", [128, DV], F32, kind="ExternalOutput").ap(),
            "otok": nc.dram_tensor("dbg_otok", [128, D], BF16, kind="ExternalOutput").ap(),
            "X6": nc.dram_tensor("dbg_X6", [128, CC], BF16, kind="ExternalOutput").ap(),
            "dbf": nc.dram_tensor("dbg_dbf", [128, DV], BF16, kind="ExternalOutput").ap(),
            "ubf": nc.dram_tensor("dbg_ubf", [128, DV], BF16, kind="ExternalOutput").ap(),
            "nbf": nc.dram_tensor("dbg_nbf", [128, CC], BF16, kind="ExternalOutput").ap(),
            "bcol": nc.dram_tensor("dbg_bcol", [128, 4], F32, kind="ExternalOutput").ap(),
            "wq": nc.dram_tensor("dbg_wq", [128, NCH * 512], BF16, kind="ExternalOutput").ap(),
            "xn7": nc.dram_tensor("dbg_xn7", [128, T], BF16, kind="ExternalOutput").ap(),
        }
    with stack:
        k = Ker(nc, stack)
        env = Env(k, dins)
        k.barrier()
        emit_l0_attn(k, env, dins, xin, xout, dbg)
        global _LAST_K
        _LAST_K = k
        k.lower(verbose=True)
    return nc




def build_fused_program():
    nc = bass.Bass("TRN2", target_bir_lowering=False)
    stack = contextlib.ExitStack()
    dins = {}
    specs = {**_CONST_SPECS, "c_eyef": ([128, 128], F32),
             "c_onesrowb": ([1, TTS], BF16), "c_ones1rb": ([1, 128], BF16),
             "c_umask_s": ([128, 128], BF16), "c_umask_i": ([128, 128], BF16),
             "c_ones65": ([65, 128], BF16),
             **_L0ATTN_SPECS, **_L0FFN_SPECS, **_L1ATTN_SPECS, **_L1FFN_SPECS}
    for name, (shape, dt) in specs.items():
        dins[name] = nc.dram_tensor(name, shape, dt, kind="ExternalInput").ap()
    xin = nc.dram_tensor("xin", [D, T], F32, kind="ExternalInput").ap()
    xout = nc.dram_tensor("xout", [D, T], F32, kind="ExternalOutput").ap()
    s1 = nc.dram_tensor("s1", [D, T], F32).ap()
    s2 = nc.dram_tensor("s2", [D, T], F32).ap()
    s3 = nc.dram_tensor("s3", [D, T], F32).ap()
    w1b_scr = nc.dram_tensor("w1b_scr", [3, D, HID], BF16).ap()
    w2b_scr = nc.dram_tensor("w2b_scr", [3, HID, D], BF16).ap()
    rows_scr = nc.dram_tensor("rows_scr", [2, T], F32).ap()
    wgq_scr = nc.dram_tensor("wgq_scr", [D, OFF], BF16).ap()
    wuq_scr = nc.dram_tensor("wuq_scr", [D, OFF], BF16).ap()
    with stack:
        k = Ker(nc, stack)
        env = Env(k, dins)
        k.barrier()
        emit_l0_attn(k, env, dins, xin, s1)
        emit_l0_ffn(k, env, dins, s1, s2)
        emit_l1_attn(k, env, dins, s2, s3, w1b_scr, w2b_scr, rows_scr)
        emit_l1_ffn(k, env, dins, s3, xout, wgq_scr, wuq_scr)
        k.lower(verbose=True)
    return nc


_PROGRAMS = {}


def kernel(**inputs):
    if "fused" not in _PROGRAMS:
        _PROGRAMS["fused"] = build_fused_program()
    consts = host_consts()
    prep = {}
    prep.update(host_prep_l0_attn(inputs))
    prep.update(host_prep_l0_ffn(inputs))
    prep.update(host_prep_l1_attn(inputs))
    prep.update(host_prep_l1_ffn(inputs))
    base = {**consts, **prep}

    x = inputs["L0_attn_x"]
    n_cores = 8
    in_maps = [{**base, "xin": np.ascontiguousarray(x[i].T).astype(np.float32)}
               for i in range(n_cores)]
    res = run_bass_kernel_spmd(_PROGRAMS["fused"], in_maps, core_ids=list(range(n_cores)))
    out = np.stack([np.asarray(res.results[i]["xout"]).T for i in range(n_cores)], axis=0)
    return out.astype(np.float32)


# version 09
# speedup vs baseline: 1.2353x; 1.0009x over previous
import contextlib
import numpy as np
import ml_dtypes
import concourse.bass as bass
import concourse.mybir as mybir
from concourse.bass_utils import run_bass_kernel_spmd

F32 = mybir.dt.float32
BF16 = mybir.dt.bfloat16
I32 = mybir.dt.int32
AF = mybir.ActivationFunctionType
ALU = mybir.AluOpType
AX = mybir.AxisListType

T = 2048
D = 1024
NCH = D // 128
TTS = 512
NTT = T // TTS

DMA_RR = True
SCOPE_TAG = False
ENGS = ("sp", "pe", "act", "dve", "gp")
_BASS_ENG = {"sp": "sync", "pe": "tensor", "act": "scalar", "dve": "vector", "gp": "gpsimd"}


class Tile:
    __slots__ = ("h", "lw", "readers", "name")

    def __init__(self, h, name=""):
        self.h = h
        self.name = name
        self.lw = None
        self.readers = []

    def __getitem__(self, sl):
        return self.h[sl]


class Op:
    __slots__ = ("eng", "fn", "gidx", "deps", "need_inc", "inc_ord", "is_dma",
                 "phase", "tag", "strict")

    def __init__(self, eng, fn, is_dma, gidx, tag=""):
        self.strict = False
        self.eng = eng
        self.fn = fn
        self.is_dma = is_dma
        self.gidx = gidx
        self.deps = []
        self.need_inc = False
        self.inc_ord = None
        self.phase = None
        self.tag = tag


class _Barrier:
    pass


class Ker:
    def __init__(self, nc, stack: contextlib.ExitStack):
        self.nc = nc
        self.stack = stack
        self.ops = []
        self.dbg_waits = {}
        self.n_barriers = 0
        self._uid = 0
        self._dma_rr = 0

    def sb(self, name, shape, dt, scope=None):
        h = (scope or self.stack).enter_context(
            self.nc.sbuf_tensor(self._u(name), list(shape), dt))
        return Tile(h, name)

    def ps(self, name, shape, dt, scope=None):
        h = (scope or self.stack).enter_context(
            self.nc.psum_tensor(self._u(name), list(shape), dt))
        return Tile(h, name)

    def _u(self, name):
        self._uid += 1
        return f"{name}_{self._uid}"

    def op(self, eng, fn, reads=(), writes=(), tag="", strict=False):
        o = Op(eng, fn, False, len(self.ops), tag)
        o.strict = strict
        self._track(o, reads, writes)
        self.ops.append(o)
        return o

    def dma(self, eng, out_ap, in_ap, reads=(), writes=(), transpose=False, tag=""):
        if eng == "sp" and DMA_RR:
            eng = ("sp", "act")[self._dma_rr % 2]
            self._dma_rr += 1
        if transpose:
            fn = lambda e, o=out_ap, i=in_ap: e.dma_start_transpose(o, i)
        else:
            fn = lambda e, o=out_ap, i=in_ap: e.dma_start(o, i)
        o = Op(eng, fn, True, len(self.ops), tag)
        self._track(o, reads, writes)
        self.ops.append(o)
        return o

    def _track(self, o, reads, writes):
        deps = {}
        for t in reads:
            if t.lw is not None:
                deps[t.lw.gidx] = t.lw
        for t in writes:
            if t.lw is not None:
                deps[t.lw.gidx] = t.lw
            for r in t.readers:
                deps[r.gidx] = r
        for d in deps.values():
            if d is o:
                continue
            if d.eng == o.eng and not d.is_dma and not d.strict:
                continue
            o.deps.append(d)
        for t in reads:
            t.readers.append(o)
        for t in writes:
            t.lw = o
            t.readers = []

    def barrier(self):
        self.ops.append(_Barrier())
        self.n_barriers += 1

    def lower(self, verbose=False):
        nc = self.nc
        phase = 0
        for o in self.ops:
            if isinstance(o, _Barrier):
                phase += 1
            else:
                o.phase = phase
        n_phases = phase + 1
        for o in self.ops:
            if isinstance(o, _Barrier):
                continue
            for d in o.deps:
                if d.phase == o.phase and not d.is_dma:
                    d.need_inc = True
        last_op = {}
        for o in self.ops:
            if isinstance(o, _Barrier):
                continue
            if not o.is_dma:
                last_op[(o.phase, o.eng)] = o
        for o in last_op.values():
            o.need_inc = True
        R = 6
        inc_cnt = {e: 0 for e in ENGS}
        dma_cnt = {e: 0 for e in ENGS}
        for o in self.ops:
            if isinstance(o, _Barrier):
                continue
            if o.is_dma:
                o.inc_ord = dma_cnt[o.eng]
                dma_cnt[o.eng] += 1
            elif o.need_inc:
                inc_cnt[o.eng] += 1
                o.inc_ord = inc_cnt[o.eng]
        esem = {e: self.stack.enter_context(nc.semaphore(f"e_{e}"))
                for e in ENGS if inc_cnt[e] > 0}
        dsem = {e: [self.stack.enter_context(nc.semaphore(f"d_{e}{i}"))
                    for i in range(min(R, dma_cnt[e]))]
                for e in ENGS if dma_cnt[e] > 0}
        bsem = [self.stack.enter_context(nc.semaphore(f"bar{i}"))
                for i in range(self.n_barriers)]
        streams = {e: [] for e in ENGS}
        hw = {}
        run_inc = {e: 0 for e in ENGS}
        run_dma = {e: 0 for e in ENGS}
        bar_idx = 0
        for o in self.ops:
            if isinstance(o, _Barrier):
                bi = bar_idx
                bar_idx += 1
                for e in ENGS:
                    finals = []
                    nd = run_dma[e]
                    if nd > 0:
                        ds = dsem[e]
                        for i_ in range(len(ds)):
                            cnt = (nd - i_ + len(ds) - 1) // len(ds)
                            if cnt > 0:
                                finals.append((ds[i_], 16 * cnt))
                    ni = run_inc[e]
                    es = esem.get(e)
                    bs = bsem[bi]

                    def bfn(eng, finals=finals, bs=bs, ni=ni, es=es):
                        for sem_, val_ in finals:
                            eng.wait_ge(sem_, val_)
                        if ni > 0:
                            eng.wait_ge(es, ni)
                        eng.nop().then_inc(bs, 1)
                        eng.wait_ge(bs, len(ENGS))
                    streams[e].append(bfn)
                continue
            waits = []
            for d in o.deps:
                if d.phase < o.phase:
                    continue
                if d.is_dma:
                    ds = dsem[d.eng]
                    si = d.inc_ord % len(ds)
                    key = (o.eng, d.eng, True, si)
                    val = 16 * (d.inc_ord // len(ds) + 1)
                    sem = ds[si]
                else:
                    key = (o.eng, d.eng, False)
                    val = d.inc_ord
                    sem = esem[d.eng]
                if hw.get(key, 0) < val:
                    hw[key] = val
                    waits.append((sem, val))
            pre_waits = []
            if o.is_dma:
                ds = dsem[o.eng]
                si = o.inc_ord % len(ds)
                cyc = o.inc_ord // len(ds)
                if cyc > 0:
                    pre_waits.append((ds[si], 16 * cyc))
                inc = (ds[si], 16)
                run_dma[o.eng] += 1
            elif o.need_inc:
                inc = (esem[o.eng], 1)
                run_inc[o.eng] = o.inc_ord
            else:
                inc = None

            self.dbg_waits[o.gidx] = (list(waits), o.need_inc, o.inc_ord)

            def ofn(eng, waits=waits, pre_waits=pre_waits, fn=o.fn, inc=inc):
                for sem, val in waits:
                    eng.wait_ge(sem, val)
                for sem, val in pre_waits:
                    eng.wait_ge(sem, val)
                ins = fn(eng)
                if inc is not None:
                    ins.then_inc(inc[0], inc[1])
            ofn.ph = o.phase
            streams[o.eng].append(ofn)
        for e in ENGS:
            nd = run_dma[e]
            if nd > 0:
                ds = dsem[e]
                finals = []
                for i_ in range(len(ds)):
                    cnt = (nd - i_ + len(ds) - 1) // len(ds)
                    if cnt > 0:
                        finals.append((ds[i_], 16 * cnt))

                def ffn(eng, finals=finals):
                    for sem_, val_ in finals:
                        eng.wait_ge(sem_, val_)
                streams[e].append(ffn)
        if verbose:
            tot = {e: sum(1 for o in self.ops
                          if not isinstance(o, _Barrier) and o.eng == e) for e in ENGS}
            print(f"[mk] ops per engine: {tot}, phases={n_phases}, "
                  f"incs={inc_cnt}, dmas={dma_cnt}")
        with nc.Block() as block:
            for e in ENGS:
                lst = streams[e]

                def body(eng, lst=lst):
                    for f in lst:
                        ph = getattr(f, "ph", None)
                        if SCOPE_TAG and ph is not None:
                            with nc.named_scope(f"ph{ph:02d}"):
                                f(eng)
                        else:
                            f(eng)
                getattr(block, _BASS_ENG[e])(body)



CONST_VALS = [1e-5, -0.5, 0.125, 1e-12, -1.0, 2.0, 6.0, 0.5, -0.125, 1.0/256.0]


class Env:

    def __init__(self, k: Ker, dins, scope=None):
        self.k = k
        self.ones128 = k.sb("ones128", [128, 1], F32, scope)
        self.ones1r = k.sb("ones1r", [1, 128], F32, scope)
        self.onesrow = k.sb("onesrow", [1, TTS], F32, scope)
        self.eyeb = k.sb("eyeb", [128, 128], BF16, scope)
        self.fvals = k.sb("fvals", [128, len(CONST_VALS)], F32, scope)
        self.ones1rb = k.sb("ones1rb", [1, 128], BF16, scope)
        k.dma("sp", self.ones128[:, :], dins["c_ones128"], writes=[self.ones128])
        k.dma("sp", self.ones1r[:, :], dins["c_ones1r"], writes=[self.ones1r])
        k.dma("sp", self.onesrow[:, :], dins["c_onesrow"], writes=[self.onesrow])
        k.dma("sp", self.eyeb[:, :], dins["c_eyeb"], writes=[self.eyeb])
        k.dma("sp", self.fvals[:, :], dins["c_fvals"], writes=[self.fvals])
        if "c_ones1rb" in dins:
            k.dma("sp", self.ones1rb[:, :], dins["c_ones1rb"], writes=[self.ones1rb])
        for i, v in enumerate(CONST_VALS):
            k.nc.const_aps.register(F32, float(v), self.fvals.h[:, i:i + 1])


def emit_ln_rows(k, env, pb_a, pb_b, rows, n, eps=1e-5, tag="ln"):
    m, v, r = rows["m"], rows["v"], rows["r"]
    k.op("dve", lambda e: e.tensor_single_scalar(m[0:1, :], pb_a[0:1, :], 1.0 / n, ALU.mult),
         reads=[pb_a], writes=[m], tag=tag + "_m", strict=True)
    k.op("dve", lambda e: e.tensor_single_scalar(v[0:1, :], pb_b[0:1, :], 1.0 / n, ALU.mult),
         reads=[pb_b], writes=[v], tag=tag + "_v", strict=True)
    k.op("dve", lambda e: e.tensor_mul(r[0:1, :], m[0:1, :], m[0:1, :]),
         reads=[m], writes=[r], strict=True)
    k.op("dve", lambda e: e.tensor_tensor(v[0:1, :], v[0:1, :], r[0:1, :], ALU.subtract),
         reads=[v, r], writes=[v], strict=True)
    k.op("act", lambda e: e.activation(r[0:1, :], v[0:1, :], AF.Ln, bias=float(eps)),
         reads=[v], writes=[r], strict=True)
    k.op("act", lambda e: e.activation(r[0:1, :], r[0:1, :], AF.Exp, scale=-0.5),
         reads=[r], writes=[r], strict=True)



def emit_l0_ffn(k: Ker, env, dins, xin, xout):
    with contextlib.ExitStack() as sc:
        H = [k.sb(f"H{c}", [128, T], F32, sc) for c in range(NCH)]
        hb = [k.sb(f"hb{c}", [128, T], BF16, sc) for c in range(NCH)]
        stg = [k.sb(f"stg{i}", [128, TTS], F32, sc) for i in range(2)]
        xsq_scr = [k.sb(f"sqscr{i}", [128, TTS], F32, sc) for i in range(2)]
        ap_scr = xsq_scr
        rows = {n_: k.sb(f"row{n_}", [1, TTS], F32, sc) for n_ in ("m", "v", "r")}
        wT = [k.sb(f"wT{w}", [128, NCH * D], BF16, sc) for w in range(2)]
        wstage = [k.sb(f"wst{i}", [128, D], I32, sc) for i in range(2)]
        wf_scr = k.sb("wf", [128, D], F32, sc)
        scT = k.sb("scT", [64, D], F32, sc)
        rsel = k.sb("rsel", [64, D], F32, sc)
        la_bf = [k.sb(f"labf{w}", [128, 256], BF16, sc) for w in range(2)]
        lb_bf = [k.sb(f"lbbf{w}", [32, D], BF16, sc) for w in range(2)]
        brow = k.sb("brow", [1, 2 * D], F32, sc)
        gam = k.sb("gam", [128, NCH], F32, sc)
        bet = k.sb("bet", [128, NCH], F32, sc)
        y1f = [k.sb(f"y1f{c}", [128, TTS], F32, sc) for c in range(NCH)]
        xg = [k.sb(f"xg{c}", [128, TTS], BF16, sc) for c in range(NCH)]
        u_bf = k.sb("ubf", [32, TTS], BF16, sc)
        pb_a = k.ps("pba", [128, TTS], F32, sc)
        pb_b = k.ps("pbb", [128, TTS], F32, sc)
        pb_y = [k.ps(f"pby{i}", [128, TTS], F32, sc) for i in range(2)]
        pb_u = k.ps("pbu", [32, TTS], F32, sc)
        pb_s = k.ps("pbs", [128, D], F32, sc)

        k.dma("sp", rsel[:, :], dins["c_rsel"], writes=[rsel])

        def ln_stats(src_ap_fn, read_tiles):
            for c in range(NCH):
                k.op("pe", lambda e, c=c: e.matmul(
                    pb_a[0:1, :], env.ones128[:, :], src_ap_fn(c),
                    start=(c == 0), stop=(c == NCH - 1)),
                    reads=[env.ones128, read_tiles[c]], writes=[pb_a])
            for c in range(NCH):
                scr = xsq_scr[c % 2]
                k.op("dve", lambda e, c=c, scr=scr: e.tensor_mul(
                    scr[:, :], src_ap_fn(c), src_ap_fn(c)),
                    reads=[read_tiles[c]], writes=[scr])
                k.op("pe", lambda e, scr=scr, c=c: e.matmul(
                    pb_b[0:1, :], env.ones128[:, :], scr[:, :],
                    start=(c == 0), stop=(c == NCH - 1)),
                    reads=[env.ones128, scr], writes=[pb_b])
            emit_ln_rows(k, env, pb_a, pb_b, rows, D)
            k.op("pe", lambda e: e.matmul(pb_a[:, :], env.ones1r[:, :],
                                          rows["m"][0:1, :], start=True, stop=True),
                 reads=[env.ones1r, rows["m"]], writes=[pb_a])
            k.op("pe", lambda e: e.matmul(pb_b[:, :], env.ones1r[:, :],
                                          rows["r"][0:1, :], start=True, stop=True),
                 reads=[env.ones1r, rows["r"]], writes=[pb_b])

        def preln_tt(tt):
            ts = slice(tt * TTS, (tt + 1) * TTS)
            for c in range(NCH):
                st = stg[c % 2]
                k.dma("sp", st[:, :], xin[c * 128:(c + 1) * 128, ts],
                      writes=[st], tag="ldx")
                k.op("pe", lambda e, c=c, st=st: e.matmul(
                    pb_a[0:1, :], env.ones128[:, :], st[:, :],
                    start=(c == 0), stop=(c == NCH - 1)),
                    reads=[env.ones128, st], writes=[pb_a])
                scr = xsq_scr[c % 2]
                k.op("dve", lambda e, st=st, scr=scr: e.tensor_mul(
                    scr[:, :], st[:, :], st[:, :]),
                    reads=[st], writes=[scr])
                k.op("pe", lambda e, scr=scr, c=c: e.matmul(
                    pb_b[0:1, :], env.ones128[:, :], scr[:, :],
                    start=(c == 0), stop=(c == NCH - 1)),
                    reads=[env.ones128, scr], writes=[pb_b])
            emit_ln_rows(k, env, pb_a, pb_b, rows, D)
            k.op("pe", lambda e: e.matmul(pb_a[:, :], env.ones1r[:, :],
                                          rows["m"][0:1, :], start=True, stop=True),
                 reads=[env.ones1r, rows["m"]], writes=[pb_a])
            k.op("pe", lambda e: e.matmul(pb_b[:, :], env.ones1r[:, :],
                                          rows["r"][0:1, :], start=True, stop=True),
                 reads=[env.ones1r, rows["r"]], writes=[pb_b])
            for c in range(NCH):
                st = stg[c % 2]
                k.dma("sp", st[:, :], xin[c * 128:(c + 1) * 128, ts],
                      writes=[st], tag="ldx2")
                scr = ap_scr[c % 2]
                k.op("dve", lambda e, st=st, scr=scr: e.tensor_tensor(
                    scr[:, :], st[:, :], pb_a[:, :], ALU.subtract),
                    reads=[st, pb_a], writes=[scr])
                k.op("dve", lambda e, c=c, ts=ts, scr=scr: e.tensor_tensor(
                    H[c][:, ts], scr[:, :], pb_b[:, :], ALU.mult),
                    reads=[scr, pb_b], writes=[H[c]])
                k.op("dve", lambda e, c=c, ts=ts: e.tensor_copy(
                    hb[c][:, ts], H[c][:, ts]),
                    reads=[H[c]], writes=[hb[c]])

        for tt in range(NTT):
            preln_tt(tt)

        def prep_weights(j, w):
            k.dma("sp", scT[:, :], dins["L0_ffn_scT"][j], writes=[scT], tag="ldsc")
            for dc in range(NCH):
                st = wstage[dc % 2]
                k.dma("sp", st[:, :], dins["L0_ffn_qwT"][j, dc * 128:(dc + 1) * 128, :],
                      writes=[st], tag="ldw")
                for half in range(2):
                    k.op("pe", lambda e, dc=dc, half=half: e.matmul(
                        pb_s[:, half * TTS:(half + 1) * TTS],
                        rsel[:, dc * 128:(dc + 1) * 128],
                        scT[:, half * TTS:(half + 1) * TTS],
                        start=True, stop=True),
                        reads=[rsel, scT], writes=[pb_s])
                k.op("dve", lambda e, st=st: e.tensor_scalar(
                    wf_scr[:, :], st[:, :], 1.0 / 7.5, -1.0, ALU.mult, ALU.add),
                    reads=[st], writes=[wf_scr])
                k.op("dve", lambda e, dc=dc, w=w: e.tensor_tensor(
                    wT[w][:, dc * D:(dc + 1) * D], wf_scr[:, :], pb_s[:, :], ALU.mult),
                    reads=[wf_scr, pb_s], writes=[wT[w]])

        def prep_lora_bias(j, w):
            k.dma("sp", wf_scr[:, 0:256].rearrange("p (c r) -> p c r", r=32),
                  dins["L0_ffn_laT"][j].rearrange("(c p) r -> p c r", p=128),
                  writes=[wf_scr], tag="ldla")
            k.op("dve", lambda e, w=w: e.tensor_copy(la_bf[w][:, :], wf_scr[:, 0:256]),
                 reads=[wf_scr], writes=[la_bf[w]])
            k.dma("sp", wf_scr[0:32, :], dins["L0_ffn_lbT"][j],
                  writes=[wf_scr], tag="ldlb")
            k.op("dve", lambda e, w=w: e.tensor_copy(
                lb_bf[w][0:32, :], wf_scr[0:32, :]),
                reads=[wf_scr], writes=[lb_bf[w]])
            k.dma("sp", brow[0:1, w * D:(w + 1) * D], dins["L0_ffn_biases"][j:j + 1, :],
                  writes=[brow], tag="ldbias")

        def qlora_mm(src_ap_fn, src_tiles, w, consumer):
            for dc in range(NCH):
                k.op("pe", lambda e, dc=dc, w=w: e.matmul(
                    pb_u[0:32, :], la_bf[w][:, dc * 32:(dc + 1) * 32], src_ap_fn(dc),
                    start=(dc == 0), stop=(dc == NCH - 1)),
                    reads=[la_bf[w], src_tiles[dc]], writes=[pb_u])
            k.op("dve", lambda e: e.tensor_copy(u_bf[0:32, :], pb_u[0:32, :]),
                 reads=[pb_u], writes=[u_bf])
            for oc in range(NCH):
                pb = pb_y[oc % 2]
                for dc in range(NCH):
                    k.op("pe", lambda e, dc=dc, oc=oc, pb=pb, w=w: e.matmul(
                        pb[:, :], wT[w][:, dc * D + oc * 128: dc * D + (oc + 1) * 128],
                        src_ap_fn(dc), start=(dc == 0), stop=False),
                        reads=[wT[w], src_tiles[dc]], writes=[pb])
                k.op("pe", lambda e, oc=oc, pb=pb, w=w: e.matmul(
                    pb[:, :], lb_bf[w][0:32, oc * 128:(oc + 1) * 128],
                    u_bf[0:32, :], start=False, stop=False),
                    reads=[lb_bf[w], u_bf], writes=[pb])
                k.op("pe", lambda e, oc=oc, pb=pb, w=w: e.matmul(
                    pb[:, :], brow[0:1, w * D + oc * 128:w * D + (oc + 1) * 128],
                    env.onesrow[0:1, :], start=False, stop=True),
                    reads=[brow, env.onesrow], writes=[pb])
                consumer(oc, pb)

        for i in range(3):
            prep_weights(2 * i, 0)
            prep_lora_bias(2 * i, 0)
            prep_weights(2 * i + 1, 1)
            prep_lora_bias(2 * i + 1, 1)
            k.dma("sp", gam[:, :], dins["L0_ffn_gam"][i], writes=[gam], tag="ldgam")
            k.dma("sp", bet[:, :], dins["L0_ffn_bet"][i], writes=[bet], tag="ldbet")
            for tt in range(NTT):
                ts = slice(tt * TTS, (tt + 1) * TTS)

                def eat_y1(oc, pb):
                    k.op("dve", lambda e, oc=oc, pb=pb: e.tensor_copy(
                        y1f[oc][:, :], pb[:, :]), reads=[pb], writes=[y1f[oc]])
                qlora_mm(lambda dc, ts=ts: hb[dc][:, ts], hb, 0, eat_y1)
                ln_stats(lambda c: y1f[c][:, :], y1f)
                for c in range(NCH):
                    scr = ap_scr[c % 2]
                    k.op("dve", lambda e, c=c, scr=scr: e.tensor_tensor(
                        scr[:, :], y1f[c][:, :], pb_a[:, :], ALU.subtract),
                        reads=[y1f[c], pb_a], writes=[scr])
                    k.op("dve", lambda e, scr=scr: e.tensor_tensor(
                        scr[:, :], scr[:, :], pb_b[:, :], ALU.mult),
                        reads=[scr, pb_b], writes=[scr])
                    k.op("dve", lambda e, c=c, scr=scr: e.tensor_scalar(
                        scr[:, :], scr[:, :], gam[:, c:c + 1], bet[:, c:c + 1],
                        ALU.mult, ALU.add),
                        reads=[scr, gam, bet], writes=[scr])
                    k.op("act", lambda e, c=c, scr=scr: e.activation(
                        xg[c][:, :], scr[:, :], AF.Gelu),
                        reads=[scr], writes=[xg[c]])

                def eat_y2(oc, pb, ts=ts):
                    k.op("dve", lambda e, oc=oc, pb=pb, ts=ts: e.tensor_tensor(
                        H[oc][:, ts], H[oc][:, ts], pb[:, :], ALU.add),
                        reads=[H[oc], pb], writes=[H[oc]])
                    k.op("dve", lambda e, oc=oc, ts=ts: e.tensor_copy(
                        hb[oc][:, ts], H[oc][:, ts]),
                        reads=[H[oc]], writes=[hb[oc]])
                qlora_mm(lambda dc: xg[dc][:, :], xg, 1, eat_y2)

        for tt in range(NTT):
            ts = slice(tt * TTS, (tt + 1) * TTS)
            for c in range(NCH):
                st = stg[c % 2]
                k.dma("sp", st[:, :], xin[c * 128:(c + 1) * 128, ts],
                      writes=[st], tag="ldx3")
                scr = ap_scr[c % 2]
                k.op("dve", lambda e, c=c, ts=ts, scr=scr, st=st: e.tensor_tensor(
                    scr[:, :], st[:, :], H[c][:, ts], ALU.add),
                    reads=[st, H[c]], writes=[scr])
                k.dma("sp", xout[c * 128:(c + 1) * 128, ts], scr[:, :],
                      reads=[scr], tag="stx")
    k.barrier()



def host_consts():
    c = {}
    c["c_ones128"] = np.ones((128, 1), np.float32)
    c["c_ones1r"] = np.ones((1, 128), np.float32)
    c["c_onesrow"] = np.ones((1, TTS), np.float32)
    c["c_eyeb"] = np.eye(128, dtype=ml_dtypes.bfloat16)
    rsel = np.zeros((64, D), np.float32)
    for p in range(D):
        rsel[(p % 128) // 16 + 0, p] = 0.0
    rsel[:] = 0.0
    for dc in range(8):
        for p in range(128):
            rsel[dc * 8 + p // 16, dc * 128 + p] = 1.0
    c["c_rsel"] = rsel
    c["c_fvals"] = np.tile(np.array(CONST_VALS, np.float32)[None, :], (128, 1))
    c["c_onesrowb"] = np.ones((1, TTS), ml_dtypes.bfloat16)
    c["c_ones1rb"] = np.ones((1, 128), ml_dtypes.bfloat16)
    c["c_eyef"] = np.eye(128, dtype=np.float32)
    c["c_ones65"] = np.ones((65, 128), ml_dtypes.bfloat16)
    c["c_umask_s"] = np.triu(np.ones((128, 128)), 1).astype(ml_dtypes.bfloat16)
    c["c_umask_i"] = np.triu(np.ones((128, 128)), 0).astype(ml_dtypes.bfloat16)
    return c


def host_prep_l0_ffn(inputs):
    d = {}
    d["L0_ffn_qwT"] = np.ascontiguousarray(
        inputs["L0_ffn_qweights"].transpose(0, 2, 1)).astype(np.int32)
    d["L0_ffn_scT"] = np.ascontiguousarray(
        inputs["L0_ffn_scales"].transpose(0, 2, 1)).astype(np.float32)
    d["L0_ffn_laT"] = np.ascontiguousarray(
        inputs["L0_ffn_lora_a"].transpose(0, 2, 1)).astype(np.float32)
    d["L0_ffn_lbT"] = np.ascontiguousarray(
        inputs["L0_ffn_lora_b"].transpose(0, 2, 1)).astype(np.float32)
    d["L0_ffn_biases"] = np.ascontiguousarray(inputs["L0_ffn_biases"])
    d["L0_ffn_gam"] = np.ascontiguousarray(
        inputs["L0_ffn_ln_gamma"].reshape(3, 8, 128).transpose(0, 2, 1))
    d["L0_ffn_bet"] = np.ascontiguousarray(
        inputs["L0_ffn_ln_beta"].reshape(3, 8, 128).transpose(0, 2, 1))
    return d


_L0FFN_SPECS = {
    "L0_ffn_qwT": ([6, D, D], I32),
    "L0_ffn_scT": ([6, 64, D], F32),
    "L0_ffn_laT": ([6, D, 32], F32),
    "L0_ffn_lbT": ([6, 32, D], F32),
    "L0_ffn_biases": ([6, D], F32),
    "L0_ffn_gam": ([3, 128, 8], F32),
    "L0_ffn_bet": ([3, 128, 8], F32),
}

_CONST_SPECS = {
    "c_ones128": ([128, 1], F32),
    "c_ones1r": ([1, 128], F32),
    "c_onesrow": ([1, TTS], F32),
    "c_eyeb": ([128, 128], BF16),
    "c_rsel": ([64, D], F32),
    "c_fvals": ([128, len(CONST_VALS)], F32),
}


def build_l0_ffn_program():
    nc = bass.Bass("TRN2", target_bir_lowering=False)
    stack = contextlib.ExitStack()
    dins = {}
    for name, (shape, dt) in {**_CONST_SPECS, **_L0FFN_SPECS}.items():
        dins[name] = nc.dram_tensor(name, shape, dt, kind="ExternalInput").ap()
    xin = nc.dram_tensor("xin", [D, T], F32, kind="ExternalInput").ap()
    xout = nc.dram_tensor("xout", [D, T], F32, kind="ExternalOutput").ap()
    with stack:
        k = Ker(nc, stack)
        env = Env(k, dins)
        k.barrier()
        emit_l0_ffn(k, env, dins, xin, xout)
        k.lower(verbose=True)
    return nc

def emit_fp4_qdq(k: Ker, env, src, out_bf, scr, N, src_sl=None, out_sl=None, eng2="dve", boff=0):
    nb = N // 16
    sa = src[:, 0:N] if src_sl is None else src_sl
    oa = out_bf[:, 0:N] if out_sl is None else out_sl
    bm, s, rcp = scr["bm"], scr["s"], scr["rcp"]
    absx, acc1, acc2, sb16 = scr["absx"], scr["acc1"], scr["acc2"], scr["sb16"]
    b0, b1 = boff, boff + nb
    k.op("dve", lambda e: e.reduce_max(
        bm[:, b0:b1].unsqueeze(2), sa.rearrange("p (a b) -> p a b", b=16),
        axis=AX.X, apply_absolute_value=True),
        reads=[src], writes=[bm], tag="q_bm", strict=True)
    k.op("dve", lambda e: e.tensor_scalar(
        s[:, b0:b1], bm[:, b0:b1], 1.0 / 6.0, 1e-12, ALU.mult, ALU.max),
        reads=[bm], writes=[s], tag="q_s", strict=True)
    for i0 in range(b0, b1, 128):
        i1 = min(b1, i0 + 128)
        k.op("dve", lambda e, i0=i0, i1=i1: e.reciprocal(rcp[:, i0:i1], s[:, i0:i1]),
             reads=[s], writes=[rcp], tag="q_rcp", strict=True)
    k.op("act", lambda e: e.activation(absx[:, 0:N], sa, AF.Abs),
         reads=[src], writes=[absx], tag="q_abs", strict=True)
    rep = lambda t: t[:, b0:b1].unsqueeze(2).broadcast_to([128, nb, 16])
    r3 = lambda ap: ap.rearrange("p (a b) -> p a b", b=16)
    k.op(eng2, lambda e: e.tensor_tensor(
        r3(absx[:, 0:N]), r3(absx[:, 0:N]), rep(rcp), ALU.mult),
        reads=[absx, rcp], writes=[absx], tag="q_div", strict=True)
    a = absx
    k.op("dve", lambda e: e.tensor_single_scalar(acc1[:, 0:N], a[:, 0:N], 0.25, ALU.is_gt),
         reads=[a], writes=[acc1], strict=True)
    for thr in (0.75, 1.25, 1.75):
        k.op("dve", lambda e, thr=thr: e.scalar_tensor_tensor(
            acc1[:, 0:N], a[:, 0:N], thr, acc1[:, 0:N], ALU.is_gt, ALU.add),
            reads=[a, acc1], writes=[acc1], strict=True)
    k.op("dve", lambda e: e.tensor_scalar(acc2[:, 0:N], a[:, 0:N], 5.0, 2.0, ALU.is_gt, ALU.mult),
         reads=[a], writes=[acc2], strict=True)
    for thr in (2.5, 3.5):
        k.op("dve", lambda e, thr=thr: e.scalar_tensor_tensor(
            acc2[:, 0:N], a[:, 0:N], thr, acc2[:, 0:N], ALU.is_gt, ALU.add),
            reads=[a, acc2], writes=[acc2], strict=True)
    k.op("dve", lambda e: e.scalar_tensor_tensor(
        acc1[:, 0:N], acc1[:, 0:N], 0.5, acc2[:, 0:N], ALU.mult, ALU.add),
        reads=[acc1, acc2], writes=[acc1], tag="q_comb", strict=True)
    k.op(eng2, lambda e: e.tensor_tensor(
        r3(oa), r3(acc1[:, 0:N]), rep(s), ALU.mult),
        reads=[acc1, s], writes=[out_bf], tag="q_deq", strict=True)
    I16 = mybir.dt.int16
    k.op("dve", lambda e: e.tensor_single_scalar(
        sb16[:, 0:N], sa.bitcast(I16)[:, 1:2 * N:2], -32768, ALU.bitwise_and),
        reads=[src], writes=[sb16], tag="q_sgn", strict=True)
    k.op("dve", lambda e: e.tensor_tensor(
        oa.bitcast(I16), oa.bitcast(I16), sb16[:, 0:N], ALU.bitwise_or),
        reads=[out_bf, sb16], writes=[out_bf], tag="q_or", strict=True)



OFF = 2816
NOC = OFF // 128
NOT = 6


def emit_l1_ffn(k: Ker, env, dins, xin, xout, wgq_scr, wuq_scr, dbg=None):
    with contextlib.ExitStack() as sc:
        stg = [k.sb(f"fstg{i}", [128, TTS], F32, sc) for i in range(2)]
        xsq_scr = [k.sb(f"fsq{i}", [128, TTS], F32, sc) for i in range(2)]
        rows = {n_: k.sb(f"frow{n_}", [1, TTS], F32, sc) for n_ in ("m", "v", "r")}
        mrcol = k.sb("mrcol", [128, 32], F32, sc)
        eyef = k.sb("eyef", [128, 128], F32, sc)
        k.dma("sp", eyef[:, :], dins["c_eyef"], writes=[eyef])
        scr = {
            "bm": k.sb("qbm", [128, 176], F32, sc),
            "s": k.sb("qs", [128, 176], F32, sc),
            "rcp": k.sb("qrcp", [128, 176], F32, sc),
            "absx": k.sb("qabs", [128, D], F32, sc),
            "acc1": k.sb("qa1", [128, D], F32, sc),
            "acc2": k.sb("qa2", [128, D], F32, sc),
            "sb16": k.sb("qsb", [128, D], mybir.dt.int16, sc),
        }
        wstage = [k.sb("fwst0", [128, OFF], F32, sc), k.sb("fwst1", [128, D], F32, sc)]
        wq_nat = k.sb("wqnat", [128, OFF], BF16, sc)
        trst = k.sb("trst", [128, 1024], BF16, sc)
        wdqT = k.sb("wdqT", [128, NOC * 1024], BF16, sc)
        xqT = k.sb("xqT", [128, NCH * TTS], BF16, sc)
        hqT = k.sb("hqT", [128, NOC * TTS], BF16, sc)
        wgt = [k.sb(f"wgt{i}", [128, NCH * TTS], BF16, sc) for i in range(2)]
        wut = [k.sb(f"wut{i}", [128, NCH * TTS], BF16, sc) for i in range(2)]
        x_tok = k.sb("x_tok", [128, D], F32, sc)
        xq_tok = k.sb("xq_tok", [128, D], BF16, sc)
        h_f32 = k.sb("h_f32", [128, TTS], F32, sc)
        sg_f32 = k.sb("sg_f32", [128, TTS], F32, sc)
        hq_bf = k.sb("hq_bf", [128, TTS], BF16, sc)
        pb_a = k.ps("fpba", [128, TTS], F32, sc)
        pb_b = k.ps("fpbb", [128, TTS], F32, sc)
        pb_g = k.ps("fpbg", [128, TTS], F32, sc)
        pb_u2 = k.ps("fpbu", [128, TTS], F32, sc)
        pb_t2 = k.ps("fpbt2", [128, 1024], F32, sc)
        pb_tr = k.ps("fpbtr", [128, 1024], BF16, sc)
        pb_y = [pb_g, pb_u2]

        for tt in range(NTT):
            ts = slice(tt * TTS, (tt + 1) * TTS)
            for c in range(NCH):
                st = stg[c % 2]
                k.dma("sp", st[:, :], xin[c * 128:(c + 1) * 128, ts],
                      writes=[st], tag="f_ldx")
                k.op("pe", lambda e, c=c, st=st: e.matmul(
                    pb_a[0:1, :], env.ones128[:, :], st[:, :],
                    start=(c == 0), stop=(c == NCH - 1)),
                    reads=[env.ones128, st], writes=[pb_a])
                sq = xsq_scr[c % 2]
                k.op("dve", lambda e, st=st, sq=sq: e.tensor_mul(
                    sq[:, :], st[:, :], st[:, :]),
                    reads=[st], writes=[sq])
                k.op("pe", lambda e, sq=sq, c=c: e.matmul(
                    pb_b[0:1, :], env.ones128[:, :], sq[:, :],
                    start=(c == 0), stop=(c == NCH - 1)),
                    reads=[env.ones128, sq], writes=[pb_b])
            emit_ln_rows(k, env, pb_a, pb_b, rows, D)
            for i in range(4):
                ci = tt * 4 + i
                k.op("pe", lambda e, i=i, ci=ci: e.transpose(
                    pb_t2[:, ci:ci + 1], rows["m"][0:1, i * 128:(i + 1) * 128], eyef[0:1, 0:1]),
                    reads=[rows["m"], eyef], writes=[pb_t2])
                k.op("pe", lambda e, i=i, ci=ci: e.transpose(
                    pb_t2[:, 16 + ci:16 + ci + 1], rows["r"][0:1, i * 128:(i + 1) * 128],
                    eyef[0:1, 0:1]),
                    reads=[rows["r"], eyef], writes=[pb_t2])
        k.op("dve", lambda e: e.tensor_copy(mrcol[:, 0:32], pb_t2[:, 0:32]),
             reads=[pb_t2], writes=[mrcol])

        def quant_gateup(dsrc, dscr):
            for ocn in range(NOC):
                st = wstage[ocn % 2]
                k.dma("sp", st[:, 0:D], dsrc[ocn * 128:(ocn + 1) * 128, :],
                      writes=[st], tag="f_ldw")
                emit_fp4_qdq(k, env, st, wq_nat, scr, D,
                             src_sl=st[:, 0:D], out_sl=wq_nat[:, 0:D], eng2="gp")
                for c in range(NCH):
                    k.op("pe", lambda e, c=c: e.transpose(
                        pb_tr[:, c * 128:(c + 1) * 128], wq_nat[:, c * 128:(c + 1) * 128],
                        env.eyeb[:, :]),
                        reads=[wq_nat, env.eyeb], writes=[pb_tr])
                k.op("act", lambda e: e.copy(trst[:, :], pb_tr[:, 0:1024]),
                     reads=[pb_tr], writes=[trst])
                k.dma("sp",
                      dscr.rearrange("(c p) o -> p c o", p=128)[:, :, ocn * 128:(ocn + 1) * 128],
                      trst[:, :].rearrange("p (c o) -> p c o", o=128),
                      reads=[trst], tag="f_stw")

        quant_gateup(dins["L1_ffn_w_gate"], wgq_scr)
        quant_gateup(dins["L1_ffn_w_up"], wuq_scr)

        for oc in range(NCH):
            st = wstage[0]
            k.dma("sp", st[:, :], dins["L1_ffn_w_down"][oc * 128:(oc + 1) * 128, :],
                  writes=[st], tag="f_ldwd")
            for o0 in range(0, OFF, D):
                n_ = min(D, OFF - o0)
                emit_fp4_qdq(k, env, st, wq_nat, scr, n_,
                             src_sl=st[:, o0:o0 + n_], out_sl=wq_nat[:, o0:o0 + n_],
                             eng2="gp")
            for b0 in range(0, NOC, 8):
                b1 = min(NOC, b0 + 8)
                for kc in range(b0, b1):
                    k.op("pe", lambda e, kc=kc, b0=b0: e.transpose(
                        pb_tr[:, (kc - b0) * 128:(kc - b0 + 1) * 128],
                        wq_nat[:, kc * 128:(kc + 1) * 128], env.eyeb[:, :]),
                        reads=[wq_nat, env.eyeb], writes=[pb_tr])
                n = b1 - b0
                k.op("act", lambda e, b0=b0, n=n, oc=oc: e.copy(
                    wdqT[:, :].rearrange("p (kc o) -> p kc o", o=1024)[
                        :, b0:b0 + n, oc * 128:(oc + 1) * 128],
                    pb_tr[:, 0:n * 128].rearrange("p (kc o) -> p kc o", o=128)),
                    reads=[pb_tr], writes=[wdqT])

        if dbg and "wdqT" in dbg:
            k.dma("sp", dbg["wdqT"], wdqT[:, :], reads=[wdqT], tag="dbg")
        for tt in range(NTT):
            ts = slice(tt * TTS, (tt + 1) * TTS)
            for tsub in range(4):
                t0 = tt * TTS + tsub * 128
                for c in range(NCH):
                    st = stg[c % 2]
                    k.dma("sp", st[:, 0:128], xin[c * 128:(c + 1) * 128, t0:t0 + 128],
                          writes=[st], tag="f_ldx2")
                    k.op("pe", lambda e, c=c, st=st: e.transpose(
                        pb_t2[:, c * 128:(c + 1) * 128], st[:, 0:128], eyef[:, :]),
                        reads=[st, eyef], writes=[pb_t2])
                k.op("act", lambda e: e.copy(x_tok[:, :], pb_t2[:, 0:D]),
                     reads=[pb_t2], writes=[x_tok])
                ci = tt * 4 + tsub
                k.op("dve", lambda e, ci=ci: e.tensor_scalar(
                    x_tok[:, :], x_tok[:, :], mrcol[:, ci:ci + 1], mrcol[:, 16 + ci:16 + ci + 1],
                    ALU.subtract, ALU.mult),
                    reads=[x_tok, mrcol], writes=[x_tok])
                emit_fp4_qdq(k, env, x_tok, xq_tok, scr, D, eng2="gp")
                for c in range(NCH):
                    k.op("pe", lambda e, c=c: e.transpose(
                        pb_tr[:, c * 128:(c + 1) * 128], xq_tok[:, c * 128:(c + 1) * 128],
                        env.eyeb[:, :]),
                        reads=[xq_tok, env.eyeb], writes=[pb_tr])
                k.op("act", lambda e, tsub=tsub: e.copy(
                    xqT[:, :].rearrange("p (c t) -> p c t", t=TTS)[:, :, tsub * 128:(tsub + 1) * 128],
                    pb_tr[:, 0:1024].rearrange("p (c t) -> p c t", t=128)),
                    reads=[pb_tr], writes=[xqT])
            if dbg and tt == 0 and "xqT" in dbg:
                k.dma("sp", dbg["xqT"], xqT[:, :], reads=[xqT], tag="dbg")
            for ot in range(NOT):
                osz = 512 if ot < 5 else 256
                kcs = list(range(ot * 4, ot * 4 + osz // 128))
                wg = wgt[ot % 2]
                wu = wut[ot % 2]
                k.dma("sp",
                      wg[:, 0:NCH * osz].rearrange("p (c o) -> p c o", o=osz),
                      wgq_scr.rearrange("(c p) o -> p c o", p=128)[:, :, ot * 512:ot * 512 + osz],
                      writes=[wg], tag="f_ldgw")
                k.dma("sp",
                      wu[:, 0:NCH * osz].rearrange("p (c o) -> p c o", o=osz),
                      wuq_scr.rearrange("(c p) o -> p c o", p=128)[:, :, ot * 512:ot * 512 + osz],
                      writes=[wu], tag="f_lduw")
                for tsub in range(4):
                    tso = tsub * 128
                    for c in range(NCH):
                        k.op("pe", lambda e, c=c, tso=tso, wg=wg, osz=osz: e.matmul(
                            pb_g[:, 0:osz], xqT[:, c * TTS + tso:c * TTS + tso + 128],
                            wg[:, c * osz:(c + 1) * osz],
                            start=(c == 0), stop=(c == NCH - 1)),
                            reads=[xqT, wg], writes=[pb_g])
                    for c in range(NCH):
                        k.op("pe", lambda e, c=c, tso=tso, wu=wu, osz=osz: e.matmul(
                            pb_u2[:, 0:osz], xqT[:, c * TTS + tso:c * TTS + tso + 128],
                            wu[:, c * osz:(c + 1) * osz],
                            start=(c == 0), stop=(c == NCH - 1)),
                            reads=[xqT, wu], writes=[pb_u2])
                    k.op("act", lambda e, osz=osz: e.activation(
                        sg_f32[:, 0:osz], pb_g[:, 0:osz], AF.Sigmoid),
                        reads=[pb_g], writes=[sg_f32])
                    k.op("dve", lambda e, osz=osz: e.tensor_tensor(
                        sg_f32[:, 0:osz], sg_f32[:, 0:osz], pb_g[:, 0:osz], ALU.mult),
                        reads=[sg_f32, pb_g], writes=[sg_f32])
                    k.op("dve", lambda e, osz=osz: e.tensor_tensor(
                        h_f32[:, 0:osz], sg_f32[:, 0:osz], pb_u2[:, 0:osz], ALU.mult),
                        reads=[sg_f32, pb_u2], writes=[h_f32])
                    emit_fp4_qdq(k, env, h_f32, hq_bf, scr, osz,
                                 src_sl=h_f32[:, 0:osz], out_sl=hq_bf[:, 0:osz], eng2="gp")
                    for i, kc in enumerate(kcs):
                        k.op("pe", lambda e, i=i: e.transpose(
                            pb_tr[:, i * 128:(i + 1) * 128], hq_bf[:, i * 128:(i + 1) * 128],
                            env.eyeb[:, :]),
                            reads=[hq_bf, env.eyeb], writes=[pb_tr])
                    nk = len(kcs)
                    k.op("act", lambda e, nk=nk, kcs=kcs, tso=tso: e.copy(
                        hqT[:, :].rearrange("p (kc t) -> p kc t", t=TTS)[
                            :, kcs[0]:kcs[0] + nk, tso:tso + 128],
                        pb_tr[:, 0:nk * 128].rearrange("p (kc t) -> p kc t", t=128)),
                        reads=[pb_tr], writes=[hqT])
            if dbg and tt == 0 and "hqT" in dbg:
                k.dma("sp", dbg["hqT"], hqT[:, :], reads=[hqT], tag="dbg")
            for oc in range(NCH):
                pb = pb_y[oc % 2]
                for kc in range(NOC):
                    k.op("pe", lambda e, kc=kc, oc=oc, pb=pb: e.matmul(
                        pb[:, :], wdqT[:, kc * 1024 + oc * 128:kc * 1024 + (oc + 1) * 128],
                        hqT[:, kc * TTS:(kc + 1) * TTS],
                        start=(kc == 0), stop=(kc == NOC - 1)),
                        reads=[wdqT, hqT], writes=[pb])
                st = stg[oc % 2]
                k.dma("sp", st[:, :], xin[oc * 128:(oc + 1) * 128, ts],
                      writes=[st], tag="f_ldx3")
                sq = xsq_scr[oc % 2]
                k.op("dve", lambda e, pb=pb, st=st, sq=sq: e.tensor_tensor(
                    sq[:, :], st[:, :], pb[:, :], ALU.add),
                    reads=[st, pb], writes=[sq])
                k.dma("sp", xout[oc * 128:(oc + 1) * 128, ts], sq[:, :],
                      reads=[sq], tag="f_stx")
    k.barrier()


def host_prep_l1_ffn(inputs):
    return {
        "L1_ffn_w_gate": np.ascontiguousarray(inputs["L1_ffn_w_gate"]),
        "L1_ffn_w_up": np.ascontiguousarray(inputs["L1_ffn_w_up"]),
        "L1_ffn_w_down": np.ascontiguousarray(inputs["L1_ffn_w_down"]),
    }


_L1FFN_SPECS = {
    "L1_ffn_w_gate": ([OFF, D], F32),
    "L1_ffn_w_up": ([OFF, D], F32),
    "L1_ffn_w_down": ([D, OFF], F32),
}


def build_l1_ffn_program(debug=False):
    nc = bass.Bass("TRN2", target_bir_lowering=False)
    stack = contextlib.ExitStack()
    dins = {}
    specs = {**_CONST_SPECS, "c_eyef": ([128, 128], F32), **_L1FFN_SPECS}
    for name, (shape, dt) in specs.items():
        dins[name] = nc.dram_tensor(name, shape, dt, kind="ExternalInput").ap()
    xin = nc.dram_tensor("xin", [D, T], F32, kind="ExternalInput").ap()
    xout = nc.dram_tensor("xout", [D, T], F32, kind="ExternalOutput").ap()
    kind = dict(kind="ExternalOutput") if debug else {}
    wgq_scr = nc.dram_tensor("wgq_scr", [D, OFF], BF16, **kind).ap()
    wuq_scr = nc.dram_tensor("wuq_scr", [D, OFF], BF16, **kind).ap()
    dbg = None
    if debug:
        dbg = {
            "wdqT": nc.dram_tensor("dbg_wdqT", [128, NOC * 1024], BF16, kind="ExternalOutput").ap(),
            "xqT": nc.dram_tensor("dbg_xqT", [128, NCH * TTS], BF16, kind="ExternalOutput").ap(),
            "hqT": nc.dram_tensor("dbg_hqT", [128, NOC * TTS], BF16, kind="ExternalOutput").ap(),
        }
    with stack:
        k = Ker(nc, stack)
        env = Env(k, dins)
        k.barrier()
        emit_l1_ffn(k, env, dins, xin, xout, wgq_scr, wuq_scr, dbg)
        k.lower(verbose=True)
    return nc

HID = 4096
NHC = HID // 128
NHEAD = 16
HD = 64


def emit_l1_attn(k: Ker, env, dins, xin, xout, w1b_scr, w2b_scr, rows_scr):
    names = ("q", "k", "v")
    with contextlib.ExitStack() as sc0:
        kT = k.sb("l1kT", [128, NCH * T], BF16, sc0)
        qT = k.sb("l1qT", [128, NCH * T], BF16, sc0)
        vaug = [k.sb(f"l1va{i}", [128, NHEAD * 65], BF16, sc0) for i in range(T // 128)]
        eyef = k.sb("l1eyef", [128, 128], F32, sc0)
        onesrb = k.sb("onesrb", [1, TTS], BF16, sc0)
        k.dma("sp", eyef[:, :], dins["c_eyef"], writes=[eyef])
        k.dma("sp", onesrb[:, :], dins["c_onesrowb"], writes=[onesrb])

        with contextlib.ExitStack() as sc:
            stg = [k.sb(f"astg{i}", [128, TTS], F32, sc) for i in range(2)]
            sq = [k.sb(f"asq{i}", [128, TTS], F32, sc) for i in range(2)]
            rows = {n_: k.sb(f"arow{n_}", [1, TTS], F32, sc) for n_ in ("m", "v", "r")}
            pb_a = k.ps("apba", [128, TTS], F32, sc)
            pb_b = k.ps("apbb", [128, TTS], F32, sc)
            for tt in range(NTT):
                ts = slice(tt * TTS, (tt + 1) * TTS)
                for c in range(NCH):
                    st = stg[c % 2]
                    k.dma("sp", st[:, :], xin[c * 128:(c + 1) * 128, ts],
                          writes=[st], tag="a_ldx")
                    k.op("pe", lambda e, st=st, c=c: e.matmul(
                        pb_a[0:1, :], env.ones128[:, :], st[:, :],
                        start=(c == 0), stop=(c == NCH - 1)),
                        reads=[env.ones128, st], writes=[pb_a])
                    s2 = sq[c % 2]
                    k.op("dve", lambda e, st=st, s2=s2: e.tensor_mul(
                        s2[:, :], st[:, :], st[:, :]),
                        reads=[st], writes=[s2])
                    k.op("pe", lambda e, s2=s2, c=c: e.matmul(
                        pb_b[0:1, :], env.ones128[:, :], s2[:, :],
                        start=(c == 0), stop=(c == NCH - 1)),
                        reads=[env.ones128, s2], writes=[pb_b])
                emit_ln_rows(k, env, pb_a, pb_b, rows, D)
                k.dma("sp", rows_scr[0:1, ts], rows["m"][0:1, :], reads=[rows["m"]],
                      tag="a_strow")
                k.dma("sp", rows_scr[1:2, ts], rows["r"][0:1, :], reads=[rows["r"]],
                      tag="a_strow")
        k.barrier()

        with contextlib.ExitStack() as sc:
            wst = [k.sb(f"cwst{i}", [128, HID], F32, sc) for i in range(2)]
            wbf = k.sb("cwbf", [128, HID], BF16, sc)
            for j, nm in enumerate(names):
                for c in range(NCH):
                    st = wst[c % 2]
                    k.dma("sp", st[:, :], dins[f"L1_attn_{nm}_w1"][c * 128:(c + 1) * 128, :],
                          writes=[st], tag="a_ldw1")
                    k.op("dve", lambda e, st=st: e.tensor_copy(wbf[:, :], st[:, :]),
                         reads=[st], writes=[wbf])
                    k.dma("sp", w1b_scr[j, c * 128:(c + 1) * 128, :], wbf[:, :],
                          reads=[wbf], tag="a_stw1")
                for hc in range(NHC):
                    st = wst[hc % 2]
                    k.dma("sp", st[:, 0:1024], dins[f"L1_attn_{nm}_w2"][hc * 128:(hc + 1) * 128, :],
                          writes=[st], tag="a_ldw2")
                    k.op("dve", lambda e, st=st: e.tensor_copy(wbf[:, 0:1024], st[:, 0:1024]),
                         reads=[st], writes=[wbf])
                    k.dma("sp", w2b_scr[j, hc * 128:(hc + 1) * 128, :], wbf[:, 0:1024],
                          reads=[wbf], tag="a_stw2")
        k.barrier()

        for j, nm in enumerate(names):
            with contextlib.ExitStack() as sc:
                b1r = k.sb("ab1r", [1, HID], BF16, sc)
                b2r = k.sb("ab2r", [1, D], BF16, sc)
                mrow = k.sb("amrow", [1, TTS], F32, sc)
                rrow = k.sb("arrow", [1, TTS], F32, sc)
                for i_ in range(HID // TTS):
                    k.dma("sp", mrow[:, :], dins[f"L1_attn_{nm}_b1"][0:1, i_ * TTS:(i_ + 1) * TTS],
                          writes=[mrow], tag="a_ldb1")
                    k.op("dve", lambda e, i_=i_: e.tensor_copy(
                        b1r[0:1, i_ * TTS:(i_ + 1) * TTS], mrow[0:1, :]),
                        reads=[mrow], writes=[b1r])
                for i_ in range(D // TTS):
                    k.dma("sp", mrow[:, :], dins[f"L1_attn_{nm}_b2"][0:1, i_ * TTS:(i_ + 1) * TTS],
                          writes=[mrow], tag="a_ldb2")
                    k.op("dve", lambda e, i_=i_: e.tensor_copy(
                        b2r[0:1, i_ * TTS:(i_ + 1) * TTS], mrow[0:1, :]),
                        reads=[mrow], writes=[b2r])
                stg = [k.sb(f"bstg{i}", [128, TTS], F32, sc) for i in range(2)]
                xn = [k.sb(f"axn{c}", [128, TTS], BF16, sc) for c in range(NCH)]
                apscr = [k.sb(f"aap{i}", [128, TTS], F32, sc) for i in range(2)]
                h_tt = k.sb("ahtt", [128, NHC * TTS], BF16, sc)
                w1w = [k.sb(f"aw1w{i}", [128, NCH * 128], BF16, sc) for i in range(2)]
                w2h = k.sb("aw2h", [128, NHC * TTS // 1], BF16, sc)
                pb_a = k.ps("bpba", [128, TTS], F32, sc)
                pb_b = k.ps("bpbb", [128, TTS], F32, sc)
                pb_h = [k.ps(f"bpbh{i}", [128, TTS], F32, sc) for i in range(2)]
                pb_o = [k.ps(f"bpbo{i}", [128, TTS], F32, sc) for i in range(2)]
                for tt in range(NTT):
                    ts = slice(tt * TTS, (tt + 1) * TTS)
                    k.dma("sp", mrow[:, :], rows_scr[0:1, ts], writes=[mrow], tag="a_ldrow")
                    k.dma("sp", rrow[:, :], rows_scr[1:2, ts], writes=[rrow], tag="a_ldrow")
                    k.op("pe", lambda e: e.matmul(pb_a[:, :], env.ones1r[:, :], mrow[0:1, :],
                                                  start=True, stop=True),
                         reads=[env.ones1r, mrow], writes=[pb_a])
                    k.op("pe", lambda e: e.matmul(pb_b[:, :], env.ones1r[:, :], rrow[0:1, :],
                                                  start=True, stop=True),
                         reads=[env.ones1r, rrow], writes=[pb_b])
                    for c in range(NCH):
                        st = stg[c % 2]
                        k.dma("sp", st[:, :], xin[c * 128:(c + 1) * 128, ts],
                              writes=[st], tag="a_ldx2")
                        scr = apscr[c % 2]
                        k.op("dve", lambda e, st=st, scr=scr: e.tensor_tensor(
                            scr[:, :], st[:, :], pb_a[:, :], ALU.subtract),
                            reads=[st, pb_a], writes=[scr])
                        k.op("dve", lambda e, c=c, scr=scr: e.tensor_tensor(
                            xn[c][:, :], scr[:, :], pb_b[:, :], ALU.mult),
                            reads=[scr, pb_b], writes=[xn[c]])
                    for g in range(4):
                        k.dma("sp",
                              w2h[:, g * 8 * TTS:(g + 1) * 8 * TTS].rearrange(
                                  "p (c n) -> p c n", n=TTS),
                              w2b_scr[j].rearrange("(c p) n -> p c n", p=128)[
                                  :, g * 8:(g + 1) * 8, 0:512],
                              writes=[w2h], tag="a_ldw2b")
                    for hc in range(NHC):
                        ww = w1w[hc % 2]
                        k.dma("sp",
                              ww[:, :].rearrange("p (c n) -> p c n", n=128),
                              w1b_scr[j].rearrange("(c p) n -> p c n", p=128)[
                                  :, :, hc * 128:(hc + 1) * 128],
                              writes=[ww], tag="a_ldw1b")
                        pbh = pb_h[hc % 2]
                        for dc in range(NCH):
                            k.op("pe", lambda e, dc=dc, ww=ww, pbh=pbh: e.matmul(
                                pbh[:, :], ww[:, dc * 128:(dc + 1) * 128], xn[dc][:, :],
                                start=(dc == 0), stop=False),
                                reads=[ww, xn[dc]], writes=[pbh])
                        k.op("pe", lambda e, hc=hc, pbh=pbh: e.matmul(
                            pbh[:, :], b1r[0:1, hc * 128:(hc + 1) * 128], onesrb[0:1, :],
                            start=False, stop=True),
                            reads=[b1r, onesrb], writes=[pbh])
                        k.op("act", lambda e, hc=hc, pbh=pbh: e.activation(
                            h_tt[:, hc * TTS:(hc + 1) * TTS], pbh[:, :], AF.Gelu),
                            reads=[pbh], writes=[h_tt])
                    for ohalf in range(2):
                        osl = slice(ohalf * 512, ohalf * 512 + 512)
                        if ohalf == 1:
                            for g in range(4):
                                k.dma("sp",
                                      w2h[:, g * 8 * TTS:(g + 1) * 8 * TTS].rearrange(
                                          "p (c n) -> p c n", n=TTS),
                                      w2b_scr[j].rearrange("(c p) n -> p c n", p=128)[
                                          :, g * 8:(g + 1) * 8, osl],
                                      writes=[w2h], tag="a_ldw2b")
                        if nm == "v":
                            for tsub in range(4):
                                ti = tt * 4 + tsub
                                pbo = pb_o[tsub % 2]
                                for hc in range(NHC):
                                    k.op("pe", lambda e, hc=hc, tsub=tsub, pbo=pbo: e.matmul(
                                        pbo[:, :],
                                        h_tt[:, hc * TTS + tsub * 128:hc * TTS + (tsub + 1) * 128],
                                        w2h[:, hc * TTS:(hc + 1) * TTS],
                                        start=(hc == 0), stop=False),
                                        reads=[h_tt, w2h], writes=[pbo])
                                k.op("pe", lambda e, osl=osl, pbo=pbo: e.matmul(
                                    pbo[:, :], env.ones1rb[0:1, :], b2r[0:1, osl],
                                    start=False, stop=True),
                                    reads=[env.ones1rb, b2r], writes=[pbo])
                                h0 = ohalf * 8
                                k.op("dve", lambda e, ti=ti, h0=h0, pbo=pbo: e.tensor_copy(
                                    vaug[ti][:, :].rearrange("p (hh c) -> p hh c", c=65)[
                                        :, h0:h0 + 8, 0:64],
                                    pbo[:, :].rearrange("p (hh c) -> p hh c", c=64)),
                                    reads=[pbo], writes=[vaug[ti]])
                        else:
                            dst = qT if nm == "q" else kT
                            for oq in range(4):
                                oc = ohalf * 4 + oq
                                pbo = pb_o[oq % 2]
                                for hc in range(NHC):
                                    k.op("pe", lambda e, hc=hc, oq=oq, pbo=pbo: e.matmul(
                                        pbo[:, :], w2h[:, hc * TTS + oq * 128:hc * TTS + (oq + 1) * 128],
                                        h_tt[:, hc * TTS:(hc + 1) * TTS],
                                        start=(hc == 0), stop=False),
                                        reads=[w2h, h_tt], writes=[pbo])
                                k.op("pe", lambda e, oc=oc, pbo=pbo: e.matmul(
                                    pbo[:, :], b2r[0:1, oc * 128:(oc + 1) * 128], onesrb[0:1, :],
                                    start=False, stop=True),
                                    reads=[b2r, onesrb], writes=[pbo])
                                k.op("dve", lambda e, oc=oc, tt=tt, dst=dst, pbo=pbo: e.tensor_copy(
                                    dst[:, oc * T + tt * TTS:oc * T + (tt + 1) * TTS], pbo[:, :]),
                                    reads=[pbo], writes=[dst])
            k.barrier()

        for ti in range(T // 128):
            k.op("dve", lambda e, ti=ti: e.memset(
                vaug[ti][:, :].rearrange("p (hh c) -> p hh c", c=65)[:, :, 64:65], 1.0),
                writes=[vaug[ti]], strict=True)

        with contextlib.ExitStack() as sc:
            wp = k.sb("awp", [128, NCH * 1024], BF16, sc)
            wpst = [k.sb(f"awpst{i}", [128, 1024], F32, sc) for i in range(2)]
            for c in range(NCH):
                st = wpst[c % 2]
                k.dma("sp", st[:, :], dins["L1_attn_proj_w"][c * 128:(c + 1) * 128, :],
                      writes=[st], tag="a_ldwp")
                k.op("dve", lambda e, st=st, c=c: e.tensor_copy(
                    wp[:, c * 1024:(c + 1) * 1024], st[:, :]),
                    reads=[st], writes=[wp])
            pbrow = k.sb("apbr", [1, D], BF16, sc)
            pbst = k.sb("apbst", [1, D], F32, sc)
            k.dma("sp", pbst[:, :], dins["L1_attn_proj_b"], writes=[pbst])
            k.op("dve", lambda e: e.tensor_copy(pbrow[:, :], pbst[:, :]),
                 reads=[pbst], writes=[pbrow])
            nsq = k.sb("ansq", [128, TTS], F32, sc)
            nB = k.sb("anB", [1, 32], F32, sc)
            negB = k.sb("anegB", [128, 16], F32, sc)
            ptil = [k.sb(f"aptil{i}", [128, TTS], BF16, sc) for i in range(3)]
            otok4 = [k.sb(f"aotok{i}", [128, NHEAD * HD], BF16, sc) for i in range(4)]
            otT = k.sb("aotT", [128, NCH * TTS], BF16, sc)
            rq2 = [k.sb(f"arq{i}", [128, 1], F32, sc) for i in range(2)]
            stg = [k.sb(f"cstg{i}", [128, TTS], F32, sc) for i in range(2)]
            res = [k.sb(f"cres{i}", [128, TTS], F32, sc) for i in range(2)]
            pb_s = [k.ps(f"cpbs{i}", [128, TTS], F32, sc) for i in range(2)]
            pb_av = [k.ps(f"cpbav{j}", [128, TTS], F32, sc) for j in range(2)]
            pb_t = k.ps("cpbt", [128, 1024], BF16, sc)
            pb_y = k.ps("cpby", [128, TTS], F32, sc)

            for h in range(NHEAD):
                c0, r0 = h // 2, (h % 2) * 64
                psl = slice(r0, r0 + 64)
                for which, src_t in ((0, qT), (1, kT)):
                    for tt4 in range(NTT):
                        k.op("dve", lambda e, src_t=src_t, c0=c0, psl=psl, tt4=tt4: e.tensor_mul(
                            nsq[psl, :],
                            src_t[psl, c0 * T + tt4 * TTS:c0 * T + (tt4 + 1) * TTS],
                            src_t[psl, c0 * T + tt4 * TTS:c0 * T + (tt4 + 1) * TTS]),
                            reads=[src_t], writes=[nsq], strict=True)
                        k.op("pe", lambda e, psl=psl: e.matmul(
                            pb_s[0][0:1, :], env.ones128[psl, :], nsq[psl, :],
                            start=True, stop=True),
                            reads=[env.ones128, nsq], writes=[pb_s[0]])
                        k.op("dve", lambda e, which=which, tt4=tt4: e.reduce_max(
                            nB[0:1, 3:4], pb_s[0][0:1, :], axis=AX.X),
                            reads=[pb_s[0]], writes=[nB], strict=True)
                        if tt4 == 0:
                            k.op("dve", lambda e, which=which: e.tensor_copy(
                                nB[0:1, which:which + 1], nB[0:1, 3:4]),
                                reads=[nB], writes=[nB], strict=True)
                        else:
                            k.op("dve", lambda e, which=which: e.tensor_tensor(
                                nB[0:1, which:which + 1], nB[0:1, which:which + 1],
                                nB[0:1, 3:4], ALU.max),
                                reads=[nB], writes=[nB], strict=True)
                k.op("dve", lambda e: e.tensor_mul(nB[0:1, 2:3], nB[0:1, 0:1], nB[0:1, 1:2]),
                     reads=[nB], writes=[nB], strict=True)
                k.op("act", lambda e: e.activation(nB[0:1, 2:3], nB[0:1, 2:3], AF.Ln, bias=1e-12),
                     reads=[nB], writes=[nB], strict=True)
                k.op("act", lambda e: e.activation(nB[0:1, 2:3], nB[0:1, 2:3], AF.Exp, scale=0.5),
                     reads=[nB], writes=[nB], strict=True)
                k.op("dve", lambda e, h=h: e.tensor_single_scalar(
                    nB[0:1, 8 + h:9 + h], nB[0:1, 2:3], -0.125, ALU.mult),
                    reads=[nB], writes=[nB], strict=True)
            for h in range(NHEAD):
                k.op("pe", lambda e, h=h: e.matmul(
                    pb_s[0][:, h:h + 1], env.ones1r[:, :], nB[0:1, 8 + h:9 + h],
                    start=True, stop=True),
                    reads=[env.ones1r, nB], writes=[pb_s[0]])
            k.op("dve", lambda e: e.tensor_copy(negB[:, 0:16], pb_s[0][:, 0:16]),
                 reads=[pb_s[0]], writes=[negB], strict=True)

            for qt in range(NTT):
                qs0 = qt * TTS
                for h in range(NHEAD):
                    c0, r0 = h // 2, (h % 2) * 64
                    psl = slice(r0, r0 + 64)
                    NKT = T // 128
                    pav = pb_av[h % 2]
                    for kt in range(NKT + 1):
                        if kt < NKT:
                            pbs = pb_s[kt % 2]
                            k.op("pe", lambda e, kt=kt, c0=c0, psl=psl, qs0=qs0, pbs=pbs: e.matmul(
                                pbs[:, :], kT[psl, c0 * T + kt * 128:c0 * T + (kt + 1) * 128],
                                qT[psl, c0 * T + qs0:c0 * T + qs0 + TTS],
                                start=True, stop=True),
                                reads=[kT, qT], writes=[pbs])
                            pt = ptil[kt % 3]
                            k.op("act", lambda e, h=h, pbs=pbs, pt=pt: e.activation(
                                pt[:, :], pbs[:, :], AF.Exp, bias=negB[:, h:h + 1], scale=0.125),
                                reads=[pbs, negB], writes=[pt])
                        if kt >= 1:
                            ktp = kt - 1
                            ptp = ptil[ktp % 3]
                            for qsub in range(4):
                                k.op("pe", lambda e, ktp=ktp, qsub=qsub, h=h, ptp=ptp, pav=pav: e.matmul(
                                    pav[:, qsub * 128:qsub * 128 + 65], ptp[:, qsub * 128:(qsub + 1) * 128],
                                    vaug[ktp][:, h * 65:(h + 1) * 65],
                                    start=(ktp == 0), stop=(ktp == NKT - 1)),
                                    reads=[ptp, vaug[ktp]], writes=[pav])
                    pav = pb_av[h % 2]
                    rq = rq2[h % 2]
                    for qsub in range(4):
                        k.op("dve", lambda e, qsub=qsub, pav=pav, rq=rq: e.reciprocal(
                            rq[:, 0:1], pav[:, qsub * 128 + 64:qsub * 128 + 65]),
                            reads=[pav], writes=[rq], strict=True)
                        k.op("dve", lambda e, h=h, qsub=qsub, pav=pav, rq=rq: e.tensor_scalar(
                            otok4[qsub][:, h * HD:(h + 1) * HD], pav[:, qsub * 128:qsub * 128 + 64],
                            rq[:, 0:1], None, ALU.mult),
                            reads=[pav, rq], writes=[otok4[qsub]])
                for qsub in range(4):
                    for c in range(NCH):
                        k.op("pe", lambda e, c=c, qsub=qsub: e.transpose(
                            pb_t[:, c * 128:(c + 1) * 128], otok4[qsub][:, c * 128:(c + 1) * 128],
                            env.eyeb[:, :]),
                            reads=[otok4[qsub], env.eyeb], writes=[pb_t])
                    k.op("dve", lambda e, qsub=qsub: e.tensor_copy(
                        otT[:, :].rearrange("p (c n) -> p c n", n=TTS)[
                            :, :, qsub * 128:(qsub + 1) * 128],
                        pb_t[:, :].rearrange("p (c n) -> p c n", n=128)),
                        reads=[pb_t], writes=[otT])
                for oc in range(NCH):
                    for fc in range(NCH):
                        k.op("pe", lambda e, fc=fc, oc=oc: e.matmul(
                            pb_y[:, :], wp[:, fc * 1024 + oc * 128:fc * 1024 + (oc + 1) * 128],
                            otT[:, fc * TTS:(fc + 1) * TTS],
                            start=(fc == 0), stop=False),
                            reads=[wp, otT], writes=[pb_y])
                    k.op("pe", lambda e, oc=oc: e.matmul(
                        pb_y[:, :], pbrow[0:1, oc * 128:(oc + 1) * 128], onesrb[0:1, :],
                        start=False, stop=True),
                        reads=[pbrow, onesrb], writes=[pb_y])
                    st = stg[oc % 2]
                    k.dma("sp", st[:, :], xin[oc * 128:(oc + 1) * 128, qs0:qs0 + TTS],
                          writes=[st], tag="a_ldx3")
                    rs = res[oc % 2]
                    k.op("dve", lambda e, st=st, rs=rs: e.tensor_tensor(
                        rs[:, :], st[:, :], pb_y[:, :], ALU.add),
                        reads=[st, pb_y], writes=[rs])
                    k.dma("sp", xout[oc * 128:(oc + 1) * 128, qs0:qs0 + TTS], rs[:, :],
                          reads=[rs], tag="a_stx")
    k.barrier()


def host_prep_l1_attn(inputs):
    d = {}
    for nm in ("q", "k", "v"):
        d[f"L1_attn_{nm}_w1"] = np.ascontiguousarray(inputs[f"L1_attn_{nm}_w1"])
        d[f"L1_attn_{nm}_w2"] = np.ascontiguousarray(inputs[f"L1_attn_{nm}_w2"])
        d[f"L1_attn_{nm}_b1"] = inputs[f"L1_attn_{nm}_b1"].reshape(1, HID)
        d[f"L1_attn_{nm}_b2"] = inputs[f"L1_attn_{nm}_b2"].reshape(1, D)
    d["L1_attn_proj_w"] = np.ascontiguousarray(inputs["L1_attn_proj_w"])
    d["L1_attn_proj_b"] = inputs["L1_attn_proj_b"].reshape(1, D)
    return d


_L1ATTN_SPECS = {}
for _nm in ("q", "k", "v"):
    _L1ATTN_SPECS[f"L1_attn_{_nm}_w1"] = ([D, HID], F32)
    _L1ATTN_SPECS[f"L1_attn_{_nm}_w2"] = ([HID, D], F32)
    _L1ATTN_SPECS[f"L1_attn_{_nm}_b1"] = ([1, HID], F32)
    _L1ATTN_SPECS[f"L1_attn_{_nm}_b2"] = ([1, D], F32)
_L1ATTN_SPECS["L1_attn_proj_w"] = ([D, D], F32)
_L1ATTN_SPECS["L1_attn_proj_b"] = ([1, D], F32)


def build_l1_attn_program():
    nc = bass.Bass("TRN2", target_bir_lowering=False)
    stack = contextlib.ExitStack()
    dins = {}
    specs = {**_CONST_SPECS, "c_eyef": ([128, 128], F32),
             "c_onesrowb": ([1, TTS], BF16), "c_ones1rb": ([1, 128], BF16),
             **_L1ATTN_SPECS}
    for name, (shape, dt) in specs.items():
        dins[name] = nc.dram_tensor(name, shape, dt, kind="ExternalInput").ap()
    xin = nc.dram_tensor("xin", [D, T], F32, kind="ExternalInput").ap()
    xout = nc.dram_tensor("xout", [D, T], F32, kind="ExternalOutput").ap()
    w1b_scr = nc.dram_tensor("w1b_scr", [3, D, HID], BF16).ap()
    w2b_scr = nc.dram_tensor("w2b_scr", [3, HID, D], BF16).ap()
    rows_scr = nc.dram_tensor("rows_scr", [2, T], F32).ap()
    with stack:
        k = Ker(nc, stack)
        env = Env(k, dins)
        k.barrier()
        emit_l1_attn(k, env, dins, xin, xout, w1b_scr, w2b_scr, rows_scr)
        k.lower(verbose=True)
    return nc

L0CUT = 99
NCK_RUN = 99
CHUNK_BAR = False
STAGE_CUT = 99
DK = 128
DV = 256
NH4 = 4
CC = 128
NCK = T // CC


def emit_l0_attn(k: Ker, env, dins, xin, xout, dbg=None):
    with contextlib.ExitStack() as sc0:
        eyef = k.sb("deyef", [128, 128], F32, sc0)
        umask_s = k.sb("dums", [128, 128], BF16, sc0)
        umask_i = k.sb("dumi", [128, 128], BF16, sc0)
        k.dma("sp", eyef[:, :], dins["c_eyef"], writes=[eyef])
        k.dma("sp", umask_s[:, :], dins["c_umask_s"], writes=[umask_s])
        k.dma("sp", umask_i[:, :], dins["c_umask_i"], writes=[umask_i])
        xn = [k.sb(f"dxn{c}", [128, T], BF16, sc0) for c in range(NCH)]
        qT = k.sb("dqT", [128, NH4 * T], BF16, sc0)
        kT = k.sb("dkT", [128, NH4 * T], BF16, sc0)
        beta = [k.sb(f"dbeta{i}", [65, T], BF16, sc0) for i in range(2)]
        ones65 = k.sb("dones65", [65, 128], BF16, sc0)
        k.dma("sp", ones65[:, :], dins["c_ones65"], writes=[ones65])
        wq = k.sb("dwq", [128, NCH * 512], BF16, sc0)
        wk = k.sb("dwk", [128, NCH * 512], BF16, sc0)
        wv = k.sb("dwv", [128, NCH * D], BF16, sc0)
        wg = k.sb("dwg", [128, NCH * D], BF16, sc0)
        wo = k.sb("dwo", [128, NCH * D], BF16, sc0)
        wb = k.sb("dwb", [128, NCH * 4], BF16, sc0)
        nwcol = k.sb("dnw", [128, 2], F32, sc0)
        k.dma("sp", nwcol[:, :], dins["L0_attn_nwcol"], writes=[nwcol])

        QSCALE = 2.0 ** -3.5

        def load_w(dst, src_ap, ncols, scale=None, nw=False):
            for dc in range(NCH):
                st = wst[dc % 2]
                k.dma("sp", st[:, 0:ncols], src_ap[dc * 128:(dc + 1) * 128, :],
                      writes=[st], tag="d_ldw")
                if nw:
                    k.op("dve", lambda e, dc=dc, st=st: e.tensor_scalar(
                        dst[:, dc * ncols:(dc + 1) * ncols], st[:, 0:ncols],
                        nwcol[:, dc % 2:dc % 2 + 1], None, ALU.mult),
                        reads=[st, nwcol], writes=[dst])
                elif scale is not None:
                    k.op("dve", lambda e, dc=dc, st=st, scale=scale: e.tensor_single_scalar(
                        dst[:, dc * ncols:(dc + 1) * ncols], st[:, 0:ncols], scale, ALU.mult),
                        reads=[st], writes=[dst])
                else:
                    k.op("dve", lambda e, dc=dc, st=st: e.tensor_copy(
                        dst[:, dc * ncols:(dc + 1) * ncols], st[:, 0:ncols]),
                        reads=[st], writes=[dst])

        with contextlib.ExitStack() as sc:
            wst = [k.sb(f"dwst{i}", [128, D], F32, sc) for i in range(2)]
            load_w(wq, dins["L0_attn_Wq"], 512, scale=QSCALE)
            if L0CUT >= 2:
                load_w(wk, dins["L0_attn_Wk"], 512)
            if L0CUT >= 5:
                load_w(wv, dins["L0_attn_Wv"], D)
            if L0CUT >= 6:
                load_w(wg, dins["L0_attn_Wg"], D)
            if L0CUT >= 7:
                load_w(wo, dins["L0_attn_Wo"], D, nw=True)
            if L0CUT >= 4:
                load_w(wb, dins["L0_attn_Wb"], 4)
            stg = [k.sb(f"dstg{i}", [128, TTS], F32, sc) for i in range(2)]
            sq = [k.sb(f"dsq{i}", [128, TTS], F32, sc) for i in range(2)]
            rows = {n_: k.sb(f"drow{n_}", [1, TTS], F32, sc) for n_ in ("m", "v", "r")}
            pb_a = k.ps("dpba", [128, TTS], F32, sc)
            pb_b = k.ps("dpbb", [128, TTS], F32, sc)
            for tt in range(NTT):
                ts = slice(tt * TTS, (tt + 1) * TTS)
                for c in range(NCH):
                    st = stg[c % 2]
                    k.dma("sp", st[:, :], xin[c * 128:(c + 1) * 128, ts],
                          writes=[st], tag="d_ldx")
                    k.op("pe", lambda e, st=st, c=c: e.matmul(
                        pb_a[0:1, :], env.ones128[:, :], st[:, :],
                        start=(c == 0), stop=(c == NCH - 1)),
                        reads=[env.ones128, st], writes=[pb_a])
                    s2 = sq[c % 2]
                    k.op("dve", lambda e, st=st, s2=s2: e.tensor_mul(
                        s2[:, :], st[:, :], st[:, :]),
                        reads=[st], writes=[s2])
                    k.op("pe", lambda e, s2=s2, c=c: e.matmul(
                        pb_b[0:1, :], env.ones128[:, :], s2[:, :],
                        start=(c == 0), stop=(c == NCH - 1)),
                        reads=[env.ones128, s2], writes=[pb_b])
                emit_ln_rows(k, env, pb_a, pb_b, rows, D)
                k.op("pe", lambda e: e.matmul(pb_a[:, :], env.ones1r[:, :], rows["m"][0:1, :],
                                              start=True, stop=True),
                     reads=[env.ones1r, rows["m"]], writes=[pb_a])
                k.op("pe", lambda e: e.matmul(pb_b[:, :], env.ones1r[:, :], rows["r"][0:1, :],
                                              start=True, stop=True),
                     reads=[env.ones1r, rows["r"]], writes=[pb_b])
                for c in range(NCH):
                    st = stg[c % 2]
                    k.dma("sp", st[:, :], xin[c * 128:(c + 1) * 128, ts],
                          writes=[st], tag="d_ldx2")
                    scr = sq[c % 2]
                    k.op("dve", lambda e, st=st, scr=scr: e.tensor_tensor(
                        scr[:, :], st[:, :], pb_a[:, :], ALU.subtract),
                        reads=[st, pb_a], writes=[scr])
                    k.op("dve", lambda e, c=c, ts=ts, scr=scr: e.tensor_tensor(
                        xn[c][:, ts], scr[:, :], pb_b[:, :], ALU.mult),
                        reads=[scr, pb_b], writes=[xn[c]])
        k.barrier()

        with contextlib.ExitStack() as sc:
            rrow = k.sb("drr", [1, TTS], F32, sc)
            sqs = k.sb("dsqs", [128, TTS], F32, sc)
            pb_p = k.ps("dpbp", [128, TTS], F32, sc)
            pb_r = k.ps("dpbr", [128, TTS], F32, sc)
            pb_q = [k.ps(f"dpbq{i}", [128, TTS], F32, sc) for i in range(2)]
            for h in range(NH4):
                for tt in range(NTT):
                    ts = slice(tt * TTS, (tt + 1) * TTS)
                    pbq = pb_q[tt % 2]
                    for dc in range(NCH):
                        k.op("pe", lambda e, dc=dc, h=h, ts=ts, pbq=pbq: e.matmul(
                            pbq[:, :], wq[:, dc * 512 + h * 128:dc * 512 + (h + 1) * 128],
                            xn[dc][:, ts], start=(dc == 0), stop=(dc == NCH - 1)),
                            reads=[wq, xn[dc]], writes=[pbq])
                    k.op("dve", lambda e, h=h, tt=tt, pbq=pbq: e.tensor_copy(
                        qT[:, h * T + tt * TTS:h * T + (tt + 1) * TTS], pbq[:, :]),
                        reads=[pbq], writes=[qT])
                    if L0CUT < 2:
                        continue
                    pbk = pb_q[(tt + 1) % 2]
                    for dc in range(NCH):
                        k.op("pe", lambda e, dc=dc, h=h, ts=ts, pbk=pbk: e.matmul(
                            pbk[:, :], wk[:, dc * 512 + h * 128:dc * 512 + (h + 1) * 128],
                            xn[dc][:, ts], start=(dc == 0), stop=(dc == NCH - 1)),
                            reads=[wk, xn[dc]], writes=[pbk])
                    k.op("dve", lambda e, h=h, tt=tt, pbk=pbk: e.tensor_copy(
                        kT[:, h * T + tt * TTS:h * T + (tt + 1) * TTS], pbk[:, :]),
                        reads=[pbk], writes=[kT])
                    if L0CUT < 3:
                        continue
                    k.op("dve", lambda e, h=h, tt=tt: e.tensor_mul(
                        sqs[:, :], kT[:, h * T + tt * TTS:h * T + (tt + 1) * TTS], kT[:, h * T + tt * TTS:h * T + (tt + 1) * TTS]),
                        reads=[kT], writes=[sqs])
                    k.op("pe", lambda e: e.matmul(
                        pb_p[0:1, :], env.ones128[:, :], sqs[:, :], start=True, stop=True),
                        reads=[env.ones128, sqs], writes=[pb_p])
                    k.op("dve", lambda e: e.tensor_copy(rrow[0:1, :], pb_p[0:1, :]),
                         reads=[pb_p], writes=[rrow], strict=True)
                    k.op("act", lambda e: e.activation(rrow[0:1, :], rrow[0:1, :],
                                                       AF.Ln, bias=1e-12),
                         reads=[rrow], writes=[rrow], strict=True)
                    k.op("act", lambda e: e.activation(rrow[0:1, :], rrow[0:1, :],
                                                       AF.Exp, scale=-0.5),
                         reads=[rrow], writes=[rrow], strict=True)
                    k.op("pe", lambda e: e.matmul(pb_r[:, :], env.ones1r[:, :], rrow[0:1, :],
                                                  start=True, stop=True),
                         reads=[env.ones1r, rrow], writes=[pb_r])
                    k.op("dve", lambda e, h=h, tt=tt: e.tensor_tensor(
                        kT[:, h * T + tt * TTS:h * T + (tt + 1) * TTS],
                        kT[:, h * T + tt * TTS:h * T + (tt + 1) * TTS], pb_r[:, :], ALU.mult),
                        reads=[kT, pb_r], writes=[kT])
                    k.op("pe", lambda e, h=h, ts=ts: e.matmul(
                        pb_p[0:1, 0:TTS], wb[0:128, 0 * 4 + h:h + 1] if False else
                        wb[:, h:h + 1], xn[0][:, ts], start=True, stop=False),
                        reads=[wb, xn[0]], writes=[pb_p]) if False else None
            for h in range(NH4 if L0CUT >= 4 else 0):
                for tt in range(NTT):
                    ts = slice(tt * TTS, (tt + 1) * TTS)
                    for dc in range(NCH):
                        k.op("pe", lambda e, dc=dc, h=h, ts=ts: e.matmul(
                            pb_p[0:1, :], wb[:, dc * 4 + h:dc * 4 + h + 1], xn[dc][:, ts],
                            start=(dc == 0), stop=(dc == NCH - 1)),
                            reads=[wb, xn[dc]], writes=[pb_p])
                    k.op("act", lambda e, h=h, tt=tt: e.activation(
                        beta[h // 2][32 * (h % 2):32 * (h % 2) + 1,
                                     tt * TTS:(tt + 1) * TTS],
                        pb_p[0:1, :], AF.Sigmoid),
                        reads=[pb_p], writes=[beta[h // 2]])
        if dbg:
            k.dma("sp", dbg["wq"], wq[:, :], reads=[wq], tag="dbg")
            k.dma("sp", dbg["xn7"], xn[7][:, :], reads=[xn[7]], tag="dbg")
            k.dma("sp", dbg["qT"], qT[:, :], reads=[qT], tag="dbg")
            k.dma("sp", dbg["kT"], kT[:, :], reads=[kT], tag="dbg")
            k.dma("sp", dbg["xn"], xn[0][:, :], reads=[xn[0]], tag="dbg")
        k.barrier()

        with contextlib.ExitStack() as sc:
            S = [k.sb(f"dS{h}", [128, DV], F32, sc) for h in range(NH4)]
            for h in range(NH4):
                k.op("dve", lambda e, h=h: e.memset(S[h][:, :], 0.0), writes=[S[h]])
            S_bf = [k.sb(f"dSbf{h}", [128, DV], BF16, sc) for h in range(NH4)]
            v_bf = [k.sb(f"dvbf{h}", [128, DV], BF16, sc) for h in range(NH4)]
            u_scr = k.sb("duscr", [128, DV], F32, sc)
            u_bf = [k.sb(f"dubf{h}", [128, DV], BF16, sc) for h in range(NH4)]
            d_bf = [k.sb(f"ddbf{h}", [128, DV], BF16, sc) for h in range(NH4)]
            n_bf = [k.sb(f"dnbf{h}", [128, CC], BF16, sc) for h in range(NH4)]
            nt_bf = [k.sb(f"dntbf{h}", [128, CC], BF16, sc) for h in range(NH4)]
            gt_bf = [k.sb(f"dgtbf{h}", [128, CC], BF16, sc) for h in range(NH4)]
            ktok = [k.sb(f"dktok{h}", [128, CC], BF16, sc) for h in range(NH4)]
            bcol = k.sb("dbcol", [128, 4], F32, sc)
            EE = [[k.sb(f"dE{i}{h}", [128, CC], BF16, sc) for h in range(NH4)]
                  for i in range(2)]
            FF = [[k.sb(f"dF{i}{h}", [128, CC], BF16, sc) for h in range(NH4)]
                  for i in range(2)]
            XX = [[k.sb(f"dX{i}{h}", [128, CC], BF16, sc) for h in range(NH4)]
                  for i in range(2)]
            YY = [[k.sb(f"dY{i}{h}", [128, CC], BF16, sc) for h in range(NH4)]
                  for i in range(2)]
            rms = k.sb("drms", [128, 4], F32, sc)
            o_f32 = [k.sb(f"dof32{h}", [128, DV], F32, sc) for h in range(NH4)]
            on_scr = k.sb("don", [128, DV], F32, sc)
            gsilu = k.sb("dgsilu", [128, (T // 128) * D], BF16, sc)
            otok = k.sb("dotok", [128, D], BF16, sc)
            otT = k.sb("dotT", [128, D], BF16, sc)
            stg = [k.sb(f"estg{i}", [128, CC], F32, sc) for i in range(2)]
            res = [o_f32[0], o_f32[1]]
            pb1 = k.ps("epb1", [128, TTS], F32, sc)
            pb2 = k.ps("epb2", [128, TTS], F32, sc)
            pb3 = k.ps("epb3", [128, TTS], F32, sc)
            pb4 = k.ps("epb4", [128, TTS], F32, sc)
            pb5 = k.ps("epb5", [128, TTS], F32, sc)
            pbt = k.ps("epbt", [128, 1024], BF16, sc)

            sg_scr = u_scr
            for ti in range(T // 128):
                tcs = slice(ti * 128, (ti + 1) * 128)
                for h in range(NH4):
                    pbg = pb4 if (h % 2 == 0) else pb2
                    off = 0 if (h % 2 == 0) else DV
                    for dc in range(NCH):
                        k.op("pe", lambda e, dc=dc, h=h, tcs=tcs, pbg=pbg, off=off: e.matmul(
                            pbg[:, off:off + DV], xn[dc][:, tcs],
                            wg[:, dc * D + h * DV:dc * D + (h + 1) * DV],
                            start=(dc == 0), stop=(dc == NCH - 1)),
                            reads=[xn[dc], wg], writes=[pbg])
                    k.op("act", lambda e, pbg=pbg, off=off: e.activation(
                        sg_scr[:, :], pbg[:, off:off + DV], AF.Sigmoid),
                        reads=[pbg], writes=[sg_scr])
                    k.op("dve", lambda e, ti=ti, h=h, pbg=pbg, off=off: e.tensor_tensor(
                        gsilu[:, ti * D + h * DV:ti * D + (h + 1) * DV],
                        sg_scr[:, :], pbg[:, off:off + DV], ALU.mult),
                        reads=[sg_scr, pbg], writes=[gsilu], strict=True)
            for c in range(min(NCK, NCK_RUN) if L0CUT >= 5 else 0):
                cs = slice(c * CC, (c + 1) * CC)
                ks = [kT[:, h * T + c * CC:h * T + (c + 1) * CC] for h in range(NH4)]
                qs = [qT[:, h * T + c * CC:h * T + (c + 1) * CC] for h in range(NH4)]
                pbV = lambda h: (pb2 if h < 2 else pb3)[:, (h % 2) * DV:(h % 2 + 1) * DV]
                pbVt = lambda h: pb2 if h < 2 else pb3
                for h in range(NH4):
                    k.op("dve", lambda e, h=h: e.tensor_copy(S_bf[h][:, :], S[h][:, :]),
                         reads=[S[h]], writes=[S_bf[h]], tag=f"s0cp_c{c}_h{h}")
                for h in range(NH4 if STAGE_CUT >= 1 else 0):
                    for dc in range(NCH):
                        k.op("pe", lambda e, dc=dc, h=h, cs=cs: e.matmul(
                            pbV(h), xn[dc][:, cs],
                            wv[:, dc * D + h * DV:dc * D + (h + 1) * DV],
                            start=(dc == 0), stop=(dc == NCH - 1)),
                            reads=[xn[dc], wv], writes=[pbVt(h)])
                for h in range(NH4 if STAGE_CUT >= 1 else 0):
                    k.op("act", lambda e, h=h, pbV=pbV: e.activation(
                        v_bf[h][:, :], pbV(h), AF.Copy),
                        reads=[pbVt(h)], writes=[v_bf[h]])
                for h in range(NH4 if STAGE_CUT >= 2 else 0):
                    k.op("pe", lambda e, h=h, ks=ks: e.matmul(
                        pb1[:, h * CC:(h + 1) * CC], ks[h], ks[h], start=True, stop=True),
                        reads=[kT], writes=[pb1])
                for h in range(NH4 if STAGE_CUT >= 3 else 0):
                    r0_ = 32 * (h % 2)
                    k.op("dve", lambda e, h=h, c=c, r0_=r0_: e.tensor_copy(
                        otT[0:1, h * CC:(h + 1) * CC],
                        beta[h // 2][r0_:r0_ + 1, c * CC:(c + 1) * CC]),
                        reads=[beta[h // 2]], writes=[otT], strict=True)
                for h in range(NH4 if STAGE_CUT >= 3 else 0):
                    k.op("pe", lambda e, h=h: e.matmul(
                        pb4[:, h * CC:(h + 1) * CC], env.ones1rb[:, :],
                        otT[0:1, h * CC:(h + 1) * CC],
                        start=True, stop=True),
                        reads=[env.ones1rb, otT], writes=[pb4], tag=f"s3bc_c{c}_h{h}")
                for h in range(NH4 if STAGE_CUT >= 4 else 0):
                    k.op("dve", lambda e, h=h: e.tensor_tensor(
                        u_scr[:, 0:CC], pb1[:, h * CC:(h + 1) * CC], umask_s[:, :], ALU.mult),
                        reads=[pb1, umask_s], writes=[u_scr], strict=True)
                    k.op("dve", lambda e, h=h: e.tensor_tensor(
                        n_bf[h][:, :], u_scr[:, 0:CC], pb4[:, h * CC:(h + 1) * CC], ALU.mult),
                        reads=[u_scr, pb4], writes=[n_bf[h]], strict=True)
                for h in range(NH4 if STAGE_CUT >= 5 else 0):
                    k.op("pe", lambda e, h=h: e.transpose(
                        pbt[:, 896 + 2 * h:897 + 2 * h],
                        otT[0:1, h * CC:(h + 1) * CC], env.eyeb[0:1, 0:1]),
                        reads=[otT, env.eyeb], writes=[pbt])
                for h in range(NH4 if STAGE_CUT >= 5 else 0):
                    k.op("dve", lambda e, h=h: e.tensor_copy(
                        bcol[:, h:h + 1], pbt[:, 896 + 2 * h:897 + 2 * h]),
                        reads=[pbt], writes=[bcol], strict=True)
                for h in range(NH4 if STAGE_CUT >= 6 else 0):
                    k.op("pe", lambda e, h=h: e.transpose(
                        pbt[:, h * 128:(h + 1) * 128], n_bf[h][:, :], env.eyeb[:, :]),
                        reads=[n_bf[h], env.eyeb], writes=[pbt])
                for h in range(NH4 if STAGE_CUT >= 6 else 0):
                    k.op("act", lambda e, h=h: e.activation(
                        nt_bf[h][:, :], pbt[:, h * 128:(h + 1) * 128], AF.Copy),
                        reads=[pbt], writes=[nt_bf[h]], strict=True)
                for h in range(NH4 if STAGE_CUT >= 7 else 0):
                    k.op("dve", lambda e, h=h: e.tensor_tensor(
                        XX[0][h][:, :], env.eyeb[:, :], n_bf[h][:, :], ALU.subtract),
                        reads=[env.eyeb, n_bf[h]], writes=[XX[0][h]], strict=True)
                    k.op("dve", lambda e, h=h: e.tensor_tensor(
                        YY[0][h][:, :], env.eyeb[:, :], nt_bf[h][:, :], ALU.subtract),
                        reads=[env.eyeb, nt_bf[h]], writes=[YY[0][h]], strict=True)
                Ep, Fp = n_bf, nt_bf
                Xp, Yp = XX[0], YY[0]
                for it in range(6 if STAGE_CUT >= 8 else 0):
                    En = EE[it % 2]
                    Fn = FF[it % 2]
                    Xn_ = XX[(it + 1) % 2]
                    Yn_ = YY[(it + 1) % 2]
                    for h in range(NH4):
                        k.op("pe", lambda e, h=h, Fp=Fp, Ep=Ep: e.matmul(
                            pb5[:, h * CC:(h + 1) * CC], Fp[h][:, :], Ep[h][:, :],
                            start=True, stop=True),
                            reads=[Fp[h], Ep[h]], writes=[pb5], tag=f"s8nm_c{c}_it{it}_h{h}")
                    for h in range(NH4):
                        k.op("act", lambda e, h=h, En=En: e.activation(
                            En[h][:, :], pb5[:, h * CC:(h + 1) * CC], AF.Copy),
                            reads=[pb5], writes=[En[h]], strict=True)
                    if it < 5:
                        for h in range(NH4):
                            k.op("pe", lambda e, h=h, Ep=Ep, Fp=Fp: e.matmul(
                                pb5[:, h * CC:(h + 1) * CC], Ep[h][:, :], Fp[h][:, :],
                                start=True, stop=True),
                                reads=[Ep[h], Fp[h]], writes=[pb5])
                        for h in range(NH4):
                            k.op("act", lambda e, h=h, Fn=Fn: e.activation(
                                Fn[h][:, :], pb5[:, h * CC:(h + 1) * CC], AF.Copy),
                                reads=[pb5], writes=[Fn[h]], strict=True)
                    for h in range(NH4):
                        k.op("pe", lambda e, h=h, Yp=Yp, En=En: e.matmul(
                            pb5[:, h * CC:(h + 1) * CC], Yp[h][:, :], En[h][:, :],
                            start=True, stop=True),
                            reads=[Yp[h], En[h]], writes=[pb5])
                    for h in range(NH4):
                        k.op("dve", lambda e, h=h, Xp=Xp, Xn_=Xn_: e.tensor_tensor(
                            Xn_[h][:, :], Xp[h][:, :], pb5[:, h * CC:(h + 1) * CC], ALU.add),
                            reads=[Xp[h], pb5], writes=[Xn_[h]], strict=True)
                    if it < 5:
                        for h in range(NH4):
                            k.op("pe", lambda e, h=h, En=En, Yp=Yp: e.matmul(
                                pb5[:, h * CC:(h + 1) * CC], En[h][:, :], Yp[h][:, :],
                                start=True, stop=True),
                                reads=[En[h], Yp[h]], writes=[pb5])
                        for h in range(NH4):
                            k.op("dve", lambda e, h=h, Yp=Yp, Yn_=Yn_: e.tensor_tensor(
                                Yn_[h][:, :], Yp[h][:, :], pb5[:, h * CC:(h + 1) * CC],
                                ALU.add),
                                reads=[Yp[h], pb5], writes=[Yn_[h]], strict=True)
                    Ep, Fp = En, Fn
                    Xp, Yp = Xn_, Yn_
                Xfin = Xp
                for h in range(NH4 if STAGE_CUT >= 9 else 0):
                    k.op("pe", lambda e, h=h, ks=ks, pbV=pbV: e.matmul(
                        pbV(h), ks[h], S_bf[h][:, :], start=True, stop=True),
                        reads=[kT, S_bf[h]], writes=[pbVt(h)], tag=f"s9p0_c{c}_h{h}")
                for h in range(NH4 if STAGE_CUT >= 10 else 0):
                    k.op("dve", lambda e, h=h: e.tensor_tensor(
                        u_scr[:, :], v_bf[h][:, :], pbV(h), ALU.subtract),
                        reads=[v_bf[h], pbVt(h)], writes=[u_scr], strict=True)
                    k.op("dve", lambda e, h=h: e.tensor_scalar(
                        u_bf[h][:, :], u_scr[:, :], bcol[:, h:h + 1], None, ALU.mult),
                        reads=[u_scr, bcol], writes=[u_bf[h]], strict=True)
                for h in range(NH4 if STAGE_CUT >= 11 else 0):
                    k.op("pe", lambda e, h=h, Xfin=Xfin: e.matmul(
                        pbV(h), Xfin[h][:, :], u_bf[h][:, :], start=True, stop=True),
                        reads=[Xfin[h], u_bf[h]], writes=[pbVt(h)])
                for h in range(NH4 if STAGE_CUT >= 11 else 0):
                    k.op("act", lambda e, h=h, pbV=pbV: e.activation(
                        d_bf[h][:, :], pbV(h), AF.Copy),
                        reads=[pbVt(h)], writes=[d_bf[h]], strict=True)
                for h in range(NH4 if STAGE_CUT >= 12 else 0):
                    k.op("pe", lambda e, h=h, ks=ks, qs=qs: e.matmul(
                        pb1[:, h * CC:(h + 1) * CC], ks[h], qs[h], start=True, stop=True),
                        reads=[kT, qT], writes=[pb1])
                for h in range(NH4 if STAGE_CUT >= 12 else 0):
                    k.op("dve", lambda e, h=h: e.tensor_tensor(
                        gt_bf[h][:, :], pb1[:, h * CC:(h + 1) * CC], umask_i[:, :], ALU.mult),
                        reads=[pb1, umask_i], writes=[gt_bf[h]], strict=True)
                for h in range(NH4 if STAGE_CUT >= 13 else 0):
                    k.op("pe", lambda e, h=h, ks=ks: e.transpose(
                        pbt[:, 512 + h * 128:512 + (h + 1) * 128], ks[h], env.eyeb[:, :]),
                        reads=[kT, env.eyeb], writes=[pbt])
                for h in range(NH4 if STAGE_CUT >= 13 else 0):
                    k.op("act", lambda e, h=h: e.activation(
                        ktok[h][:, :], pbt[:, 512 + h * 128:512 + (h + 1) * 128], AF.Copy),
                        reads=[pbt], writes=[ktok[h]], strict=True)
                for h in range(NH4 if STAGE_CUT >= 14 else 0):
                    k.op("pe", lambda e, h=h, qs=qs, pbV=pbV: e.matmul(
                        pbV(h), qs[h], S_bf[h][:, :], start=True, stop=False),
                        reads=[qT, S_bf[h]], writes=[pbVt(h)], tag=f"s14o_c{c}_h{h}")
                    k.op("pe", lambda e, h=h: e.matmul(
                        pbV(h), gt_bf[h][:, :], d_bf[h][:, :], start=False, stop=True),
                        reads=[gt_bf[h], d_bf[h]], writes=[pbVt(h)])
                for h in range(NH4 if STAGE_CUT >= 15 else 0):
                    k.op("pe", lambda e, h=h: e.matmul(
                        pb5[:, (h % 2) * DV:(h % 2 + 1) * DV] if False else
                        pb5[:, h * CC:(h + 1) * CC] if False else
                        (pb5 if h < 2 else pb4)[:, (h % 2) * DV:(h % 2 + 1) * DV],
                        ktok[h][:, :], d_bf[h][:, :], start=True, stop=True),
                        reads=[ktok[h], d_bf[h]], writes=[pb5 if h < 2 else pb4])
                for h in range(NH4 if STAGE_CUT >= 15 else 0):
                    k.op("dve", lambda e, h=h: e.tensor_tensor(
                        S[h][:, :], S[h][:, :],
                        (pb5 if h < 2 else pb4)[:, (h % 2) * DV:(h % 2 + 1) * DV], ALU.add),
                        reads=[S[h], pb5 if h < 2 else pb4], writes=[S[h]], strict=True,
                        tag=f"s15add_c{c}_h{h}")
                for h in range(NH4 if STAGE_CUT >= 16 else 0):
                    k.op("dve", lambda e, h=h: e.tensor_copy(o_f32[h][:, :], pbV(h)),
                         reads=[pbVt(h)], writes=[o_f32[h]], strict=True)
                    k.op("dve", lambda e, h=h: e.tensor_mul(
                        u_scr[:, :], o_f32[h][:, :], o_f32[h][:, :]),
                        reads=[o_f32[h]], writes=[u_scr], strict=True)
                    k.op("dve", lambda e, h=h: e.reduce_sum(
                        rms[:, h:h + 1], u_scr[:, :], axis=AX.X),
                        reads=[u_scr], writes=[rms], strict=True)
                if STAGE_CUT >= 16:
                    k.op("act", lambda e: e.activation(rms[:, 0:4], rms[:, 0:4], AF.Ln,
                                                       scale=1.0 / 256.0, bias=1e-5),
                         reads=[rms], writes=[rms], strict=True)
                    k.op("act", lambda e: e.activation(rms[:, 0:4], rms[:, 0:4], AF.Exp,
                                                       scale=-0.5),
                         reads=[rms], writes=[rms], strict=True)
                for h in range(NH4 if STAGE_CUT >= 16 else 0):
                    k.op("dve", lambda e, h=h: e.tensor_scalar(
                        on_scr[:, :], o_f32[h][:, :], rms[:, h:h + 1], None, ALU.mult),
                        reads=[o_f32[h], rms], writes=[on_scr], strict=True)
                    k.op("dve", lambda e, h=h, c=c: e.tensor_tensor(
                        otok[:, h * DV:(h + 1) * DV], on_scr[:, :],
                        gsilu[:, c * D + h * DV:c * D + (h + 1) * DV], ALU.mult),
                        reads=[on_scr, gsilu], writes=[otok], strict=True)
                if dbg and c == 0:
                    k.dma("sp", dbg["S0"], S[0][:, :], reads=[S[0]], tag="dbg")
                    k.dma("sp", dbg["otok"], otok[:, :], reads=[otok], tag="dbg")
                    k.dma("sp", dbg["X6"], Xfin[3][:, :], reads=[Xfin[3]], tag="dbg")
                    k.dma("sp", dbg["dbf"], d_bf[3][:, :], reads=[d_bf[3]], tag="dbg")
                    k.dma("sp", dbg["ubf"], u_bf[3][:, :], reads=[u_bf[3]], tag="dbg")
                    k.dma("sp", dbg["nbf"], n_bf[3][:, :], reads=[n_bf[3]], tag="dbg")
                    k.dma("sp", dbg["bcol"], bcol[:, 0:4], reads=[bcol], tag="dbg")
                for fc in range(NCH if STAGE_CUT >= 17 else 0):
                    k.op("pe", lambda e, fc=fc: e.transpose(
                        pbt[:, 384 + fc * 128:384 + (fc + 1) * 128] if False else
                        pbt[:, fc * 128:(fc + 1) * 128],
                        otok[:, fc * 128:(fc + 1) * 128], env.eyeb[:, :]),
                        reads=[otok, env.eyeb], writes=[pbt])
                if STAGE_CUT >= 17:
                    k.op("dve", lambda e: e.tensor_copy(otT[:, :], pbt[:, 0:D]),
                         reads=[pbt], writes=[otT], strict=True)
                for oc in range(NCH if STAGE_CUT >= 17 else 0):
                    for fc in range(NCH):
                        k.op("pe", lambda e, fc=fc, oc=oc: e.matmul(
                            pb5[:, 0:CC], wo[:, fc * D + oc * 128:fc * D + (oc + 1) * 128],
                            otT[:, fc * 128:(fc + 1) * 128],
                            start=(fc == 0), stop=(fc == NCH - 1)),
                            reads=[wo, otT], writes=[pb5])
                    st = stg[0] if False else None
                    st = res[oc % 2]
                    ld = stg[oc % 2] if False else None
                    pass
                    ldt = kimported = None
                    del st, ld, ldt, kimported
                    stl = res[oc % 2]
                    k.dma("sp", stl[:, 0:CC], xin[oc * 128:(oc + 1) * 128, cs],
                          writes=[stl], tag="d_ldx3")
                    k.op("dve", lambda e, stl=stl: e.tensor_tensor(
                        stl[:, 0:CC], stl[:, 0:CC], pb5[:, 0:CC], ALU.add),
                        reads=[stl, pb5], writes=[stl], strict=True)
                    k.dma("sp", xout[oc * 128:(oc + 1) * 128, cs], stl[:, 0:CC],
                          reads=[stl], tag="d_stx")
                if CHUNK_BAR:
                    k.barrier()
    k.barrier()


def host_prep_l0_attn(inputs):
    d = {}
    for nm in ("Wq", "Wk", "Wv", "Wb", "Wg", "Wo"):
        d[f"L0_attn_{nm}"] = np.ascontiguousarray(inputs[f"L0_attn_{nm}"])
    nw = inputs["L0_attn_norm_w"].reshape(2, 128).T
    d["L0_attn_nwcol"] = np.ascontiguousarray(nw).astype(np.float32)
    return d


_L0ATTN_SPECS = {
    "L0_attn_Wq": ([D, 512], F32),
    "L0_attn_Wk": ([D, 512], F32),
    "L0_attn_Wv": ([D, D], F32),
    "L0_attn_Wb": ([D, 4], F32),
    "L0_attn_Wg": ([D, D], F32),
    "L0_attn_Wo": ([D, D], F32),
    "L0_attn_nwcol": ([128, 2], F32),
}


def build_l0_attn_program(debug=False):
    nc = bass.Bass("TRN2", target_bir_lowering=False)
    stack = contextlib.ExitStack()
    dins = {}
    specs = {**_CONST_SPECS, "c_eyef": ([128, 128], F32),
             "c_ones1rb": ([1, 128], BF16),
             "c_umask_s": ([128, 128], BF16), "c_umask_i": ([128, 128], BF16),
             "c_ones65": ([65, 128], BF16),
             **_L0ATTN_SPECS}
    for name, (shape, dt) in specs.items():
        dins[name] = nc.dram_tensor(name, shape, dt, kind="ExternalInput").ap()
    xin = nc.dram_tensor("xin", [D, T], F32, kind="ExternalInput").ap()
    xout = nc.dram_tensor("xout", [D, T], F32, kind="ExternalOutput").ap()
    dbg = None
    if debug:
        dbg = {
            "qT": nc.dram_tensor("dbg_qT", [128, NH4 * T], BF16, kind="ExternalOutput").ap(),
            "kT": nc.dram_tensor("dbg_kT", [128, NH4 * T], BF16, kind="ExternalOutput").ap(),
            "beta": nc.dram_tensor("dbg_beta", [1, NH4 * T], BF16, kind="ExternalOutput").ap(),
            "xn": nc.dram_tensor("dbg_xn", [128, T], BF16, kind="ExternalOutput").ap(),
            "S0": nc.dram_tensor("dbg_S0", [128, DV], F32, kind="ExternalOutput").ap(),
            "otok": nc.dram_tensor("dbg_otok", [128, D], BF16, kind="ExternalOutput").ap(),
            "X6": nc.dram_tensor("dbg_X6", [128, CC], BF16, kind="ExternalOutput").ap(),
            "dbf": nc.dram_tensor("dbg_dbf", [128, DV], BF16, kind="ExternalOutput").ap(),
            "ubf": nc.dram_tensor("dbg_ubf", [128, DV], BF16, kind="ExternalOutput").ap(),
            "nbf": nc.dram_tensor("dbg_nbf", [128, CC], BF16, kind="ExternalOutput").ap(),
            "bcol": nc.dram_tensor("dbg_bcol", [128, 4], F32, kind="ExternalOutput").ap(),
            "wq": nc.dram_tensor("dbg_wq", [128, NCH * 512], BF16, kind="ExternalOutput").ap(),
            "xn7": nc.dram_tensor("dbg_xn7", [128, T], BF16, kind="ExternalOutput").ap(),
        }
    with stack:
        k = Ker(nc, stack)
        env = Env(k, dins)
        k.barrier()
        emit_l0_attn(k, env, dins, xin, xout, dbg)
        global _LAST_K
        _LAST_K = k
        k.lower(verbose=True)
    return nc




def build_fused_program():
    nc = bass.Bass("TRN2", target_bir_lowering=False)
    stack = contextlib.ExitStack()
    dins = {}
    specs = {**_CONST_SPECS, "c_eyef": ([128, 128], F32),
             "c_onesrowb": ([1, TTS], BF16), "c_ones1rb": ([1, 128], BF16),
             "c_umask_s": ([128, 128], BF16), "c_umask_i": ([128, 128], BF16),
             "c_ones65": ([65, 128], BF16),
             **_L0ATTN_SPECS, **_L0FFN_SPECS, **_L1ATTN_SPECS, **_L1FFN_SPECS}
    for name, (shape, dt) in specs.items():
        dins[name] = nc.dram_tensor(name, shape, dt, kind="ExternalInput").ap()
    xin = nc.dram_tensor("xin", [D, T], F32, kind="ExternalInput").ap()
    xout = nc.dram_tensor("xout", [D, T], F32, kind="ExternalOutput").ap()
    s1 = nc.dram_tensor("s1", [D, T], F32).ap()
    s2 = nc.dram_tensor("s2", [D, T], F32).ap()
    s3 = nc.dram_tensor("s3", [D, T], F32).ap()
    w1b_scr = nc.dram_tensor("w1b_scr", [3, D, HID], BF16).ap()
    w2b_scr = nc.dram_tensor("w2b_scr", [3, HID, D], BF16).ap()
    rows_scr = nc.dram_tensor("rows_scr", [2, T], F32).ap()
    wgq_scr = nc.dram_tensor("wgq_scr", [D, OFF], BF16).ap()
    wuq_scr = nc.dram_tensor("wuq_scr", [D, OFF], BF16).ap()
    with stack:
        k = Ker(nc, stack)
        env = Env(k, dins)
        k.barrier()
        emit_l0_attn(k, env, dins, xin, s1)
        emit_l0_ffn(k, env, dins, s1, s2)
        emit_l1_attn(k, env, dins, s2, s3, w1b_scr, w2b_scr, rows_scr)
        emit_l1_ffn(k, env, dins, s3, xout, wgq_scr, wuq_scr)
        k.lower(verbose=True)
    return nc


_PROGRAMS = {}


def kernel(**inputs):
    if "fused" not in _PROGRAMS:
        _PROGRAMS["fused"] = build_fused_program()
    consts = host_consts()
    prep = {}
    prep.update(host_prep_l0_attn(inputs))
    prep.update(host_prep_l0_ffn(inputs))
    prep.update(host_prep_l1_attn(inputs))
    prep.update(host_prep_l1_ffn(inputs))
    base = {**consts, **prep}

    x = inputs["L0_attn_x"]
    n_cores = 8
    in_maps = [{**base, "xin": np.ascontiguousarray(x[i].T).astype(np.float32)}
               for i in range(n_cores)]
    res = run_bass_kernel_spmd(_PROGRAMS["fused"], in_maps, core_ids=list(range(n_cores)))
    out = np.stack([np.asarray(res.results[i]["xout"]).T for i in range(n_cores)], axis=0)
    return out.astype(np.float32)


# version 10
# speedup vs baseline: 1.2366x; 1.0010x over previous
import contextlib
import numpy as np
import ml_dtypes
import concourse.bass as bass
import concourse.mybir as mybir
from concourse.bass_utils import run_bass_kernel_spmd

F32 = mybir.dt.float32
BF16 = mybir.dt.bfloat16
I32 = mybir.dt.int32
AF = mybir.ActivationFunctionType
ALU = mybir.AluOpType
AX = mybir.AxisListType

T = 2048
D = 1024
NCH = D // 128
TTS = 512
NTT = T // TTS

DMA_RR = True
SCOPE_TAG = False
ENGS = ("sp", "pe", "act", "dve", "gp")
_BASS_ENG = {"sp": "sync", "pe": "tensor", "act": "scalar", "dve": "vector", "gp": "gpsimd"}


class Tile:
    __slots__ = ("h", "lw", "readers", "name")

    def __init__(self, h, name=""):
        self.h = h
        self.name = name
        self.lw = None
        self.readers = []

    def __getitem__(self, sl):
        return self.h[sl]


class Op:
    __slots__ = ("eng", "fn", "gidx", "deps", "need_inc", "inc_ord", "is_dma",
                 "phase", "tag", "strict")

    def __init__(self, eng, fn, is_dma, gidx, tag=""):
        self.strict = False
        self.eng = eng
        self.fn = fn
        self.is_dma = is_dma
        self.gidx = gidx
        self.deps = []
        self.need_inc = False
        self.inc_ord = None
        self.phase = None
        self.tag = tag


class _Barrier:
    pass


class Ker:
    def __init__(self, nc, stack: contextlib.ExitStack):
        self.nc = nc
        self.stack = stack
        self.ops = []
        self.dbg_waits = {}
        self.n_barriers = 0
        self._uid = 0
        self._dma_rr = 0

    def sb(self, name, shape, dt, scope=None):
        h = (scope or self.stack).enter_context(
            self.nc.sbuf_tensor(self._u(name), list(shape), dt))
        return Tile(h, name)

    def ps(self, name, shape, dt, scope=None):
        h = (scope or self.stack).enter_context(
            self.nc.psum_tensor(self._u(name), list(shape), dt))
        return Tile(h, name)

    def _u(self, name):
        self._uid += 1
        return f"{name}_{self._uid}"

    def op(self, eng, fn, reads=(), writes=(), tag="", strict=False):
        o = Op(eng, fn, False, len(self.ops), tag)
        o.strict = strict
        self._track(o, reads, writes)
        self.ops.append(o)
        return o

    def dma(self, eng, out_ap, in_ap, reads=(), writes=(), transpose=False, tag=""):
        if eng == "sp" and DMA_RR:
            eng = ("sp", "act")[self._dma_rr % 2]
            self._dma_rr += 1
        if transpose:
            fn = lambda e, o=out_ap, i=in_ap: e.dma_start_transpose(o, i)
        else:
            fn = lambda e, o=out_ap, i=in_ap: e.dma_start(o, i)
        o = Op(eng, fn, True, len(self.ops), tag)
        self._track(o, reads, writes)
        self.ops.append(o)
        return o

    def _track(self, o, reads, writes):
        deps = {}
        for t in reads:
            if t.lw is not None:
                deps[t.lw.gidx] = t.lw
        for t in writes:
            if t.lw is not None:
                deps[t.lw.gidx] = t.lw
            for r in t.readers:
                deps[r.gidx] = r
        for d in deps.values():
            if d is o:
                continue
            if d.eng == o.eng and not d.is_dma and not d.strict:
                continue
            o.deps.append(d)
        for t in reads:
            t.readers.append(o)
        for t in writes:
            t.lw = o
            t.readers = []

    def barrier(self):
        self.ops.append(_Barrier())
        self.n_barriers += 1

    def lower(self, verbose=False):
        nc = self.nc
        phase = 0
        for o in self.ops:
            if isinstance(o, _Barrier):
                phase += 1
            else:
                o.phase = phase
        n_phases = phase + 1
        for o in self.ops:
            if isinstance(o, _Barrier):
                continue
            for d in o.deps:
                if d.phase == o.phase and not d.is_dma:
                    d.need_inc = True
        last_op = {}
        for o in self.ops:
            if isinstance(o, _Barrier):
                continue
            if not o.is_dma:
                last_op[(o.phase, o.eng)] = o
        for o in last_op.values():
            o.need_inc = True
        R = 6
        inc_cnt = {e: 0 for e in ENGS}
        dma_cnt = {e: 0 for e in ENGS}
        for o in self.ops:
            if isinstance(o, _Barrier):
                continue
            if o.is_dma:
                o.inc_ord = dma_cnt[o.eng]
                dma_cnt[o.eng] += 1
            elif o.need_inc:
                inc_cnt[o.eng] += 1
                o.inc_ord = inc_cnt[o.eng]
        esem = {e: self.stack.enter_context(nc.semaphore(f"e_{e}"))
                for e in ENGS if inc_cnt[e] > 0}
        dsem = {e: [self.stack.enter_context(nc.semaphore(f"d_{e}{i}"))
                    for i in range(min(R, dma_cnt[e]))]
                for e in ENGS if dma_cnt[e] > 0}
        bsem = [self.stack.enter_context(nc.semaphore(f"bar{i}"))
                for i in range(self.n_barriers)]
        streams = {e: [] for e in ENGS}
        hw = {}
        run_inc = {e: 0 for e in ENGS}
        run_dma = {e: 0 for e in ENGS}
        bar_idx = 0
        for o in self.ops:
            if isinstance(o, _Barrier):
                bi = bar_idx
                bar_idx += 1
                for e in ENGS:
                    finals = []
                    nd = run_dma[e]
                    if nd > 0:
                        ds = dsem[e]
                        for i_ in range(len(ds)):
                            cnt = (nd - i_ + len(ds) - 1) // len(ds)
                            if cnt > 0:
                                finals.append((ds[i_], 16 * cnt))
                    ni = run_inc[e]
                    es = esem.get(e)
                    bs = bsem[bi]

                    def bfn(eng, finals=finals, bs=bs, ni=ni, es=es):
                        for sem_, val_ in finals:
                            eng.wait_ge(sem_, val_)
                        if ni > 0:
                            eng.wait_ge(es, ni)
                        eng.nop().then_inc(bs, 1)
                        eng.wait_ge(bs, len(ENGS))
                    streams[e].append(bfn)
                continue
            waits = []
            for d in o.deps:
                if d.phase < o.phase:
                    continue
                if d.is_dma:
                    ds = dsem[d.eng]
                    si = d.inc_ord % len(ds)
                    key = (o.eng, d.eng, True, si)
                    val = 16 * (d.inc_ord // len(ds) + 1)
                    sem = ds[si]
                else:
                    key = (o.eng, d.eng, False)
                    val = d.inc_ord
                    sem = esem[d.eng]
                if hw.get(key, 0) < val:
                    hw[key] = val
                    waits.append((sem, val))
            pre_waits = []
            if o.is_dma:
                ds = dsem[o.eng]
                si = o.inc_ord % len(ds)
                cyc = o.inc_ord // len(ds)
                if cyc > 0:
                    pre_waits.append((ds[si], 16 * cyc))
                inc = (ds[si], 16)
                run_dma[o.eng] += 1
            elif o.need_inc:
                inc = (esem[o.eng], 1)
                run_inc[o.eng] = o.inc_ord
            else:
                inc = None

            self.dbg_waits[o.gidx] = (list(waits), o.need_inc, o.inc_ord)

            def ofn(eng, waits=waits, pre_waits=pre_waits, fn=o.fn, inc=inc):
                for sem, val in waits:
                    eng.wait_ge(sem, val)
                for sem, val in pre_waits:
                    eng.wait_ge(sem, val)
                ins = fn(eng)
                if inc is not None:
                    ins.then_inc(inc[0], inc[1])
            ofn.ph = o.phase
            streams[o.eng].append(ofn)
        for e in ENGS:
            nd = run_dma[e]
            if nd > 0:
                ds = dsem[e]
                finals = []
                for i_ in range(len(ds)):
                    cnt = (nd - i_ + len(ds) - 1) // len(ds)
                    if cnt > 0:
                        finals.append((ds[i_], 16 * cnt))

                def ffn(eng, finals=finals):
                    for sem_, val_ in finals:
                        eng.wait_ge(sem_, val_)
                streams[e].append(ffn)
        if verbose:
            tot = {e: sum(1 for o in self.ops
                          if not isinstance(o, _Barrier) and o.eng == e) for e in ENGS}
            print(f"[mk] ops per engine: {tot}, phases={n_phases}, "
                  f"incs={inc_cnt}, dmas={dma_cnt}")
        with nc.Block() as block:
            for e in ENGS:
                lst = streams[e]

                def body(eng, lst=lst):
                    for f in lst:
                        ph = getattr(f, "ph", None)
                        if SCOPE_TAG and ph is not None:
                            with nc.named_scope(f"ph{ph:02d}"):
                                f(eng)
                        else:
                            f(eng)
                getattr(block, _BASS_ENG[e])(body)



CONST_VALS = [1e-5, -0.5, 0.125, 1e-12, -1.0, 2.0, 6.0, 0.5, -0.125, 1.0/256.0]


class Env:

    def __init__(self, k: Ker, dins, scope=None):
        self.k = k
        self.ones128 = k.sb("ones128", [128, 1], F32, scope)
        self.ones1r = k.sb("ones1r", [1, 128], F32, scope)
        self.onesrow = k.sb("onesrow", [1, TTS], F32, scope)
        self.eyeb = k.sb("eyeb", [128, 128], BF16, scope)
        self.fvals = k.sb("fvals", [128, len(CONST_VALS)], F32, scope)
        self.ones1rb = k.sb("ones1rb", [1, 128], BF16, scope)
        k.dma("sp", self.ones128[:, :], dins["c_ones128"], writes=[self.ones128])
        k.dma("sp", self.ones1r[:, :], dins["c_ones1r"], writes=[self.ones1r])
        k.dma("sp", self.onesrow[:, :], dins["c_onesrow"], writes=[self.onesrow])
        k.dma("sp", self.eyeb[:, :], dins["c_eyeb"], writes=[self.eyeb])
        k.dma("sp", self.fvals[:, :], dins["c_fvals"], writes=[self.fvals])
        if "c_ones1rb" in dins:
            k.dma("sp", self.ones1rb[:, :], dins["c_ones1rb"], writes=[self.ones1rb])
        for i, v in enumerate(CONST_VALS):
            k.nc.const_aps.register(F32, float(v), self.fvals.h[:, i:i + 1])


def emit_ln_rows(k, env, pb_a, pb_b, rows, n, eps=1e-5, tag="ln"):
    m, v, r = rows["m"], rows["v"], rows["r"]
    k.op("dve", lambda e: e.tensor_single_scalar(m[0:1, :], pb_a[0:1, :], 1.0 / n, ALU.mult),
         reads=[pb_a], writes=[m], tag=tag + "_m", strict=True)
    k.op("dve", lambda e: e.tensor_single_scalar(v[0:1, :], pb_b[0:1, :], 1.0 / n, ALU.mult),
         reads=[pb_b], writes=[v], tag=tag + "_v", strict=True)
    k.op("dve", lambda e: e.tensor_mul(r[0:1, :], m[0:1, :], m[0:1, :]),
         reads=[m], writes=[r], strict=True)
    k.op("dve", lambda e: e.tensor_tensor(v[0:1, :], v[0:1, :], r[0:1, :], ALU.subtract),
         reads=[v, r], writes=[v], strict=True)
    k.op("act", lambda e: e.activation(r[0:1, :], v[0:1, :], AF.Ln, bias=float(eps)),
         reads=[v], writes=[r], strict=True)
    k.op("act", lambda e: e.activation(r[0:1, :], r[0:1, :], AF.Exp, scale=-0.5),
         reads=[r], writes=[r], strict=True)



def emit_l0_ffn(k: Ker, env, dins, xin, xout):
    with contextlib.ExitStack() as sc:
        H = [k.sb(f"H{c}", [128, T], F32, sc) for c in range(NCH)]
        hb = [k.sb(f"hb{c}", [128, T], BF16, sc) for c in range(NCH)]
        stg = [k.sb(f"stg{i}", [128, TTS], F32, sc) for i in range(2)]
        xsq_scr = [k.sb(f"sqscr{i}", [128, TTS], F32, sc) for i in range(2)]
        ap_scr = xsq_scr
        rows = {n_: k.sb(f"row{n_}", [1, TTS], F32, sc) for n_ in ("m", "v", "r")}
        wT = [k.sb(f"wT{w}", [128, NCH * D], BF16, sc) for w in range(2)]
        wstage = [k.sb(f"wst{i}", [128, D], I32, sc) for i in range(2)]
        wf_scr = k.sb("wf", [128, D], F32, sc)
        scT = k.sb("scT", [64, D], F32, sc)
        rsel = k.sb("rsel", [64, D], F32, sc)
        la_bf = [k.sb(f"labf{w}", [128, 256], BF16, sc) for w in range(2)]
        lb_bf = [k.sb(f"lbbf{w}", [32, D], BF16, sc) for w in range(2)]
        brow = k.sb("brow", [1, 2 * D], F32, sc)
        gam = k.sb("gam", [128, NCH], F32, sc)
        bet = k.sb("bet", [128, NCH], F32, sc)
        y1f = [k.sb(f"y1f{c}", [128, TTS], F32, sc) for c in range(NCH)]
        xg = [k.sb(f"xg{c}", [128, TTS], BF16, sc) for c in range(NCH)]
        u_bf = k.sb("ubf", [32, TTS], BF16, sc)
        pb_a = k.ps("pba", [128, TTS], F32, sc)
        pb_b = k.ps("pbb", [128, TTS], F32, sc)
        pb_y = [k.ps(f"pby{i}", [128, TTS], F32, sc) for i in range(2)]
        pb_u = k.ps("pbu", [32, TTS], F32, sc)
        pb_s = k.ps("pbs", [128, D], F32, sc)

        k.dma("sp", rsel[:, :], dins["c_rsel"], writes=[rsel])

        def ln_stats(src_ap_fn, read_tiles):
            for c in range(NCH):
                k.op("pe", lambda e, c=c: e.matmul(
                    pb_a[0:1, :], env.ones128[:, :], src_ap_fn(c),
                    start=(c == 0), stop=(c == NCH - 1)),
                    reads=[env.ones128, read_tiles[c]], writes=[pb_a])
            for c in range(NCH):
                scr = xsq_scr[c % 2]
                k.op("dve", lambda e, c=c, scr=scr: e.tensor_mul(
                    scr[:, :], src_ap_fn(c), src_ap_fn(c)),
                    reads=[read_tiles[c]], writes=[scr])
                k.op("pe", lambda e, scr=scr, c=c: e.matmul(
                    pb_b[0:1, :], env.ones128[:, :], scr[:, :],
                    start=(c == 0), stop=(c == NCH - 1)),
                    reads=[env.ones128, scr], writes=[pb_b])
            emit_ln_rows(k, env, pb_a, pb_b, rows, D)
            k.op("pe", lambda e: e.matmul(pb_a[:, :], env.ones1r[:, :],
                                          rows["m"][0:1, :], start=True, stop=True),
                 reads=[env.ones1r, rows["m"]], writes=[pb_a])
            k.op("pe", lambda e: e.matmul(pb_b[:, :], env.ones1r[:, :],
                                          rows["r"][0:1, :], start=True, stop=True),
                 reads=[env.ones1r, rows["r"]], writes=[pb_b])

        def preln_tt(tt):
            ts = slice(tt * TTS, (tt + 1) * TTS)
            for c in range(NCH):
                st = stg[c % 2]
                k.dma("sp", st[:, :], xin[c * 128:(c + 1) * 128, ts],
                      writes=[st], tag="ldx")
                k.op("pe", lambda e, c=c, st=st: e.matmul(
                    pb_a[0:1, :], env.ones128[:, :], st[:, :],
                    start=(c == 0), stop=(c == NCH - 1)),
                    reads=[env.ones128, st], writes=[pb_a])
                scr = xsq_scr[c % 2]
                k.op("dve", lambda e, st=st, scr=scr: e.tensor_mul(
                    scr[:, :], st[:, :], st[:, :]),
                    reads=[st], writes=[scr])
                k.op("pe", lambda e, scr=scr, c=c: e.matmul(
                    pb_b[0:1, :], env.ones128[:, :], scr[:, :],
                    start=(c == 0), stop=(c == NCH - 1)),
                    reads=[env.ones128, scr], writes=[pb_b])
            emit_ln_rows(k, env, pb_a, pb_b, rows, D)
            k.op("pe", lambda e: e.matmul(pb_a[:, :], env.ones1r[:, :],
                                          rows["m"][0:1, :], start=True, stop=True),
                 reads=[env.ones1r, rows["m"]], writes=[pb_a])
            k.op("pe", lambda e: e.matmul(pb_b[:, :], env.ones1r[:, :],
                                          rows["r"][0:1, :], start=True, stop=True),
                 reads=[env.ones1r, rows["r"]], writes=[pb_b])
            for c in range(NCH):
                st = stg[c % 2]
                k.dma("sp", st[:, :], xin[c * 128:(c + 1) * 128, ts],
                      writes=[st], tag="ldx2")
                scr = ap_scr[c % 2]
                k.op("dve", lambda e, st=st, scr=scr: e.tensor_tensor(
                    scr[:, :], st[:, :], pb_a[:, :], ALU.subtract),
                    reads=[st, pb_a], writes=[scr])
                k.op("dve", lambda e, c=c, ts=ts, scr=scr: e.tensor_tensor(
                    H[c][:, ts], scr[:, :], pb_b[:, :], ALU.mult),
                    reads=[scr, pb_b], writes=[H[c]])
                k.op("dve", lambda e, c=c, ts=ts: e.tensor_copy(
                    hb[c][:, ts], H[c][:, ts]),
                    reads=[H[c]], writes=[hb[c]])

        for tt in range(NTT):
            preln_tt(tt)

        def prep_weights(j, w):
            k.dma("sp", scT[:, :], dins["L0_ffn_scT"][j], writes=[scT], tag="ldsc")
            for dc in range(NCH):
                st = wstage[dc % 2]
                k.dma("sp", st[:, :], dins["L0_ffn_qwT"][j, dc * 128:(dc + 1) * 128, :],
                      writes=[st], tag="ldw")
                for half in range(2):
                    k.op("pe", lambda e, dc=dc, half=half: e.matmul(
                        pb_s[:, half * TTS:(half + 1) * TTS],
                        rsel[:, dc * 128:(dc + 1) * 128],
                        scT[:, half * TTS:(half + 1) * TTS],
                        start=True, stop=True),
                        reads=[rsel, scT], writes=[pb_s])
                k.op("dve", lambda e, st=st: e.tensor_scalar(
                    wf_scr[:, :], st[:, :], 1.0 / 7.5, -1.0, ALU.mult, ALU.add),
                    reads=[st], writes=[wf_scr])
                k.op("dve", lambda e, dc=dc, w=w: e.tensor_tensor(
                    wT[w][:, dc * D:(dc + 1) * D], wf_scr[:, :], pb_s[:, :], ALU.mult),
                    reads=[wf_scr, pb_s], writes=[wT[w]])

        def prep_lora_bias(j, w):
            k.dma("sp", wf_scr[:, 0:256].rearrange("p (c r) -> p c r", r=32),
                  dins["L0_ffn_laT"][j].rearrange("(c p) r -> p c r", p=128),
                  writes=[wf_scr], tag="ldla")
            k.op("dve", lambda e, w=w: e.tensor_copy(la_bf[w][:, :], wf_scr[:, 0:256]),
                 reads=[wf_scr], writes=[la_bf[w]])
            k.dma("sp", wf_scr[0:32, :], dins["L0_ffn_lbT"][j],
                  writes=[wf_scr], tag="ldlb")
            k.op("dve", lambda e, w=w: e.tensor_copy(
                lb_bf[w][0:32, :], wf_scr[0:32, :]),
                reads=[wf_scr], writes=[lb_bf[w]])
            k.dma("sp", brow[0:1, w * D:(w + 1) * D], dins["L0_ffn_biases"][j:j + 1, :],
                  writes=[brow], tag="ldbias")

        def qlora_mm(src_ap_fn, src_tiles, w, consumer):
            for dc in range(NCH):
                k.op("pe", lambda e, dc=dc, w=w: e.matmul(
                    pb_u[0:32, :], la_bf[w][:, dc * 32:(dc + 1) * 32], src_ap_fn(dc),
                    start=(dc == 0), stop=(dc == NCH - 1)),
                    reads=[la_bf[w], src_tiles[dc]], writes=[pb_u])
            k.op("dve", lambda e: e.tensor_copy(u_bf[0:32, :], pb_u[0:32, :]),
                 reads=[pb_u], writes=[u_bf])
            for oc in range(NCH):
                pb = pb_y[oc % 2]
                for dc in range(NCH):
                    k.op("pe", lambda e, dc=dc, oc=oc, pb=pb, w=w: e.matmul(
                        pb[:, :], wT[w][:, dc * D + oc * 128: dc * D + (oc + 1) * 128],
                        src_ap_fn(dc), start=(dc == 0), stop=False),
                        reads=[wT[w], src_tiles[dc]], writes=[pb])
                k.op("pe", lambda e, oc=oc, pb=pb, w=w: e.matmul(
                    pb[:, :], lb_bf[w][0:32, oc * 128:(oc + 1) * 128],
                    u_bf[0:32, :], start=False, stop=False),
                    reads=[lb_bf[w], u_bf], writes=[pb])
                k.op("pe", lambda e, oc=oc, pb=pb, w=w: e.matmul(
                    pb[:, :], brow[0:1, w * D + oc * 128:w * D + (oc + 1) * 128],
                    env.onesrow[0:1, :], start=False, stop=True),
                    reads=[brow, env.onesrow], writes=[pb])
                consumer(oc, pb)

        for i in range(3):
            prep_weights(2 * i, 0)
            prep_lora_bias(2 * i, 0)
            prep_weights(2 * i + 1, 1)
            prep_lora_bias(2 * i + 1, 1)
            k.dma("sp", gam[:, :], dins["L0_ffn_gam"][i], writes=[gam], tag="ldgam")
            k.dma("sp", bet[:, :], dins["L0_ffn_bet"][i], writes=[bet], tag="ldbet")
            for tt in range(NTT):
                ts = slice(tt * TTS, (tt + 1) * TTS)

                def eat_y1(oc, pb):
                    k.op("dve", lambda e, oc=oc, pb=pb: e.tensor_copy(
                        y1f[oc][:, :], pb[:, :]), reads=[pb], writes=[y1f[oc]])
                qlora_mm(lambda dc, ts=ts: hb[dc][:, ts], hb, 0, eat_y1)
                ln_stats(lambda c: y1f[c][:, :], y1f)
                for c in range(NCH):
                    scr = ap_scr[c % 2]
                    k.op("dve", lambda e, c=c, scr=scr: e.tensor_tensor(
                        scr[:, :], y1f[c][:, :], pb_a[:, :], ALU.subtract),
                        reads=[y1f[c], pb_a], writes=[scr])
                    k.op("dve", lambda e, scr=scr: e.tensor_tensor(
                        scr[:, :], scr[:, :], pb_b[:, :], ALU.mult),
                        reads=[scr, pb_b], writes=[scr])
                    k.op("dve", lambda e, c=c, scr=scr: e.tensor_scalar(
                        scr[:, :], scr[:, :], gam[:, c:c + 1], bet[:, c:c + 1],
                        ALU.mult, ALU.add),
                        reads=[scr, gam, bet], writes=[scr])
                    k.op("act", lambda e, c=c, scr=scr: e.activation(
                        xg[c][:, :], scr[:, :], AF.Gelu),
                        reads=[scr], writes=[xg[c]])

                def eat_y2(oc, pb, ts=ts):
                    k.op("dve", lambda e, oc=oc, pb=pb, ts=ts: e.tensor_tensor(
                        H[oc][:, ts], H[oc][:, ts], pb[:, :], ALU.add),
                        reads=[H[oc], pb], writes=[H[oc]])
                    k.op("dve", lambda e, oc=oc, ts=ts: e.tensor_copy(
                        hb[oc][:, ts], H[oc][:, ts]),
                        reads=[H[oc]], writes=[hb[oc]])
                qlora_mm(lambda dc: xg[dc][:, :], xg, 1, eat_y2)

        for tt in range(NTT):
            ts = slice(tt * TTS, (tt + 1) * TTS)
            for c in range(NCH):
                st = stg[c % 2]
                k.dma("sp", st[:, :], xin[c * 128:(c + 1) * 128, ts],
                      writes=[st], tag="ldx3")
                scr = ap_scr[c % 2]
                k.op("dve", lambda e, c=c, ts=ts, scr=scr, st=st: e.tensor_tensor(
                    scr[:, :], st[:, :], H[c][:, ts], ALU.add),
                    reads=[st, H[c]], writes=[scr])
                k.dma("sp", xout[c * 128:(c + 1) * 128, ts], scr[:, :],
                      reads=[scr], tag="stx")
    k.barrier()



def host_consts():
    c = {}
    c["c_ones128"] = np.ones((128, 1), np.float32)
    c["c_ones1r"] = np.ones((1, 128), np.float32)
    c["c_onesrow"] = np.ones((1, TTS), np.float32)
    c["c_eyeb"] = np.eye(128, dtype=ml_dtypes.bfloat16)
    rsel = np.zeros((64, D), np.float32)
    for p in range(D):
        rsel[(p % 128) // 16 + 0, p] = 0.0
    rsel[:] = 0.0
    for dc in range(8):
        for p in range(128):
            rsel[dc * 8 + p // 16, dc * 128 + p] = 1.0
    c["c_rsel"] = rsel
    c["c_fvals"] = np.tile(np.array(CONST_VALS, np.float32)[None, :], (128, 1))
    c["c_onesrowb"] = np.ones((1, TTS), ml_dtypes.bfloat16)
    c["c_ones1rb"] = np.ones((1, 128), ml_dtypes.bfloat16)
    c["c_eyef"] = np.eye(128, dtype=np.float32)
    c["c_ones65"] = np.ones((65, 128), ml_dtypes.bfloat16)
    c["c_umask_s"] = np.triu(np.ones((128, 128)), 1).astype(ml_dtypes.bfloat16)
    c["c_umask_i"] = np.triu(np.ones((128, 128)), 0).astype(ml_dtypes.bfloat16)
    return c


def host_prep_l0_ffn(inputs):
    d = {}
    d["L0_ffn_qwT"] = np.ascontiguousarray(
        inputs["L0_ffn_qweights"].transpose(0, 2, 1)).astype(np.int32)
    d["L0_ffn_scT"] = np.ascontiguousarray(
        inputs["L0_ffn_scales"].transpose(0, 2, 1)).astype(np.float32)
    d["L0_ffn_laT"] = np.ascontiguousarray(
        inputs["L0_ffn_lora_a"].transpose(0, 2, 1)).astype(np.float32)
    d["L0_ffn_lbT"] = np.ascontiguousarray(
        inputs["L0_ffn_lora_b"].transpose(0, 2, 1)).astype(np.float32)
    d["L0_ffn_biases"] = np.ascontiguousarray(inputs["L0_ffn_biases"])
    d["L0_ffn_gam"] = np.ascontiguousarray(
        inputs["L0_ffn_ln_gamma"].reshape(3, 8, 128).transpose(0, 2, 1))
    d["L0_ffn_bet"] = np.ascontiguousarray(
        inputs["L0_ffn_ln_beta"].reshape(3, 8, 128).transpose(0, 2, 1))
    return d


_L0FFN_SPECS = {
    "L0_ffn_qwT": ([6, D, D], I32),
    "L0_ffn_scT": ([6, 64, D], F32),
    "L0_ffn_laT": ([6, D, 32], F32),
    "L0_ffn_lbT": ([6, 32, D], F32),
    "L0_ffn_biases": ([6, D], F32),
    "L0_ffn_gam": ([3, 128, 8], F32),
    "L0_ffn_bet": ([3, 128, 8], F32),
}

_CONST_SPECS = {
    "c_ones128": ([128, 1], F32),
    "c_ones1r": ([1, 128], F32),
    "c_onesrow": ([1, TTS], F32),
    "c_eyeb": ([128, 128], BF16),
    "c_rsel": ([64, D], F32),
    "c_fvals": ([128, len(CONST_VALS)], F32),
}


def build_l0_ffn_program():
    nc = bass.Bass("TRN2", target_bir_lowering=False)
    stack = contextlib.ExitStack()
    dins = {}
    for name, (shape, dt) in {**_CONST_SPECS, **_L0FFN_SPECS}.items():
        dins[name] = nc.dram_tensor(name, shape, dt, kind="ExternalInput").ap()
    xin = nc.dram_tensor("xin", [D, T], F32, kind="ExternalInput").ap()
    xout = nc.dram_tensor("xout", [D, T], F32, kind="ExternalOutput").ap()
    with stack:
        k = Ker(nc, stack)
        env = Env(k, dins)
        k.barrier()
        emit_l0_ffn(k, env, dins, xin, xout)
        k.lower(verbose=True)
    return nc

def emit_fp4_qdq(k: Ker, env, src, out_bf, scr, N, src_sl=None, out_sl=None, eng2="dve", boff=0):
    nb = N // 16
    sa = src[:, 0:N] if src_sl is None else src_sl
    oa = out_bf[:, 0:N] if out_sl is None else out_sl
    bm, s, rcp = scr["bm"], scr["s"], scr["rcp"]
    absx, acc1, acc2, sb16 = scr["absx"], scr["acc1"], scr["acc2"], scr["sb16"]
    b0, b1 = boff, boff + nb
    k.op("dve", lambda e: e.reduce_max(
        bm[:, b0:b1].unsqueeze(2), sa.rearrange("p (a b) -> p a b", b=16),
        axis=AX.X, apply_absolute_value=True),
        reads=[src], writes=[bm], tag="q_bm", strict=True)
    k.op("dve", lambda e: e.tensor_scalar(
        s[:, b0:b1], bm[:, b0:b1], 1.0 / 6.0, 1e-12, ALU.mult, ALU.max),
        reads=[bm], writes=[s], tag="q_s", strict=True)
    for i0 in range(b0, b1, 128):
        i1 = min(b1, i0 + 128)
        k.op("dve", lambda e, i0=i0, i1=i1: e.reciprocal(rcp[:, i0:i1], s[:, i0:i1]),
             reads=[s], writes=[rcp], tag="q_rcp", strict=True)
    k.op("act", lambda e: e.activation(absx[:, 0:N], sa, AF.Abs),
         reads=[src], writes=[absx], tag="q_abs", strict=True)
    rep = lambda t: t[:, b0:b1].unsqueeze(2).broadcast_to([128, nb, 16])
    r3 = lambda ap: ap.rearrange("p (a b) -> p a b", b=16)
    k.op(eng2, lambda e: e.tensor_tensor(
        r3(absx[:, 0:N]), r3(absx[:, 0:N]), rep(rcp), ALU.mult),
        reads=[absx, rcp], writes=[absx], tag="q_div", strict=True)
    a = absx
    k.op("dve", lambda e: e.tensor_single_scalar(acc1[:, 0:N], a[:, 0:N], 0.25, ALU.is_gt),
         reads=[a], writes=[acc1], strict=True)
    for thr in (0.75, 1.25, 1.75):
        k.op("dve", lambda e, thr=thr: e.scalar_tensor_tensor(
            acc1[:, 0:N], a[:, 0:N], thr, acc1[:, 0:N], ALU.is_gt, ALU.add),
            reads=[a, acc1], writes=[acc1], strict=True)
    k.op("dve", lambda e: e.tensor_scalar(acc2[:, 0:N], a[:, 0:N], 5.0, 2.0, ALU.is_gt, ALU.mult),
         reads=[a], writes=[acc2], strict=True)
    for thr in (2.5, 3.5):
        k.op("dve", lambda e, thr=thr: e.scalar_tensor_tensor(
            acc2[:, 0:N], a[:, 0:N], thr, acc2[:, 0:N], ALU.is_gt, ALU.add),
            reads=[a, acc2], writes=[acc2], strict=True)
    k.op("dve", lambda e: e.scalar_tensor_tensor(
        acc1[:, 0:N], acc1[:, 0:N], 0.5, acc2[:, 0:N], ALU.mult, ALU.add),
        reads=[acc1, acc2], writes=[acc1], tag="q_comb", strict=True)
    k.op(eng2, lambda e: e.tensor_tensor(
        r3(oa), r3(acc1[:, 0:N]), rep(s), ALU.mult),
        reads=[acc1, s], writes=[out_bf], tag="q_deq", strict=True)
    I16 = mybir.dt.int16
    k.op("dve", lambda e: e.tensor_single_scalar(
        sb16[:, 0:N], sa.bitcast(I16)[:, 1:2 * N:2], -32768, ALU.bitwise_and),
        reads=[src], writes=[sb16], tag="q_sgn", strict=True)
    k.op("dve", lambda e: e.tensor_tensor(
        oa.bitcast(I16), oa.bitcast(I16), sb16[:, 0:N], ALU.bitwise_or),
        reads=[out_bf, sb16], writes=[out_bf], tag="q_or", strict=True)



OFF = 2816
NOC = OFF // 128
NOT = 6


def emit_l1_ffn(k: Ker, env, dins, xin, xout, wgq_scr, wuq_scr, dbg=None):
    with contextlib.ExitStack() as sc:
        stg = [k.sb(f"fstg{i}", [128, TTS], F32, sc) for i in range(2)]
        xsq_scr = [k.sb(f"fsq{i}", [128, TTS], F32, sc) for i in range(2)]
        rows = {n_: k.sb(f"frow{n_}", [1, TTS], F32, sc) for n_ in ("m", "v", "r")}
        mrcol = k.sb("mrcol", [128, 32], F32, sc)
        eyef = k.sb("eyef", [128, 128], F32, sc)
        k.dma("sp", eyef[:, :], dins["c_eyef"], writes=[eyef])
        scr = {
            "bm": k.sb("qbm", [128, 176], F32, sc),
            "s": k.sb("qs", [128, 176], F32, sc),
            "rcp": k.sb("qrcp", [128, 176], F32, sc),
            "absx": k.sb("qabs", [128, D], F32, sc),
            "acc1": k.sb("qa1", [128, D], F32, sc),
            "acc2": k.sb("qa2", [128, D], F32, sc),
            "sb16": k.sb("qsb", [128, D], mybir.dt.int16, sc),
        }
        wstage = [k.sb("fwst0", [128, OFF], F32, sc), k.sb("fwst1", [128, D], F32, sc)]
        wq_nat = k.sb("wqnat", [128, OFF], BF16, sc)
        trst = k.sb("trst", [128, 1024], BF16, sc)
        wdqT = k.sb("wdqT", [128, NOC * 1024], BF16, sc)
        xqT = k.sb("xqT", [128, NCH * TTS], BF16, sc)
        hqT = k.sb("hqT", [128, NOC * TTS], BF16, sc)
        wgt = [k.sb(f"wgt{i}", [128, NCH * TTS], BF16, sc) for i in range(2)]
        wut = [k.sb(f"wut{i}", [128, NCH * TTS], BF16, sc) for i in range(2)]
        x_tok = k.sb("x_tok", [128, D], F32, sc)
        xq_tok = k.sb("xq_tok", [128, D], BF16, sc)
        h_f32 = k.sb("h_f32", [128, TTS], F32, sc)
        sg_f32 = k.sb("sg_f32", [128, TTS], F32, sc)
        hq_bf = k.sb("hq_bf", [128, TTS], BF16, sc)
        pb_a = k.ps("fpba", [128, TTS], F32, sc)
        pb_b = k.ps("fpbb", [128, TTS], F32, sc)
        pb_g = k.ps("fpbg", [128, TTS], F32, sc)
        pb_u2 = k.ps("fpbu", [128, TTS], F32, sc)
        pb_t2 = k.ps("fpbt2", [128, 1024], F32, sc)
        pb_tr = k.ps("fpbtr", [128, 1024], BF16, sc)
        pb_y = [pb_g, pb_u2]

        for tt in range(NTT):
            ts = slice(tt * TTS, (tt + 1) * TTS)
            for c in range(NCH):
                st = stg[c % 2]
                k.dma("sp", st[:, :], xin[c * 128:(c + 1) * 128, ts],
                      writes=[st], tag="f_ldx")
                k.op("pe", lambda e, c=c, st=st: e.matmul(
                    pb_a[0:1, :], env.ones128[:, :], st[:, :],
                    start=(c == 0), stop=(c == NCH - 1)),
                    reads=[env.ones128, st], writes=[pb_a])
                sq = xsq_scr[c % 2]
                k.op("dve", lambda e, st=st, sq=sq: e.tensor_mul(
                    sq[:, :], st[:, :], st[:, :]),
                    reads=[st], writes=[sq])
                k.op("pe", lambda e, sq=sq, c=c: e.matmul(
                    pb_b[0:1, :], env.ones128[:, :], sq[:, :],
                    start=(c == 0), stop=(c == NCH - 1)),
                    reads=[env.ones128, sq], writes=[pb_b])
            emit_ln_rows(k, env, pb_a, pb_b, rows, D)
            for i in range(4):
                ci = tt * 4 + i
                k.op("pe", lambda e, i=i, ci=ci: e.transpose(
                    pb_t2[:, ci:ci + 1], rows["m"][0:1, i * 128:(i + 1) * 128], eyef[0:1, 0:1]),
                    reads=[rows["m"], eyef], writes=[pb_t2])
                k.op("pe", lambda e, i=i, ci=ci: e.transpose(
                    pb_t2[:, 16 + ci:16 + ci + 1], rows["r"][0:1, i * 128:(i + 1) * 128],
                    eyef[0:1, 0:1]),
                    reads=[rows["r"], eyef], writes=[pb_t2])
        k.op("dve", lambda e: e.tensor_copy(mrcol[:, 0:32], pb_t2[:, 0:32]),
             reads=[pb_t2], writes=[mrcol])

        def quant_gateup(dsrc, dscr):
            for ocn in range(NOC):
                st = wstage[ocn % 2]
                k.dma("sp", st[:, 0:D], dsrc[ocn * 128:(ocn + 1) * 128, :],
                      writes=[st], tag="f_ldw")
                emit_fp4_qdq(k, env, st, wq_nat, scr, D,
                             src_sl=st[:, 0:D], out_sl=wq_nat[:, 0:D], eng2="gp")
                for c in range(NCH):
                    k.op("pe", lambda e, c=c: e.transpose(
                        pb_tr[:, c * 128:(c + 1) * 128], wq_nat[:, c * 128:(c + 1) * 128],
                        env.eyeb[:, :]),
                        reads=[wq_nat, env.eyeb], writes=[pb_tr])
                k.op("act", lambda e: e.copy(trst[:, :], pb_tr[:, 0:1024]),
                     reads=[pb_tr], writes=[trst])
                k.dma("sp",
                      dscr.rearrange("(c p) o -> p c o", p=128)[:, :, ocn * 128:(ocn + 1) * 128],
                      trst[:, :].rearrange("p (c o) -> p c o", o=128),
                      reads=[trst], tag="f_stw")

        quant_gateup(dins["L1_ffn_w_gate"], wgq_scr)
        quant_gateup(dins["L1_ffn_w_up"], wuq_scr)

        for oc in range(NCH):
            st = wstage[0]
            k.dma("sp", st[:, :], dins["L1_ffn_w_down"][oc * 128:(oc + 1) * 128, :],
                  writes=[st], tag="f_ldwd")
            for o0 in range(0, OFF, D):
                n_ = min(D, OFF - o0)
                emit_fp4_qdq(k, env, st, wq_nat, scr, n_,
                             src_sl=st[:, o0:o0 + n_], out_sl=wq_nat[:, o0:o0 + n_],
                             eng2="gp")
            for b0 in range(0, NOC, 8):
                b1 = min(NOC, b0 + 8)
                for kc in range(b0, b1):
                    k.op("pe", lambda e, kc=kc, b0=b0: e.transpose(
                        pb_tr[:, (kc - b0) * 128:(kc - b0 + 1) * 128],
                        wq_nat[:, kc * 128:(kc + 1) * 128], env.eyeb[:, :]),
                        reads=[wq_nat, env.eyeb], writes=[pb_tr])
                n = b1 - b0
                k.op("act", lambda e, b0=b0, n=n, oc=oc: e.copy(
                    wdqT[:, :].rearrange("p (kc o) -> p kc o", o=1024)[
                        :, b0:b0 + n, oc * 128:(oc + 1) * 128],
                    pb_tr[:, 0:n * 128].rearrange("p (kc o) -> p kc o", o=128)),
                    reads=[pb_tr], writes=[wdqT])

        if dbg and "wdqT" in dbg:
            k.dma("sp", dbg["wdqT"], wdqT[:, :], reads=[wdqT], tag="dbg")
        for tt in range(NTT):
            ts = slice(tt * TTS, (tt + 1) * TTS)
            for tsub in range(4):
                t0 = tt * TTS + tsub * 128
                for c in range(NCH):
                    st = stg[c % 2]
                    k.dma("sp", st[:, 0:128], xin[c * 128:(c + 1) * 128, t0:t0 + 128],
                          writes=[st], tag="f_ldx2")
                    k.op("pe", lambda e, c=c, st=st: e.transpose(
                        pb_t2[:, c * 128:(c + 1) * 128], st[:, 0:128], eyef[:, :]),
                        reads=[st, eyef], writes=[pb_t2])
                k.op("act", lambda e: e.copy(x_tok[:, :], pb_t2[:, 0:D]),
                     reads=[pb_t2], writes=[x_tok])
                ci = tt * 4 + tsub
                k.op("dve", lambda e, ci=ci: e.tensor_scalar(
                    x_tok[:, :], x_tok[:, :], mrcol[:, ci:ci + 1], mrcol[:, 16 + ci:16 + ci + 1],
                    ALU.subtract, ALU.mult),
                    reads=[x_tok, mrcol], writes=[x_tok])
                emit_fp4_qdq(k, env, x_tok, xq_tok, scr, D, eng2="gp")
                for c in range(NCH):
                    k.op("pe", lambda e, c=c: e.transpose(
                        pb_tr[:, c * 128:(c + 1) * 128], xq_tok[:, c * 128:(c + 1) * 128],
                        env.eyeb[:, :]),
                        reads=[xq_tok, env.eyeb], writes=[pb_tr])
                k.op("act", lambda e, tsub=tsub: e.copy(
                    xqT[:, :].rearrange("p (c t) -> p c t", t=TTS)[:, :, tsub * 128:(tsub + 1) * 128],
                    pb_tr[:, 0:1024].rearrange("p (c t) -> p c t", t=128)),
                    reads=[pb_tr], writes=[xqT])
            if dbg and tt == 0 and "xqT" in dbg:
                k.dma("sp", dbg["xqT"], xqT[:, :], reads=[xqT], tag="dbg")
            for ot in range(NOT):
                osz = 512 if ot < 5 else 256
                kcs = list(range(ot * 4, ot * 4 + osz // 128))
                wg = wgt[ot % 2]
                wu = wut[ot % 2]
                k.dma("sp",
                      wg[:, 0:NCH * osz].rearrange("p (c o) -> p c o", o=osz),
                      wgq_scr.rearrange("(c p) o -> p c o", p=128)[:, :, ot * 512:ot * 512 + osz],
                      writes=[wg], tag="f_ldgw")
                k.dma("sp",
                      wu[:, 0:NCH * osz].rearrange("p (c o) -> p c o", o=osz),
                      wuq_scr.rearrange("(c p) o -> p c o", p=128)[:, :, ot * 512:ot * 512 + osz],
                      writes=[wu], tag="f_lduw")
                for tsub in range(4):
                    tso = tsub * 128
                    for c in range(NCH):
                        k.op("pe", lambda e, c=c, tso=tso, wg=wg, osz=osz: e.matmul(
                            pb_g[:, 0:osz], xqT[:, c * TTS + tso:c * TTS + tso + 128],
                            wg[:, c * osz:(c + 1) * osz],
                            start=(c == 0), stop=(c == NCH - 1)),
                            reads=[xqT, wg], writes=[pb_g])
                    for c in range(NCH):
                        k.op("pe", lambda e, c=c, tso=tso, wu=wu, osz=osz: e.matmul(
                            pb_u2[:, 0:osz], xqT[:, c * TTS + tso:c * TTS + tso + 128],
                            wu[:, c * osz:(c + 1) * osz],
                            start=(c == 0), stop=(c == NCH - 1)),
                            reads=[xqT, wu], writes=[pb_u2])
                    k.op("act", lambda e, osz=osz: e.activation(
                        sg_f32[:, 0:osz], pb_g[:, 0:osz], AF.Sigmoid),
                        reads=[pb_g], writes=[sg_f32])
                    k.op("dve", lambda e, osz=osz: e.tensor_tensor(
                        sg_f32[:, 0:osz], sg_f32[:, 0:osz], pb_g[:, 0:osz], ALU.mult),
                        reads=[sg_f32, pb_g], writes=[sg_f32])
                    k.op("dve", lambda e, osz=osz: e.tensor_tensor(
                        h_f32[:, 0:osz], sg_f32[:, 0:osz], pb_u2[:, 0:osz], ALU.mult),
                        reads=[sg_f32, pb_u2], writes=[h_f32])
                    emit_fp4_qdq(k, env, h_f32, hq_bf, scr, osz,
                                 src_sl=h_f32[:, 0:osz], out_sl=hq_bf[:, 0:osz], eng2="gp")
                    for i, kc in enumerate(kcs):
                        k.op("pe", lambda e, i=i: e.transpose(
                            pb_tr[:, i * 128:(i + 1) * 128], hq_bf[:, i * 128:(i + 1) * 128],
                            env.eyeb[:, :]),
                            reads=[hq_bf, env.eyeb], writes=[pb_tr])
                    nk = len(kcs)
                    k.op("act", lambda e, nk=nk, kcs=kcs, tso=tso: e.copy(
                        hqT[:, :].rearrange("p (kc t) -> p kc t", t=TTS)[
                            :, kcs[0]:kcs[0] + nk, tso:tso + 128],
                        pb_tr[:, 0:nk * 128].rearrange("p (kc t) -> p kc t", t=128)),
                        reads=[pb_tr], writes=[hqT])
            if dbg and tt == 0 and "hqT" in dbg:
                k.dma("sp", dbg["hqT"], hqT[:, :], reads=[hqT], tag="dbg")
            for oc in range(NCH):
                pb = pb_y[oc % 2]
                for kc in range(NOC):
                    k.op("pe", lambda e, kc=kc, oc=oc, pb=pb: e.matmul(
                        pb[:, :], wdqT[:, kc * 1024 + oc * 128:kc * 1024 + (oc + 1) * 128],
                        hqT[:, kc * TTS:(kc + 1) * TTS],
                        start=(kc == 0), stop=(kc == NOC - 1)),
                        reads=[wdqT, hqT], writes=[pb])
                st = stg[oc % 2]
                k.dma("sp", st[:, :], xin[oc * 128:(oc + 1) * 128, ts],
                      writes=[st], tag="f_ldx3")
                sq = xsq_scr[oc % 2]
                k.op("dve", lambda e, pb=pb, st=st, sq=sq: e.tensor_tensor(
                    sq[:, :], st[:, :], pb[:, :], ALU.add),
                    reads=[st, pb], writes=[sq])
                k.dma("sp", xout[oc * 128:(oc + 1) * 128, ts], sq[:, :],
                      reads=[sq], tag="f_stx")
    k.barrier()


def host_prep_l1_ffn(inputs):
    return {
        "L1_ffn_w_gate": np.ascontiguousarray(inputs["L1_ffn_w_gate"]),
        "L1_ffn_w_up": np.ascontiguousarray(inputs["L1_ffn_w_up"]),
        "L1_ffn_w_down": np.ascontiguousarray(inputs["L1_ffn_w_down"]),
    }


_L1FFN_SPECS = {
    "L1_ffn_w_gate": ([OFF, D], F32),
    "L1_ffn_w_up": ([OFF, D], F32),
    "L1_ffn_w_down": ([D, OFF], F32),
}


def build_l1_ffn_program(debug=False):
    nc = bass.Bass("TRN2", target_bir_lowering=False)
    stack = contextlib.ExitStack()
    dins = {}
    specs = {**_CONST_SPECS, "c_eyef": ([128, 128], F32), **_L1FFN_SPECS}
    for name, (shape, dt) in specs.items():
        dins[name] = nc.dram_tensor(name, shape, dt, kind="ExternalInput").ap()
    xin = nc.dram_tensor("xin", [D, T], F32, kind="ExternalInput").ap()
    xout = nc.dram_tensor("xout", [D, T], F32, kind="ExternalOutput").ap()
    kind = dict(kind="ExternalOutput") if debug else {}
    wgq_scr = nc.dram_tensor("wgq_scr", [D, OFF], BF16, **kind).ap()
    wuq_scr = nc.dram_tensor("wuq_scr", [D, OFF], BF16, **kind).ap()
    dbg = None
    if debug:
        dbg = {
            "wdqT": nc.dram_tensor("dbg_wdqT", [128, NOC * 1024], BF16, kind="ExternalOutput").ap(),
            "xqT": nc.dram_tensor("dbg_xqT", [128, NCH * TTS], BF16, kind="ExternalOutput").ap(),
            "hqT": nc.dram_tensor("dbg_hqT", [128, NOC * TTS], BF16, kind="ExternalOutput").ap(),
        }
    with stack:
        k = Ker(nc, stack)
        env = Env(k, dins)
        k.barrier()
        emit_l1_ffn(k, env, dins, xin, xout, wgq_scr, wuq_scr, dbg)
        k.lower(verbose=True)
    return nc

HID = 4096
NHC = HID // 128
NHEAD = 16
HD = 64


def emit_l1_attn(k: Ker, env, dins, xin, xout, w1b_scr, w2b_scr, rows_scr):
    names = ("q", "k", "v")
    with contextlib.ExitStack() as sc0:
        kT = k.sb("l1kT", [128, NCH * T], BF16, sc0)
        qT = k.sb("l1qT", [128, NCH * T], BF16, sc0)
        vaug = [k.sb(f"l1va{i}", [128, NHEAD * 65], BF16, sc0) for i in range(T // 128)]
        eyef = k.sb("l1eyef", [128, 128], F32, sc0)
        onesrb = k.sb("onesrb", [1, TTS], BF16, sc0)
        k.dma("sp", eyef[:, :], dins["c_eyef"], writes=[eyef])
        k.dma("sp", onesrb[:, :], dins["c_onesrowb"], writes=[onesrb])

        with contextlib.ExitStack() as sc:
            stg = [k.sb(f"astg{i}", [128, TTS], F32, sc) for i in range(2)]
            sq = [k.sb(f"asq{i}", [128, TTS], F32, sc) for i in range(2)]
            rows = {n_: k.sb(f"arow{n_}", [1, TTS], F32, sc) for n_ in ("m", "v", "r")}
            pb_a = k.ps("apba", [128, TTS], F32, sc)
            pb_b = k.ps("apbb", [128, TTS], F32, sc)
            for tt in range(NTT):
                ts = slice(tt * TTS, (tt + 1) * TTS)
                for c in range(NCH):
                    st = stg[c % 2]
                    k.dma("sp", st[:, :], xin[c * 128:(c + 1) * 128, ts],
                          writes=[st], tag="a_ldx")
                    k.op("pe", lambda e, st=st, c=c: e.matmul(
                        pb_a[0:1, :], env.ones128[:, :], st[:, :],
                        start=(c == 0), stop=(c == NCH - 1)),
                        reads=[env.ones128, st], writes=[pb_a])
                    s2 = sq[c % 2]
                    k.op("dve", lambda e, st=st, s2=s2: e.tensor_mul(
                        s2[:, :], st[:, :], st[:, :]),
                        reads=[st], writes=[s2])
                    k.op("pe", lambda e, s2=s2, c=c: e.matmul(
                        pb_b[0:1, :], env.ones128[:, :], s2[:, :],
                        start=(c == 0), stop=(c == NCH - 1)),
                        reads=[env.ones128, s2], writes=[pb_b])
                emit_ln_rows(k, env, pb_a, pb_b, rows, D)
                k.dma("sp", rows_scr[0:1, ts], rows["m"][0:1, :], reads=[rows["m"]],
                      tag="a_strow")
                k.dma("sp", rows_scr[1:2, ts], rows["r"][0:1, :], reads=[rows["r"]],
                      tag="a_strow")
        k.barrier()

        with contextlib.ExitStack() as sc:
            wst = [k.sb(f"cwst{i}", [128, HID], F32, sc) for i in range(2)]
            wbf = k.sb("cwbf", [128, HID], BF16, sc)
            for j, nm in enumerate(names):
                for c in range(NCH):
                    st = wst[c % 2]
                    k.dma("sp", st[:, :], dins[f"L1_attn_{nm}_w1"][c * 128:(c + 1) * 128, :],
                          writes=[st], tag="a_ldw1")
                    k.op("dve", lambda e, st=st: e.tensor_copy(wbf[:, :], st[:, :]),
                         reads=[st], writes=[wbf])
                    k.dma("sp", w1b_scr[j, c * 128:(c + 1) * 128, :], wbf[:, :],
                          reads=[wbf], tag="a_stw1")
                for hc in range(NHC):
                    st = wst[hc % 2]
                    k.dma("sp", st[:, 0:1024], dins[f"L1_attn_{nm}_w2"][hc * 128:(hc + 1) * 128, :],
                          writes=[st], tag="a_ldw2")
                    k.op("dve", lambda e, st=st: e.tensor_copy(wbf[:, 0:1024], st[:, 0:1024]),
                         reads=[st], writes=[wbf])
                    k.dma("sp", w2b_scr[j, hc * 128:(hc + 1) * 128, :], wbf[:, 0:1024],
                          reads=[wbf], tag="a_stw2")
        k.barrier()

        for j, nm in enumerate(names):
            with contextlib.ExitStack() as sc:
                b1r = k.sb("ab1r", [1, HID], BF16, sc)
                b2r = k.sb("ab2r", [1, D], BF16, sc)
                mrow = k.sb("amrow", [1, TTS], F32, sc)
                rrow = k.sb("arrow", [1, TTS], F32, sc)
                for i_ in range(HID // TTS):
                    k.dma("sp", mrow[:, :], dins[f"L1_attn_{nm}_b1"][0:1, i_ * TTS:(i_ + 1) * TTS],
                          writes=[mrow], tag="a_ldb1")
                    k.op("dve", lambda e, i_=i_: e.tensor_copy(
                        b1r[0:1, i_ * TTS:(i_ + 1) * TTS], mrow[0:1, :]),
                        reads=[mrow], writes=[b1r])
                for i_ in range(D // TTS):
                    k.dma("sp", mrow[:, :], dins[f"L1_attn_{nm}_b2"][0:1, i_ * TTS:(i_ + 1) * TTS],
                          writes=[mrow], tag="a_ldb2")
                    k.op("dve", lambda e, i_=i_: e.tensor_copy(
                        b2r[0:1, i_ * TTS:(i_ + 1) * TTS], mrow[0:1, :]),
                        reads=[mrow], writes=[b2r])
                stg = [k.sb(f"bstg{i}", [128, TTS], F32, sc) for i in range(2)]
                xn = [k.sb(f"axn{c}", [128, TTS], BF16, sc) for c in range(NCH)]
                apscr = [k.sb(f"aap{i}", [128, TTS], F32, sc) for i in range(2)]
                h_tt = k.sb("ahtt", [128, NHC * TTS], BF16, sc)
                w1w = [k.sb(f"aw1w{i}", [128, NCH * 128], BF16, sc) for i in range(2)]
                w2h = k.sb("aw2h", [128, NHC * TTS // 1], BF16, sc)
                pb_a = k.ps("bpba", [128, TTS], F32, sc)
                pb_b = k.ps("bpbb", [128, TTS], F32, sc)
                pb_h = [k.ps(f"bpbh{i}", [128, TTS], F32, sc) for i in range(2)]
                pb_o = [k.ps(f"bpbo{i}", [128, TTS], F32, sc) for i in range(2)]
                for tt in range(NTT):
                    ts = slice(tt * TTS, (tt + 1) * TTS)
                    k.dma("sp", mrow[:, :], rows_scr[0:1, ts], writes=[mrow], tag="a_ldrow")
                    k.dma("sp", rrow[:, :], rows_scr[1:2, ts], writes=[rrow], tag="a_ldrow")
                    k.op("pe", lambda e: e.matmul(pb_a[:, :], env.ones1r[:, :], mrow[0:1, :],
                                                  start=True, stop=True),
                         reads=[env.ones1r, mrow], writes=[pb_a])
                    k.op("pe", lambda e: e.matmul(pb_b[:, :], env.ones1r[:, :], rrow[0:1, :],
                                                  start=True, stop=True),
                         reads=[env.ones1r, rrow], writes=[pb_b])
                    for c in range(NCH):
                        st = stg[c % 2]
                        k.dma("sp", st[:, :], xin[c * 128:(c + 1) * 128, ts],
                              writes=[st], tag="a_ldx2")
                        scr = apscr[c % 2]
                        k.op("dve", lambda e, st=st, scr=scr: e.tensor_tensor(
                            scr[:, :], st[:, :], pb_a[:, :], ALU.subtract),
                            reads=[st, pb_a], writes=[scr])
                        k.op("dve", lambda e, c=c, scr=scr: e.tensor_tensor(
                            xn[c][:, :], scr[:, :], pb_b[:, :], ALU.mult),
                            reads=[scr, pb_b], writes=[xn[c]])
                    for g in range(4):
                        k.dma("sp",
                              w2h[:, g * 8 * TTS:(g + 1) * 8 * TTS].rearrange(
                                  "p (c n) -> p c n", n=TTS),
                              w2b_scr[j].rearrange("(c p) n -> p c n", p=128)[
                                  :, g * 8:(g + 1) * 8, 0:512],
                              writes=[w2h], tag="a_ldw2b")
                    for hc in range(NHC):
                        ww = w1w[hc % 2]
                        k.dma("sp",
                              ww[:, :].rearrange("p (c n) -> p c n", n=128),
                              w1b_scr[j].rearrange("(c p) n -> p c n", p=128)[
                                  :, :, hc * 128:(hc + 1) * 128],
                              writes=[ww], tag="a_ldw1b")
                        pbh = pb_h[hc % 2]
                        for dc in range(NCH):
                            k.op("pe", lambda e, dc=dc, ww=ww, pbh=pbh: e.matmul(
                                pbh[:, :], ww[:, dc * 128:(dc + 1) * 128], xn[dc][:, :],
                                start=(dc == 0), stop=False),
                                reads=[ww, xn[dc]], writes=[pbh])
                        k.op("pe", lambda e, hc=hc, pbh=pbh: e.matmul(
                            pbh[:, :], b1r[0:1, hc * 128:(hc + 1) * 128], onesrb[0:1, :],
                            start=False, stop=True),
                            reads=[b1r, onesrb], writes=[pbh])
                        k.op("act", lambda e, hc=hc, pbh=pbh: e.activation(
                            h_tt[:, hc * TTS:(hc + 1) * TTS], pbh[:, :], AF.Gelu),
                            reads=[pbh], writes=[h_tt])
                    for ohalf in range(2):
                        osl = slice(ohalf * 512, ohalf * 512 + 512)
                        if ohalf == 1:
                            for g in range(4):
                                k.dma("sp",
                                      w2h[:, g * 8 * TTS:(g + 1) * 8 * TTS].rearrange(
                                          "p (c n) -> p c n", n=TTS),
                                      w2b_scr[j].rearrange("(c p) n -> p c n", p=128)[
                                          :, g * 8:(g + 1) * 8, osl],
                                      writes=[w2h], tag="a_ldw2b")
                        if nm == "v":
                            for tsub in range(4):
                                ti = tt * 4 + tsub
                                pbo = pb_o[tsub % 2]
                                for hc in range(NHC):
                                    k.op("pe", lambda e, hc=hc, tsub=tsub, pbo=pbo: e.matmul(
                                        pbo[:, :],
                                        h_tt[:, hc * TTS + tsub * 128:hc * TTS + (tsub + 1) * 128],
                                        w2h[:, hc * TTS:(hc + 1) * TTS],
                                        start=(hc == 0), stop=False),
                                        reads=[h_tt, w2h], writes=[pbo])
                                k.op("pe", lambda e, osl=osl, pbo=pbo: e.matmul(
                                    pbo[:, :], env.ones1rb[0:1, :], b2r[0:1, osl],
                                    start=False, stop=True),
                                    reads=[env.ones1rb, b2r], writes=[pbo])
                                h0 = ohalf * 8
                                k.op("dve", lambda e, ti=ti, h0=h0, pbo=pbo: e.tensor_copy(
                                    vaug[ti][:, :].rearrange("p (hh c) -> p hh c", c=65)[
                                        :, h0:h0 + 8, 0:64],
                                    pbo[:, :].rearrange("p (hh c) -> p hh c", c=64)),
                                    reads=[pbo], writes=[vaug[ti]])
                        else:
                            dst = qT if nm == "q" else kT
                            for oq in range(4):
                                oc = ohalf * 4 + oq
                                pbo = pb_o[oq % 2]
                                for hc in range(NHC):
                                    k.op("pe", lambda e, hc=hc, oq=oq, pbo=pbo: e.matmul(
                                        pbo[:, :], w2h[:, hc * TTS + oq * 128:hc * TTS + (oq + 1) * 128],
                                        h_tt[:, hc * TTS:(hc + 1) * TTS],
                                        start=(hc == 0), stop=False),
                                        reads=[w2h, h_tt], writes=[pbo])
                                k.op("pe", lambda e, oc=oc, pbo=pbo: e.matmul(
                                    pbo[:, :], b2r[0:1, oc * 128:(oc + 1) * 128], onesrb[0:1, :],
                                    start=False, stop=True),
                                    reads=[b2r, onesrb], writes=[pbo])
                                k.op("dve", lambda e, oc=oc, tt=tt, dst=dst, pbo=pbo: e.tensor_copy(
                                    dst[:, oc * T + tt * TTS:oc * T + (tt + 1) * TTS], pbo[:, :]),
                                    reads=[pbo], writes=[dst])
            k.barrier()

        for ti in range(T // 128):
            k.op("dve", lambda e, ti=ti: e.memset(
                vaug[ti][:, :].rearrange("p (hh c) -> p hh c", c=65)[:, :, 64:65], 1.0),
                writes=[vaug[ti]], strict=True)

        with contextlib.ExitStack() as sc:
            wp = k.sb("awp", [128, NCH * 1024], BF16, sc)
            wpst = [k.sb(f"awpst{i}", [128, 1024], F32, sc) for i in range(2)]
            for c in range(NCH):
                st = wpst[c % 2]
                k.dma("sp", st[:, :], dins["L1_attn_proj_w"][c * 128:(c + 1) * 128, :],
                      writes=[st], tag="a_ldwp")
                k.op("dve", lambda e, st=st, c=c: e.tensor_copy(
                    wp[:, c * 1024:(c + 1) * 1024], st[:, :]),
                    reads=[st], writes=[wp])
            pbrow = k.sb("apbr", [1, D], BF16, sc)
            pbst = k.sb("apbst", [1, D], F32, sc)
            k.dma("sp", pbst[:, :], dins["L1_attn_proj_b"], writes=[pbst])
            k.op("dve", lambda e: e.tensor_copy(pbrow[:, :], pbst[:, :]),
                 reads=[pbst], writes=[pbrow])
            nsq = k.sb("ansq", [128, TTS], F32, sc)
            nB = k.sb("anB", [1, 32], F32, sc)
            negB = k.sb("anegB", [128, 16], F32, sc)
            ptil = [k.sb(f"aptil{i}", [128, TTS], BF16, sc) for i in range(3)]
            otok4 = [k.sb(f"aotok{i}", [128, NHEAD * HD], BF16, sc) for i in range(4)]
            otT = k.sb("aotT", [128, NCH * TTS], BF16, sc)
            rq2 = [k.sb(f"arq{i}", [128, 1], F32, sc) for i in range(2)]
            stg = [k.sb(f"cstg{i}", [128, TTS], F32, sc) for i in range(2)]
            res = [k.sb(f"cres{i}", [128, TTS], F32, sc) for i in range(2)]
            pb_s = [k.ps(f"cpbs{i}", [128, TTS], F32, sc) for i in range(2)]
            pb_av = [k.ps(f"cpbav{j}", [128, TTS], F32, sc) for j in range(2)]
            pb_t = k.ps("cpbt", [128, 1024], BF16, sc)
            pb_y = k.ps("cpby", [128, TTS], F32, sc)

            for h in range(NHEAD):
                c0, r0 = h // 2, (h % 2) * 64
                psl = slice(r0, r0 + 64)
                for which, src_t in ((0, qT), (1, kT)):
                    for tt4 in range(NTT):
                        k.op("dve", lambda e, src_t=src_t, c0=c0, psl=psl, tt4=tt4: e.tensor_mul(
                            nsq[psl, :],
                            src_t[psl, c0 * T + tt4 * TTS:c0 * T + (tt4 + 1) * TTS],
                            src_t[psl, c0 * T + tt4 * TTS:c0 * T + (tt4 + 1) * TTS]),
                            reads=[src_t], writes=[nsq], strict=True)
                        k.op("pe", lambda e, psl=psl: e.matmul(
                            pb_s[0][0:1, :], env.ones128[psl, :], nsq[psl, :],
                            start=True, stop=True),
                            reads=[env.ones128, nsq], writes=[pb_s[0]])
                        k.op("dve", lambda e, which=which, tt4=tt4: e.reduce_max(
                            nB[0:1, 3:4], pb_s[0][0:1, :], axis=AX.X),
                            reads=[pb_s[0]], writes=[nB], strict=True)
                        if tt4 == 0:
                            k.op("dve", lambda e, which=which: e.tensor_copy(
                                nB[0:1, which:which + 1], nB[0:1, 3:4]),
                                reads=[nB], writes=[nB], strict=True)
                        else:
                            k.op("dve", lambda e, which=which: e.tensor_tensor(
                                nB[0:1, which:which + 1], nB[0:1, which:which + 1],
                                nB[0:1, 3:4], ALU.max),
                                reads=[nB], writes=[nB], strict=True)
                k.op("dve", lambda e: e.tensor_mul(nB[0:1, 2:3], nB[0:1, 0:1], nB[0:1, 1:2]),
                     reads=[nB], writes=[nB], strict=True)
                k.op("act", lambda e: e.activation(nB[0:1, 2:3], nB[0:1, 2:3], AF.Ln, bias=1e-12),
                     reads=[nB], writes=[nB], strict=True)
                k.op("act", lambda e: e.activation(nB[0:1, 2:3], nB[0:1, 2:3], AF.Exp, scale=0.5),
                     reads=[nB], writes=[nB], strict=True)
                k.op("dve", lambda e, h=h: e.tensor_single_scalar(
                    nB[0:1, 8 + h:9 + h], nB[0:1, 2:3], -0.125, ALU.mult),
                    reads=[nB], writes=[nB], strict=True)
            for h in range(NHEAD):
                k.op("pe", lambda e, h=h: e.matmul(
                    pb_s[0][:, h:h + 1], env.ones1r[:, :], nB[0:1, 8 + h:9 + h],
                    start=True, stop=True),
                    reads=[env.ones1r, nB], writes=[pb_s[0]])
            k.op("dve", lambda e: e.tensor_copy(negB[:, 0:16], pb_s[0][:, 0:16]),
                 reads=[pb_s[0]], writes=[negB], strict=True)

            for qt in range(NTT):
                qs0 = qt * TTS
                for h in range(NHEAD):
                    c0, r0 = h // 2, (h % 2) * 64
                    psl = slice(r0, r0 + 64)
                    NKT = T // 128
                    pav = pb_av[h % 2]
                    for kt in range(NKT + 1):
                        if kt < NKT:
                            pbs = pb_s[kt % 2]
                            k.op("pe", lambda e, kt=kt, c0=c0, psl=psl, qs0=qs0, pbs=pbs: e.matmul(
                                pbs[:, :], kT[psl, c0 * T + kt * 128:c0 * T + (kt + 1) * 128],
                                qT[psl, c0 * T + qs0:c0 * T + qs0 + TTS],
                                start=True, stop=True),
                                reads=[kT, qT], writes=[pbs])
                            pt = ptil[kt % 3]
                            k.op("act", lambda e, h=h, pbs=pbs, pt=pt: e.activation(
                                pt[:, :], pbs[:, :], AF.Exp, bias=negB[:, h:h + 1], scale=0.125),
                                reads=[pbs, negB], writes=[pt])
                        if kt >= 1:
                            ktp = kt - 1
                            ptp = ptil[ktp % 3]
                            for qsub in range(4):
                                k.op("pe", lambda e, ktp=ktp, qsub=qsub, h=h, ptp=ptp, pav=pav: e.matmul(
                                    pav[:, qsub * 128:qsub * 128 + 65], ptp[:, qsub * 128:(qsub + 1) * 128],
                                    vaug[ktp][:, h * 65:(h + 1) * 65],
                                    start=(ktp == 0), stop=(ktp == NKT - 1)),
                                    reads=[ptp, vaug[ktp]], writes=[pav])
                    pav = pb_av[h % 2]
                    rq = rq2[h % 2]
                    for qsub in range(4):
                        k.op("dve", lambda e, qsub=qsub, pav=pav, rq=rq: e.reciprocal(
                            rq[:, 0:1], pav[:, qsub * 128 + 64:qsub * 128 + 65]),
                            reads=[pav], writes=[rq], strict=True)
                        k.op("dve", lambda e, h=h, qsub=qsub, pav=pav, rq=rq: e.tensor_scalar(
                            otok4[qsub][:, h * HD:(h + 1) * HD], pav[:, qsub * 128:qsub * 128 + 64],
                            rq[:, 0:1], None, ALU.mult),
                            reads=[pav, rq], writes=[otok4[qsub]])
                for qsub in range(4):
                    for c in range(NCH):
                        k.op("pe", lambda e, c=c, qsub=qsub: e.transpose(
                            pb_t[:, c * 128:(c + 1) * 128], otok4[qsub][:, c * 128:(c + 1) * 128],
                            env.eyeb[:, :]),
                            reads=[otok4[qsub], env.eyeb], writes=[pb_t])
                    k.op("dve", lambda e, qsub=qsub: e.tensor_copy(
                        otT[:, :].rearrange("p (c n) -> p c n", n=TTS)[
                            :, :, qsub * 128:(qsub + 1) * 128],
                        pb_t[:, :].rearrange("p (c n) -> p c n", n=128)),
                        reads=[pb_t], writes=[otT])
                for oc in range(NCH):
                    for fc in range(NCH):
                        k.op("pe", lambda e, fc=fc, oc=oc: e.matmul(
                            pb_y[:, :], wp[:, fc * 1024 + oc * 128:fc * 1024 + (oc + 1) * 128],
                            otT[:, fc * TTS:(fc + 1) * TTS],
                            start=(fc == 0), stop=False),
                            reads=[wp, otT], writes=[pb_y])
                    k.op("pe", lambda e, oc=oc: e.matmul(
                        pb_y[:, :], pbrow[0:1, oc * 128:(oc + 1) * 128], onesrb[0:1, :],
                        start=False, stop=True),
                        reads=[pbrow, onesrb], writes=[pb_y])
                    st = stg[oc % 2]
                    k.dma("sp", st[:, :], xin[oc * 128:(oc + 1) * 128, qs0:qs0 + TTS],
                          writes=[st], tag="a_ldx3")
                    rs = res[oc % 2]
                    k.op("dve", lambda e, st=st, rs=rs: e.tensor_tensor(
                        rs[:, :], st[:, :], pb_y[:, :], ALU.add),
                        reads=[st, pb_y], writes=[rs])
                    k.dma("sp", xout[oc * 128:(oc + 1) * 128, qs0:qs0 + TTS], rs[:, :],
                          reads=[rs], tag="a_stx")
    k.barrier()


def host_prep_l1_attn(inputs):
    d = {}
    for nm in ("q", "k", "v"):
        d[f"L1_attn_{nm}_w1"] = np.ascontiguousarray(inputs[f"L1_attn_{nm}_w1"])
        d[f"L1_attn_{nm}_w2"] = np.ascontiguousarray(inputs[f"L1_attn_{nm}_w2"])
        d[f"L1_attn_{nm}_b1"] = inputs[f"L1_attn_{nm}_b1"].reshape(1, HID)
        d[f"L1_attn_{nm}_b2"] = inputs[f"L1_attn_{nm}_b2"].reshape(1, D)
    d["L1_attn_proj_w"] = np.ascontiguousarray(inputs["L1_attn_proj_w"])
    d["L1_attn_proj_b"] = inputs["L1_attn_proj_b"].reshape(1, D)
    return d


_L1ATTN_SPECS = {}
for _nm in ("q", "k", "v"):
    _L1ATTN_SPECS[f"L1_attn_{_nm}_w1"] = ([D, HID], F32)
    _L1ATTN_SPECS[f"L1_attn_{_nm}_w2"] = ([HID, D], F32)
    _L1ATTN_SPECS[f"L1_attn_{_nm}_b1"] = ([1, HID], F32)
    _L1ATTN_SPECS[f"L1_attn_{_nm}_b2"] = ([1, D], F32)
_L1ATTN_SPECS["L1_attn_proj_w"] = ([D, D], F32)
_L1ATTN_SPECS["L1_attn_proj_b"] = ([1, D], F32)


def build_l1_attn_program():
    nc = bass.Bass("TRN2", target_bir_lowering=False)
    stack = contextlib.ExitStack()
    dins = {}
    specs = {**_CONST_SPECS, "c_eyef": ([128, 128], F32),
             "c_onesrowb": ([1, TTS], BF16), "c_ones1rb": ([1, 128], BF16),
             **_L1ATTN_SPECS}
    for name, (shape, dt) in specs.items():
        dins[name] = nc.dram_tensor(name, shape, dt, kind="ExternalInput").ap()
    xin = nc.dram_tensor("xin", [D, T], F32, kind="ExternalInput").ap()
    xout = nc.dram_tensor("xout", [D, T], F32, kind="ExternalOutput").ap()
    w1b_scr = nc.dram_tensor("w1b_scr", [3, D, HID], BF16).ap()
    w2b_scr = nc.dram_tensor("w2b_scr", [3, HID, D], BF16).ap()
    rows_scr = nc.dram_tensor("rows_scr", [2, T], F32).ap()
    with stack:
        k = Ker(nc, stack)
        env = Env(k, dins)
        k.barrier()
        emit_l1_attn(k, env, dins, xin, xout, w1b_scr, w2b_scr, rows_scr)
        k.lower(verbose=True)
    return nc

L0CUT = 99
NCK_RUN = 99
CHUNK_BAR = False
STAGE_CUT = 99
DK = 128
DV = 256
NH4 = 4
CC = 128
NCK = T // CC


def emit_l0_attn(k: Ker, env, dins, xin, xout, dbg=None):
    with contextlib.ExitStack() as sc0:
        eyef = k.sb("deyef", [128, 128], F32, sc0)
        umask_s = k.sb("dums", [128, 128], BF16, sc0)
        umask_i = k.sb("dumi", [128, 128], BF16, sc0)
        k.dma("sp", eyef[:, :], dins["c_eyef"], writes=[eyef])
        k.dma("sp", umask_s[:, :], dins["c_umask_s"], writes=[umask_s])
        k.dma("sp", umask_i[:, :], dins["c_umask_i"], writes=[umask_i])
        xn = [k.sb(f"dxn{c}", [128, T], BF16, sc0) for c in range(NCH)]
        qT = k.sb("dqT", [128, NH4 * T], BF16, sc0)
        kT = k.sb("dkT", [128, NH4 * T], BF16, sc0)
        beta = [k.sb(f"dbeta{i}", [65, T], BF16, sc0) for i in range(2)]
        ones65 = k.sb("dones65", [65, 128], BF16, sc0)
        k.dma("sp", ones65[:, :], dins["c_ones65"], writes=[ones65])
        wq = k.sb("dwq", [128, NCH * 512], BF16, sc0)
        wk = k.sb("dwk", [128, NCH * 512], BF16, sc0)
        wv = k.sb("dwv", [128, NCH * D], BF16, sc0)
        wg = k.sb("dwg", [128, NCH * D], BF16, sc0)
        wo = k.sb("dwo", [128, NCH * D], BF16, sc0)
        wb = k.sb("dwb", [128, NCH * 4], BF16, sc0)
        nwcol = k.sb("dnw", [128, 2], F32, sc0)
        k.dma("sp", nwcol[:, :], dins["L0_attn_nwcol"], writes=[nwcol])

        QSCALE = 2.0 ** -3.5

        def load_w(dst, src_ap, ncols, scale=None, nw=False):
            for dc in range(NCH):
                st = wst[dc % 2]
                k.dma("sp", st[:, 0:ncols], src_ap[dc * 128:(dc + 1) * 128, :],
                      writes=[st], tag="d_ldw")
                if nw:
                    k.op("dve", lambda e, dc=dc, st=st: e.tensor_scalar(
                        dst[:, dc * ncols:(dc + 1) * ncols], st[:, 0:ncols],
                        nwcol[:, dc % 2:dc % 2 + 1], None, ALU.mult),
                        reads=[st, nwcol], writes=[dst])
                elif scale is not None:
                    k.op("dve", lambda e, dc=dc, st=st, scale=scale: e.tensor_single_scalar(
                        dst[:, dc * ncols:(dc + 1) * ncols], st[:, 0:ncols], scale, ALU.mult),
                        reads=[st], writes=[dst])
                else:
                    k.op("dve", lambda e, dc=dc, st=st: e.tensor_copy(
                        dst[:, dc * ncols:(dc + 1) * ncols], st[:, 0:ncols]),
                        reads=[st], writes=[dst])

        with contextlib.ExitStack() as sc:
            wst = [k.sb(f"dwst{i}", [128, D], F32, sc) for i in range(2)]
            load_w(wq, dins["L0_attn_Wq"], 512, scale=QSCALE)
            if L0CUT >= 2:
                load_w(wk, dins["L0_attn_Wk"], 512)
            if L0CUT >= 5:
                load_w(wv, dins["L0_attn_Wv"], D)
            if L0CUT >= 6:
                load_w(wg, dins["L0_attn_Wg"], D)
            if L0CUT >= 7:
                load_w(wo, dins["L0_attn_Wo"], D, nw=True)
            if L0CUT >= 4:
                load_w(wb, dins["L0_attn_Wb"], 4)
            stg = [k.sb(f"dstg{i}", [128, TTS], F32, sc) for i in range(2)]
            sq = [k.sb(f"dsq{i}", [128, TTS], F32, sc) for i in range(2)]
            rows = {n_: k.sb(f"drow{n_}", [1, TTS], F32, sc) for n_ in ("m", "v", "r")}
            pb_a = k.ps("dpba", [128, TTS], F32, sc)
            pb_b = k.ps("dpbb", [128, TTS], F32, sc)
            for tt in range(NTT):
                ts = slice(tt * TTS, (tt + 1) * TTS)
                for c in range(NCH):
                    st = stg[c % 2]
                    k.dma("sp", st[:, :], xin[c * 128:(c + 1) * 128, ts],
                          writes=[st], tag="d_ldx")
                    k.op("pe", lambda e, st=st, c=c: e.matmul(
                        pb_a[0:1, :], env.ones128[:, :], st[:, :],
                        start=(c == 0), stop=(c == NCH - 1)),
                        reads=[env.ones128, st], writes=[pb_a])
                    s2 = sq[c % 2]
                    k.op("dve", lambda e, st=st, s2=s2: e.tensor_mul(
                        s2[:, :], st[:, :], st[:, :]),
                        reads=[st], writes=[s2])
                    k.op("pe", lambda e, s2=s2, c=c: e.matmul(
                        pb_b[0:1, :], env.ones128[:, :], s2[:, :],
                        start=(c == 0), stop=(c == NCH - 1)),
                        reads=[env.ones128, s2], writes=[pb_b])
                emit_ln_rows(k, env, pb_a, pb_b, rows, D)
                k.op("pe", lambda e: e.matmul(pb_a[:, :], env.ones1r[:, :], rows["m"][0:1, :],
                                              start=True, stop=True),
                     reads=[env.ones1r, rows["m"]], writes=[pb_a])
                k.op("pe", lambda e: e.matmul(pb_b[:, :], env.ones1r[:, :], rows["r"][0:1, :],
                                              start=True, stop=True),
                     reads=[env.ones1r, rows["r"]], writes=[pb_b])
                for c in range(NCH):
                    st = stg[c % 2]
                    k.dma("sp", st[:, :], xin[c * 128:(c + 1) * 128, ts],
                          writes=[st], tag="d_ldx2")
                    scr = sq[c % 2]
                    k.op("dve", lambda e, st=st, scr=scr: e.tensor_tensor(
                        scr[:, :], st[:, :], pb_a[:, :], ALU.subtract),
                        reads=[st, pb_a], writes=[scr])
                    k.op("dve", lambda e, c=c, ts=ts, scr=scr: e.tensor_tensor(
                        xn[c][:, ts], scr[:, :], pb_b[:, :], ALU.mult),
                        reads=[scr, pb_b], writes=[xn[c]])
        k.barrier()

        with contextlib.ExitStack() as sc:
            rrow = k.sb("drr", [1, TTS], F32, sc)
            sqs = k.sb("dsqs", [128, TTS], F32, sc)
            pb_p = k.ps("dpbp", [128, TTS], F32, sc)
            pb_r = k.ps("dpbr", [128, TTS], F32, sc)
            pb_q = [k.ps(f"dpbq{i}", [128, TTS], F32, sc) for i in range(2)]
            for h in range(NH4):
                for tt in range(NTT):
                    ts = slice(tt * TTS, (tt + 1) * TTS)
                    pbq = pb_q[tt % 2]
                    for dc in range(NCH):
                        k.op("pe", lambda e, dc=dc, h=h, ts=ts, pbq=pbq: e.matmul(
                            pbq[:, :], wq[:, dc * 512 + h * 128:dc * 512 + (h + 1) * 128],
                            xn[dc][:, ts], start=(dc == 0), stop=(dc == NCH - 1)),
                            reads=[wq, xn[dc]], writes=[pbq])
                    k.op("dve", lambda e, h=h, tt=tt, pbq=pbq: e.tensor_copy(
                        qT[:, h * T + tt * TTS:h * T + (tt + 1) * TTS], pbq[:, :]),
                        reads=[pbq], writes=[qT])
                    if L0CUT < 2:
                        continue
                    pbk = pb_q[(tt + 1) % 2]
                    for dc in range(NCH):
                        k.op("pe", lambda e, dc=dc, h=h, ts=ts, pbk=pbk: e.matmul(
                            pbk[:, :], wk[:, dc * 512 + h * 128:dc * 512 + (h + 1) * 128],
                            xn[dc][:, ts], start=(dc == 0), stop=(dc == NCH - 1)),
                            reads=[wk, xn[dc]], writes=[pbk])
                    k.op("dve", lambda e, h=h, tt=tt, pbk=pbk: e.tensor_copy(
                        kT[:, h * T + tt * TTS:h * T + (tt + 1) * TTS], pbk[:, :]),
                        reads=[pbk], writes=[kT])
                    if L0CUT < 3:
                        continue
                    k.op("dve", lambda e, h=h, tt=tt: e.tensor_mul(
                        sqs[:, :], kT[:, h * T + tt * TTS:h * T + (tt + 1) * TTS], kT[:, h * T + tt * TTS:h * T + (tt + 1) * TTS]),
                        reads=[kT], writes=[sqs])
                    k.op("pe", lambda e: e.matmul(
                        pb_p[0:1, :], env.ones128[:, :], sqs[:, :], start=True, stop=True),
                        reads=[env.ones128, sqs], writes=[pb_p])
                    k.op("dve", lambda e: e.tensor_copy(rrow[0:1, :], pb_p[0:1, :]),
                         reads=[pb_p], writes=[rrow], strict=True)
                    k.op("act", lambda e: e.activation(rrow[0:1, :], rrow[0:1, :],
                                                       AF.Ln, bias=1e-12),
                         reads=[rrow], writes=[rrow], strict=True)
                    k.op("act", lambda e: e.activation(rrow[0:1, :], rrow[0:1, :],
                                                       AF.Exp, scale=-0.5),
                         reads=[rrow], writes=[rrow], strict=True)
                    k.op("pe", lambda e: e.matmul(pb_r[:, :], env.ones1r[:, :], rrow[0:1, :],
                                                  start=True, stop=True),
                         reads=[env.ones1r, rrow], writes=[pb_r])
                    k.op("dve", lambda e, h=h, tt=tt: e.tensor_tensor(
                        kT[:, h * T + tt * TTS:h * T + (tt + 1) * TTS],
                        kT[:, h * T + tt * TTS:h * T + (tt + 1) * TTS], pb_r[:, :], ALU.mult),
                        reads=[kT, pb_r], writes=[kT])
                    k.op("pe", lambda e, h=h, ts=ts: e.matmul(
                        pb_p[0:1, 0:TTS], wb[0:128, 0 * 4 + h:h + 1] if False else
                        wb[:, h:h + 1], xn[0][:, ts], start=True, stop=False),
                        reads=[wb, xn[0]], writes=[pb_p]) if False else None
            for h in range(NH4 if L0CUT >= 4 else 0):
                for tt in range(NTT):
                    ts = slice(tt * TTS, (tt + 1) * TTS)
                    for dc in range(NCH):
                        k.op("pe", lambda e, dc=dc, h=h, ts=ts: e.matmul(
                            pb_p[0:1, :], wb[:, dc * 4 + h:dc * 4 + h + 1], xn[dc][:, ts],
                            start=(dc == 0), stop=(dc == NCH - 1)),
                            reads=[wb, xn[dc]], writes=[pb_p])
                    k.op("act", lambda e, h=h, tt=tt: e.activation(
                        beta[h // 2][32 * (h % 2):32 * (h % 2) + 1,
                                     tt * TTS:(tt + 1) * TTS],
                        pb_p[0:1, :], AF.Sigmoid),
                        reads=[pb_p], writes=[beta[h // 2]])
        if dbg:
            k.dma("sp", dbg["wq"], wq[:, :], reads=[wq], tag="dbg")
            k.dma("sp", dbg["xn7"], xn[7][:, :], reads=[xn[7]], tag="dbg")
            k.dma("sp", dbg["qT"], qT[:, :], reads=[qT], tag="dbg")
            k.dma("sp", dbg["kT"], kT[:, :], reads=[kT], tag="dbg")
            k.dma("sp", dbg["xn"], xn[0][:, :], reads=[xn[0]], tag="dbg")
        k.barrier()

        with contextlib.ExitStack() as sc:
            S = [k.sb(f"dS{h}", [128, DV], F32, sc) for h in range(NH4)]
            for h in range(NH4):
                k.op("dve", lambda e, h=h: e.memset(S[h][:, :], 0.0), writes=[S[h]])
            S_bf = [k.sb(f"dSbf{h}", [128, DV], BF16, sc) for h in range(NH4)]
            v_bf = [k.sb(f"dvbf{h}", [128, DV], BF16, sc) for h in range(NH4)]
            u_scr = k.sb("duscr", [128, DV], F32, sc)
            u_bf = [k.sb(f"dubf{h}", [128, DV], BF16, sc) for h in range(NH4)]
            d_bf = [k.sb(f"ddbf{h}", [128, DV], BF16, sc) for h in range(NH4)]
            n_bf = [k.sb(f"dnbf{h}", [128, CC], BF16, sc) for h in range(NH4)]
            nt_bf = [k.sb(f"dntbf{h}", [128, CC], BF16, sc) for h in range(NH4)]
            gt_bf = [k.sb(f"dgtbf{h}", [128, CC], BF16, sc) for h in range(NH4)]
            ktok = [k.sb(f"dktok{h}", [128, CC], BF16, sc) for h in range(NH4)]
            bcol = k.sb("dbcol", [128, 4], F32, sc)
            EE = [[k.sb(f"dE{i}{h}", [128, CC], BF16, sc) for h in range(NH4)]
                  for i in range(2)]
            FF = [[k.sb(f"dF{i}{h}", [128, CC], BF16, sc) for h in range(NH4)]
                  for i in range(2)]
            XX = [[k.sb(f"dX{i}{h}", [128, CC], BF16, sc) for h in range(NH4)]
                  for i in range(2)]
            YY = [[k.sb(f"dY{i}{h}", [128, CC], BF16, sc) for h in range(NH4)]
                  for i in range(2)]
            rms = k.sb("drms", [128, 4], F32, sc)
            o_f32 = [k.sb(f"dof32{h}", [128, DV], F32, sc) for h in range(NH4)]
            on_scr = k.sb("don", [128, DV], F32, sc)
            gsilu = k.sb("dgsilu", [128, (T // 128) * D], BF16, sc)
            otok = k.sb("dotok", [128, D], BF16, sc)
            otT = k.sb("dotT", [128, D], BF16, sc)
            stg = [k.sb(f"estg{i}", [128, CC], F32, sc) for i in range(2)]
            res = [o_f32[0], o_f32[1]]
            pb1 = k.ps("epb1", [128, TTS], F32, sc)
            pb2 = k.ps("epb2", [128, TTS], F32, sc)
            pb3 = k.ps("epb3", [128, TTS], F32, sc)
            pb4 = k.ps("epb4", [128, TTS], F32, sc)
            pb5 = k.ps("epb5", [128, TTS], F32, sc)
            pbt = k.ps("epbt", [128, 1024], BF16, sc)

            sg_scr = u_scr
            for ti in range(T // 128):
                tcs = slice(ti * 128, (ti + 1) * 128)
                for h in range(NH4):
                    pbg = pb4 if (h % 2 == 0) else pb2
                    off = 0 if (h % 2 == 0) else DV
                    for dc in range(NCH):
                        k.op("pe", lambda e, dc=dc, h=h, tcs=tcs, pbg=pbg, off=off: e.matmul(
                            pbg[:, off:off + DV], xn[dc][:, tcs],
                            wg[:, dc * D + h * DV:dc * D + (h + 1) * DV],
                            start=(dc == 0), stop=(dc == NCH - 1)),
                            reads=[xn[dc], wg], writes=[pbg])
                    k.op("act", lambda e, pbg=pbg, off=off: e.activation(
                        sg_scr[:, :], pbg[:, off:off + DV], AF.Sigmoid),
                        reads=[pbg], writes=[sg_scr])
                    k.op("dve", lambda e, ti=ti, h=h, pbg=pbg, off=off: e.tensor_tensor(
                        gsilu[:, ti * D + h * DV:ti * D + (h + 1) * DV],
                        sg_scr[:, :], pbg[:, off:off + DV], ALU.mult),
                        reads=[sg_scr, pbg], writes=[gsilu], strict=True)
            for c in range(min(NCK, NCK_RUN) if L0CUT >= 5 else 0):
                cs = slice(c * CC, (c + 1) * CC)
                ks = [kT[:, h * T + c * CC:h * T + (c + 1) * CC] for h in range(NH4)]
                qs = [qT[:, h * T + c * CC:h * T + (c + 1) * CC] for h in range(NH4)]
                pbV = lambda h: (pb2 if h < 2 else pb3)[:, (h % 2) * DV:(h % 2 + 1) * DV]
                pbVt = lambda h: pb2 if h < 2 else pb3
                for h in range(NH4):
                    k.op("act", lambda e, h=h: e.activation(
                        S_bf[h][:, :], S[h][:, :], AF.Copy),
                        reads=[S[h]], writes=[S_bf[h]], tag=f"s0cp_c{c}_h{h}")
                for h in range(NH4 if STAGE_CUT >= 1 else 0):
                    for dc in range(NCH):
                        k.op("pe", lambda e, dc=dc, h=h, cs=cs: e.matmul(
                            pbV(h), xn[dc][:, cs],
                            wv[:, dc * D + h * DV:dc * D + (h + 1) * DV],
                            start=(dc == 0), stop=(dc == NCH - 1)),
                            reads=[xn[dc], wv], writes=[pbVt(h)])
                for h in range(NH4 if STAGE_CUT >= 1 else 0):
                    k.op("act", lambda e, h=h, pbV=pbV: e.activation(
                        v_bf[h][:, :], pbV(h), AF.Copy),
                        reads=[pbVt(h)], writes=[v_bf[h]])
                for h in range(NH4 if STAGE_CUT >= 2 else 0):
                    k.op("pe", lambda e, h=h, ks=ks: e.matmul(
                        pb1[:, h * CC:(h + 1) * CC], ks[h], ks[h], start=True, stop=True),
                        reads=[kT], writes=[pb1])
                for h in range(NH4 if STAGE_CUT >= 3 else 0):
                    r0_ = 32 * (h % 2)
                    k.op("dve", lambda e, h=h, c=c, r0_=r0_: e.tensor_copy(
                        otT[0:1, h * CC:(h + 1) * CC],
                        beta[h // 2][r0_:r0_ + 1, c * CC:(c + 1) * CC]),
                        reads=[beta[h // 2]], writes=[otT], strict=True)
                for h in range(NH4 if STAGE_CUT >= 3 else 0):
                    k.op("pe", lambda e, h=h: e.matmul(
                        pb4[:, h * CC:(h + 1) * CC], env.ones1rb[:, :],
                        otT[0:1, h * CC:(h + 1) * CC],
                        start=True, stop=True),
                        reads=[env.ones1rb, otT], writes=[pb4], tag=f"s3bc_c{c}_h{h}")
                for h in range(NH4 if STAGE_CUT >= 4 else 0):
                    k.op("dve", lambda e, h=h: e.tensor_tensor(
                        u_scr[:, 0:CC], pb1[:, h * CC:(h + 1) * CC], umask_s[:, :], ALU.mult),
                        reads=[pb1, umask_s], writes=[u_scr], strict=True)
                    k.op("dve", lambda e, h=h: e.tensor_tensor(
                        n_bf[h][:, :], u_scr[:, 0:CC], pb4[:, h * CC:(h + 1) * CC], ALU.mult),
                        reads=[u_scr, pb4], writes=[n_bf[h]], strict=True)
                for h in range(NH4 if STAGE_CUT >= 5 else 0):
                    k.op("pe", lambda e, h=h: e.transpose(
                        pbt[:, 896 + 2 * h:897 + 2 * h],
                        otT[0:1, h * CC:(h + 1) * CC], env.eyeb[0:1, 0:1]),
                        reads=[otT, env.eyeb], writes=[pbt])
                for h in range(NH4 if STAGE_CUT >= 5 else 0):
                    k.op("dve", lambda e, h=h: e.tensor_copy(
                        bcol[:, h:h + 1], pbt[:, 896 + 2 * h:897 + 2 * h]),
                        reads=[pbt], writes=[bcol], strict=True)
                for h in range(NH4 if STAGE_CUT >= 6 else 0):
                    k.op("pe", lambda e, h=h: e.transpose(
                        pbt[:, h * 128:(h + 1) * 128], n_bf[h][:, :], env.eyeb[:, :]),
                        reads=[n_bf[h], env.eyeb], writes=[pbt])
                for h in range(NH4 if STAGE_CUT >= 6 else 0):
                    k.op("act", lambda e, h=h: e.activation(
                        nt_bf[h][:, :], pbt[:, h * 128:(h + 1) * 128], AF.Copy),
                        reads=[pbt], writes=[nt_bf[h]], strict=True)
                for h in range(NH4 if STAGE_CUT >= 7 else 0):
                    k.op("dve", lambda e, h=h: e.tensor_tensor(
                        XX[0][h][:, :], env.eyeb[:, :], n_bf[h][:, :], ALU.subtract),
                        reads=[env.eyeb, n_bf[h]], writes=[XX[0][h]], strict=True)
                    k.op("dve", lambda e, h=h: e.tensor_tensor(
                        YY[0][h][:, :], env.eyeb[:, :], nt_bf[h][:, :], ALU.subtract),
                        reads=[env.eyeb, nt_bf[h]], writes=[YY[0][h]], strict=True)
                Ep, Fp = n_bf, nt_bf
                Xp, Yp = XX[0], YY[0]
                for it in range(6 if STAGE_CUT >= 8 else 0):
                    En = EE[it % 2]
                    Fn = FF[it % 2]
                    Xn_ = XX[(it + 1) % 2]
                    Yn_ = YY[(it + 1) % 2]
                    for h in range(NH4):
                        k.op("pe", lambda e, h=h, Fp=Fp, Ep=Ep: e.matmul(
                            pb5[:, h * CC:(h + 1) * CC], Fp[h][:, :], Ep[h][:, :],
                            start=True, stop=True),
                            reads=[Fp[h], Ep[h]], writes=[pb5], tag=f"s8nm_c{c}_it{it}_h{h}")
                    for h in range(NH4):
                        k.op("act", lambda e, h=h, En=En: e.activation(
                            En[h][:, :], pb5[:, h * CC:(h + 1) * CC], AF.Copy),
                            reads=[pb5], writes=[En[h]], strict=True)
                    if it < 5:
                        for h in range(NH4):
                            k.op("pe", lambda e, h=h, Ep=Ep, Fp=Fp: e.matmul(
                                pb5[:, h * CC:(h + 1) * CC], Ep[h][:, :], Fp[h][:, :],
                                start=True, stop=True),
                                reads=[Ep[h], Fp[h]], writes=[pb5])
                        for h in range(NH4):
                            k.op("act", lambda e, h=h, Fn=Fn: e.activation(
                                Fn[h][:, :], pb5[:, h * CC:(h + 1) * CC], AF.Copy),
                                reads=[pb5], writes=[Fn[h]], strict=True)
                    for h in range(NH4):
                        k.op("pe", lambda e, h=h, Yp=Yp, En=En: e.matmul(
                            pb5[:, h * CC:(h + 1) * CC], Yp[h][:, :], En[h][:, :],
                            start=True, stop=True),
                            reads=[Yp[h], En[h]], writes=[pb5])
                    for h in range(NH4):
                        k.op("dve", lambda e, h=h, Xp=Xp, Xn_=Xn_: e.tensor_tensor(
                            Xn_[h][:, :], Xp[h][:, :], pb5[:, h * CC:(h + 1) * CC], ALU.add),
                            reads=[Xp[h], pb5], writes=[Xn_[h]], strict=True)
                    if it < 5:
                        for h in range(NH4):
                            k.op("pe", lambda e, h=h, En=En, Yp=Yp: e.matmul(
                                pb5[:, h * CC:(h + 1) * CC], En[h][:, :], Yp[h][:, :],
                                start=True, stop=True),
                                reads=[En[h], Yp[h]], writes=[pb5])
                        for h in range(NH4):
                            k.op("dve", lambda e, h=h, Yp=Yp, Yn_=Yn_: e.tensor_tensor(
                                Yn_[h][:, :], Yp[h][:, :], pb5[:, h * CC:(h + 1) * CC],
                                ALU.add),
                                reads=[Yp[h], pb5], writes=[Yn_[h]], strict=True)
                    Ep, Fp = En, Fn
                    Xp, Yp = Xn_, Yn_
                Xfin = Xp
                for h in range(NH4 if STAGE_CUT >= 9 else 0):
                    k.op("pe", lambda e, h=h, ks=ks, pbV=pbV: e.matmul(
                        pbV(h), ks[h], S_bf[h][:, :], start=True, stop=True),
                        reads=[kT, S_bf[h]], writes=[pbVt(h)], tag=f"s9p0_c{c}_h{h}")
                for h in range(NH4 if STAGE_CUT >= 10 else 0):
                    k.op("dve", lambda e, h=h: e.tensor_tensor(
                        u_scr[:, :], v_bf[h][:, :], pbV(h), ALU.subtract),
                        reads=[v_bf[h], pbVt(h)], writes=[u_scr], strict=True)
                    k.op("dve", lambda e, h=h: e.tensor_scalar(
                        u_bf[h][:, :], u_scr[:, :], bcol[:, h:h + 1], None, ALU.mult),
                        reads=[u_scr, bcol], writes=[u_bf[h]], strict=True)
                for h in range(NH4 if STAGE_CUT >= 11 else 0):
                    k.op("pe", lambda e, h=h, Xfin=Xfin: e.matmul(
                        pbV(h), Xfin[h][:, :], u_bf[h][:, :], start=True, stop=True),
                        reads=[Xfin[h], u_bf[h]], writes=[pbVt(h)])
                for h in range(NH4 if STAGE_CUT >= 11 else 0):
                    k.op("act", lambda e, h=h, pbV=pbV: e.activation(
                        d_bf[h][:, :], pbV(h), AF.Copy),
                        reads=[pbVt(h)], writes=[d_bf[h]], strict=True)
                for h in range(NH4 if STAGE_CUT >= 12 else 0):
                    k.op("pe", lambda e, h=h, ks=ks, qs=qs: e.matmul(
                        pb1[:, h * CC:(h + 1) * CC], ks[h], qs[h], start=True, stop=True),
                        reads=[kT, qT], writes=[pb1])
                for h in range(NH4 if STAGE_CUT >= 12 else 0):
                    k.op("dve", lambda e, h=h: e.tensor_tensor(
                        gt_bf[h][:, :], pb1[:, h * CC:(h + 1) * CC], umask_i[:, :], ALU.mult),
                        reads=[pb1, umask_i], writes=[gt_bf[h]], strict=True)
                for h in range(NH4 if STAGE_CUT >= 13 else 0):
                    k.op("pe", lambda e, h=h, ks=ks: e.transpose(
                        pbt[:, 512 + h * 128:512 + (h + 1) * 128], ks[h], env.eyeb[:, :]),
                        reads=[kT, env.eyeb], writes=[pbt])
                for h in range(NH4 if STAGE_CUT >= 13 else 0):
                    k.op("act", lambda e, h=h: e.activation(
                        ktok[h][:, :], pbt[:, 512 + h * 128:512 + (h + 1) * 128], AF.Copy),
                        reads=[pbt], writes=[ktok[h]], strict=True)
                for h in range(NH4 if STAGE_CUT >= 14 else 0):
                    k.op("pe", lambda e, h=h, qs=qs, pbV=pbV: e.matmul(
                        pbV(h), qs[h], S_bf[h][:, :], start=True, stop=False),
                        reads=[qT, S_bf[h]], writes=[pbVt(h)], tag=f"s14o_c{c}_h{h}")
                    k.op("pe", lambda e, h=h: e.matmul(
                        pbV(h), gt_bf[h][:, :], d_bf[h][:, :], start=False, stop=True),
                        reads=[gt_bf[h], d_bf[h]], writes=[pbVt(h)])
                for h in range(NH4 if STAGE_CUT >= 15 else 0):
                    k.op("pe", lambda e, h=h: e.matmul(
                        pb5[:, (h % 2) * DV:(h % 2 + 1) * DV] if False else
                        pb5[:, h * CC:(h + 1) * CC] if False else
                        (pb5 if h < 2 else pb4)[:, (h % 2) * DV:(h % 2 + 1) * DV],
                        ktok[h][:, :], d_bf[h][:, :], start=True, stop=True),
                        reads=[ktok[h], d_bf[h]], writes=[pb5 if h < 2 else pb4])
                for h in range(NH4 if STAGE_CUT >= 15 else 0):
                    k.op("dve", lambda e, h=h: e.tensor_tensor(
                        S[h][:, :], S[h][:, :],
                        (pb5 if h < 2 else pb4)[:, (h % 2) * DV:(h % 2 + 1) * DV], ALU.add),
                        reads=[S[h], pb5 if h < 2 else pb4], writes=[S[h]], strict=True,
                        tag=f"s15add_c{c}_h{h}")
                for h in range(NH4 if STAGE_CUT >= 16 else 0):
                    k.op("act", lambda e, h=h, pbV=pbV: e.activation(
                        o_f32[h][:, :], pbV(h), AF.Copy),
                        reads=[pbVt(h)], writes=[o_f32[h]], strict=True)
                    k.op("dve", lambda e, h=h: e.tensor_mul(
                        u_scr[:, :], o_f32[h][:, :], o_f32[h][:, :]),
                        reads=[o_f32[h]], writes=[u_scr], strict=True)
                    k.op("dve", lambda e, h=h: e.reduce_sum(
                        rms[:, h:h + 1], u_scr[:, :], axis=AX.X),
                        reads=[u_scr], writes=[rms], strict=True)
                if STAGE_CUT >= 16:
                    k.op("act", lambda e: e.activation(rms[:, 0:4], rms[:, 0:4], AF.Ln,
                                                       scale=1.0 / 256.0, bias=1e-5),
                         reads=[rms], writes=[rms], strict=True)
                    k.op("act", lambda e: e.activation(rms[:, 0:4], rms[:, 0:4], AF.Exp,
                                                       scale=-0.5),
                         reads=[rms], writes=[rms], strict=True)
                for h in range(NH4 if STAGE_CUT >= 16 else 0):
                    k.op("dve", lambda e, h=h: e.tensor_scalar(
                        on_scr[:, :], o_f32[h][:, :], rms[:, h:h + 1], None, ALU.mult),
                        reads=[o_f32[h], rms], writes=[on_scr], strict=True)
                    k.op("dve", lambda e, h=h, c=c: e.tensor_tensor(
                        otok[:, h * DV:(h + 1) * DV], on_scr[:, :],
                        gsilu[:, c * D + h * DV:c * D + (h + 1) * DV], ALU.mult),
                        reads=[on_scr, gsilu], writes=[otok], strict=True)
                if dbg and c == 0:
                    k.dma("sp", dbg["S0"], S[0][:, :], reads=[S[0]], tag="dbg")
                    k.dma("sp", dbg["otok"], otok[:, :], reads=[otok], tag="dbg")
                    k.dma("sp", dbg["X6"], Xfin[3][:, :], reads=[Xfin[3]], tag="dbg")
                    k.dma("sp", dbg["dbf"], d_bf[3][:, :], reads=[d_bf[3]], tag="dbg")
                    k.dma("sp", dbg["ubf"], u_bf[3][:, :], reads=[u_bf[3]], tag="dbg")
                    k.dma("sp", dbg["nbf"], n_bf[3][:, :], reads=[n_bf[3]], tag="dbg")
                    k.dma("sp", dbg["bcol"], bcol[:, 0:4], reads=[bcol], tag="dbg")
                for fc in range(NCH if STAGE_CUT >= 17 else 0):
                    k.op("pe", lambda e, fc=fc: e.transpose(
                        pbt[:, 384 + fc * 128:384 + (fc + 1) * 128] if False else
                        pbt[:, fc * 128:(fc + 1) * 128],
                        otok[:, fc * 128:(fc + 1) * 128], env.eyeb[:, :]),
                        reads=[otok, env.eyeb], writes=[pbt])
                if STAGE_CUT >= 17:
                    k.op("dve", lambda e: e.tensor_copy(otT[:, :], pbt[:, 0:D]),
                         reads=[pbt], writes=[otT], strict=True)
                for oc in range(NCH if STAGE_CUT >= 17 else 0):
                    for fc in range(NCH):
                        k.op("pe", lambda e, fc=fc, oc=oc: e.matmul(
                            pb5[:, 0:CC], wo[:, fc * D + oc * 128:fc * D + (oc + 1) * 128],
                            otT[:, fc * 128:(fc + 1) * 128],
                            start=(fc == 0), stop=(fc == NCH - 1)),
                            reads=[wo, otT], writes=[pb5])
                    st = stg[0] if False else None
                    st = res[oc % 2]
                    ld = stg[oc % 2] if False else None
                    pass
                    ldt = kimported = None
                    del st, ld, ldt, kimported
                    stl = res[oc % 2]
                    k.dma("sp", stl[:, 0:CC], xin[oc * 128:(oc + 1) * 128, cs],
                          writes=[stl], tag="d_ldx3")
                    k.op("dve", lambda e, stl=stl: e.tensor_tensor(
                        stl[:, 0:CC], stl[:, 0:CC], pb5[:, 0:CC], ALU.add),
                        reads=[stl, pb5], writes=[stl], strict=True)
                    k.dma("sp", xout[oc * 128:(oc + 1) * 128, cs], stl[:, 0:CC],
                          reads=[stl], tag="d_stx")
                if CHUNK_BAR:
                    k.barrier()
    k.barrier()


def host_prep_l0_attn(inputs):
    d = {}
    for nm in ("Wq", "Wk", "Wv", "Wb", "Wg", "Wo"):
        d[f"L0_attn_{nm}"] = np.ascontiguousarray(inputs[f"L0_attn_{nm}"])
    nw = inputs["L0_attn_norm_w"].reshape(2, 128).T
    d["L0_attn_nwcol"] = np.ascontiguousarray(nw).astype(np.float32)
    return d


_L0ATTN_SPECS = {
    "L0_attn_Wq": ([D, 512], F32),
    "L0_attn_Wk": ([D, 512], F32),
    "L0_attn_Wv": ([D, D], F32),
    "L0_attn_Wb": ([D, 4], F32),
    "L0_attn_Wg": ([D, D], F32),
    "L0_attn_Wo": ([D, D], F32),
    "L0_attn_nwcol": ([128, 2], F32),
}


def build_l0_attn_program(debug=False):
    nc = bass.Bass("TRN2", target_bir_lowering=False)
    stack = contextlib.ExitStack()
    dins = {}
    specs = {**_CONST_SPECS, "c_eyef": ([128, 128], F32),
             "c_ones1rb": ([1, 128], BF16),
             "c_umask_s": ([128, 128], BF16), "c_umask_i": ([128, 128], BF16),
             "c_ones65": ([65, 128], BF16),
             **_L0ATTN_SPECS}
    for name, (shape, dt) in specs.items():
        dins[name] = nc.dram_tensor(name, shape, dt, kind="ExternalInput").ap()
    xin = nc.dram_tensor("xin", [D, T], F32, kind="ExternalInput").ap()
    xout = nc.dram_tensor("xout", [D, T], F32, kind="ExternalOutput").ap()
    dbg = None
    if debug:
        dbg = {
            "qT": nc.dram_tensor("dbg_qT", [128, NH4 * T], BF16, kind="ExternalOutput").ap(),
            "kT": nc.dram_tensor("dbg_kT", [128, NH4 * T], BF16, kind="ExternalOutput").ap(),
            "beta": nc.dram_tensor("dbg_beta", [1, NH4 * T], BF16, kind="ExternalOutput").ap(),
            "xn": nc.dram_tensor("dbg_xn", [128, T], BF16, kind="ExternalOutput").ap(),
            "S0": nc.dram_tensor("dbg_S0", [128, DV], F32, kind="ExternalOutput").ap(),
            "otok": nc.dram_tensor("dbg_otok", [128, D], BF16, kind="ExternalOutput").ap(),
            "X6": nc.dram_tensor("dbg_X6", [128, CC], BF16, kind="ExternalOutput").ap(),
            "dbf": nc.dram_tensor("dbg_dbf", [128, DV], BF16, kind="ExternalOutput").ap(),
            "ubf": nc.dram_tensor("dbg_ubf", [128, DV], BF16, kind="ExternalOutput").ap(),
            "nbf": nc.dram_tensor("dbg_nbf", [128, CC], BF16, kind="ExternalOutput").ap(),
            "bcol": nc.dram_tensor("dbg_bcol", [128, 4], F32, kind="ExternalOutput").ap(),
            "wq": nc.dram_tensor("dbg_wq", [128, NCH * 512], BF16, kind="ExternalOutput").ap(),
            "xn7": nc.dram_tensor("dbg_xn7", [128, T], BF16, kind="ExternalOutput").ap(),
        }
    with stack:
        k = Ker(nc, stack)
        env = Env(k, dins)
        k.barrier()
        emit_l0_attn(k, env, dins, xin, xout, dbg)
        global _LAST_K
        _LAST_K = k
        k.lower(verbose=True)
    return nc




def build_fused_program():
    nc = bass.Bass("TRN2", target_bir_lowering=False)
    stack = contextlib.ExitStack()
    dins = {}
    specs = {**_CONST_SPECS, "c_eyef": ([128, 128], F32),
             "c_onesrowb": ([1, TTS], BF16), "c_ones1rb": ([1, 128], BF16),
             "c_umask_s": ([128, 128], BF16), "c_umask_i": ([128, 128], BF16),
             "c_ones65": ([65, 128], BF16),
             **_L0ATTN_SPECS, **_L0FFN_SPECS, **_L1ATTN_SPECS, **_L1FFN_SPECS}
    for name, (shape, dt) in specs.items():
        dins[name] = nc.dram_tensor(name, shape, dt, kind="ExternalInput").ap()
    xin = nc.dram_tensor("xin", [D, T], F32, kind="ExternalInput").ap()
    xout = nc.dram_tensor("xout", [D, T], F32, kind="ExternalOutput").ap()
    s1 = nc.dram_tensor("s1", [D, T], F32).ap()
    s2 = nc.dram_tensor("s2", [D, T], F32).ap()
    s3 = nc.dram_tensor("s3", [D, T], F32).ap()
    w1b_scr = nc.dram_tensor("w1b_scr", [3, D, HID], BF16).ap()
    w2b_scr = nc.dram_tensor("w2b_scr", [3, HID, D], BF16).ap()
    rows_scr = nc.dram_tensor("rows_scr", [2, T], F32).ap()
    wgq_scr = nc.dram_tensor("wgq_scr", [D, OFF], BF16).ap()
    wuq_scr = nc.dram_tensor("wuq_scr", [D, OFF], BF16).ap()
    with stack:
        k = Ker(nc, stack)
        env = Env(k, dins)
        k.barrier()
        emit_l0_attn(k, env, dins, xin, s1)
        emit_l0_ffn(k, env, dins, s1, s2)
        emit_l1_attn(k, env, dins, s2, s3, w1b_scr, w2b_scr, rows_scr)
        emit_l1_ffn(k, env, dins, s3, xout, wgq_scr, wuq_scr)
        k.lower(verbose=True)
    return nc


_PROGRAMS = {}


def kernel(**inputs):
    if "fused" not in _PROGRAMS:
        _PROGRAMS["fused"] = build_fused_program()
    consts = host_consts()
    prep = {}
    prep.update(host_prep_l0_attn(inputs))
    prep.update(host_prep_l0_ffn(inputs))
    prep.update(host_prep_l1_attn(inputs))
    prep.update(host_prep_l1_ffn(inputs))
    base = {**consts, **prep}

    x = inputs["L0_attn_x"]
    n_cores = 8
    in_maps = [{**base, "xin": np.ascontiguousarray(x[i].T).astype(np.float32)}
               for i in range(n_cores)]
    res = run_bass_kernel_spmd(_PROGRAMS["fused"], in_maps, core_ids=list(range(n_cores)))
    out = np.stack([np.asarray(res.results[i]["xout"]).T for i in range(n_cores)], axis=0)
    return out.astype(np.float32)
